# version 13
# baseline (speedup 1.0000x reference)
import contextlib

import numpy as np
import ml_dtypes

import concourse.bass as bass
import concourse.mybir as mybir
from concourse.bass_utils import run_bass_kernel_spmd

F32 = mybir.dt.float32
BF16 = mybir.dt.bfloat16
I32 = mybir.dt.int32
AX = mybir.AxisListType.X
ALU = mybir.AluOpType
AF = mybir.ActivationFunctionType
BF = ml_dtypes.bfloat16

D = 4096
KT = D // 128
NCW = 256
NNC = D // NCW
E = 8
R = 16
ER = E * R
EPS = 1e-5
NCORES = 8


class Sem:
    def __init__(self, stack, nc, name):
        self.h = stack.enter_context(nc.semaphore(name))
        self.n = 0

    def inc(self, inst, k=1):
        inst.then_inc(self.h, k)
        self.n += k
        return self.n


def build_program(t_core, chunk_sizes):
    assert sum(chunk_sizes) == t_core
    maxchunk = max(chunk_sizes)

    nc = bass.Bass("TRN2", target_bir_lowering=False)

    xin = nc.dram_tensor("xin", [t_core, D], F32, kind="ExternalInput").ap()
    mask = nc.dram_tensor("mask", [D, D], I32, kind="ExternalInput").ap()
    w0 = nc.dram_tensor("w0", [D, D], F32, kind="ExternalInput").ap()
    ba = nc.dram_tensor("ba", [1, D], F32, kind="ExternalInput").ap()
    wb = nc.dram_tensor("wb", [D, D], F32, kind="ExternalInput").ap()
    bf_ = nc.dram_tensor("bf", [1, D], F32, kind="ExternalInput").ap()
    wr = nc.dram_tensor("wr", [E, D], F32, kind="ExternalInput").ap()
    am = nc.dram_tensor("am", [ER, D], F32, kind="ExternalInput").ap()
    bmm = nc.dram_tensor("bm", [E, D, R], F32, kind="ExternalInput").ap()
    out = nc.dram_tensor("out", [t_core, D], F32, kind="ExternalOutput").ap()
    wts = nc.dram_tensor("wts", [2, NNC, KT + 1, 128, NCW], BF16,
                         kind="Internal").ap()

    identc = nc.inline_tensor(np.eye(128, dtype=BF), name="identc").ap()
    e4 = np.zeros((4, 128), dtype=BF)
    for b in range(4):
        e4[b, b * 32:(b + 1) * 32] = 1
    e4c = nc.inline_tensor(e4, name="e4c").ap()
    e4t = np.zeros((128, 4), dtype=np.float32)
    for b in range(4):
        e4t[b * 32:(b + 1) * 32, b] = 1
    e4tc = nc.inline_tensor(e4t, name="e4tc").ap()
    e16 = np.zeros((E, 128), dtype=BF)
    for e in range(E):
        e16[e, e * R:(e + 1) * R] = 1
    e16c = nc.inline_tensor(e16, name="e16c").ap()
    ones1c = nc.inline_tensor(np.ones((1, 128), dtype=BF), name="ones1c").ap()

    with contextlib.ExitStack() as st:
        sem = lambda name: Sem(st, nc, name)
        NG = 6
        sGs = [sem(f"sG{i}") for i in range(NG)]
        NH = 6
        sHs = [sem(f"sH{i}") for i in range(NH)]
        NS = 4
        sSs = [sem(f"sS{i}") for i in range(NS)]
        sT = sem("sT")
        sV = sem("sV")
        sC = sem("sC")
        sP = sem("sP")
        gidx = [0]
        hidx = [0]

        def _qdma(pool, idx, q, dst, src):
            s = pool[idx[0] % len(pool)]
            idx[0] += 1
            if s.n:
                q.wait_ge(s.h, s.n)
            q.dma_start(dst, src).then_inc(s.h, 16)
            s.n += 16
            return (s, s.n)

        gidx2 = [0]

        def gdma(g, dst, src):
            return _qdma(sGs, gidx, g, dst, src)

        def hdma(q, dst, src):
            return _qdma(sHs, hidx, q, dst, src)

        def sdma(q, dst, src):
            return _qdma(sSs, gidx2, q, dst, src)

        sbuf = lambda name, shape, dt: st.enter_context(
            nc.sbuf_tensor(name, shape, dt))
        psum = lambda name, shape, dt: st.enter_context(
            nc.psum_tensor(name, shape, dt))

        ident = sbuf("ident", [128, 128], BF16)
        ones1 = sbuf("ones1", [1, 128], BF16)
        e4b = sbuf("e4b", [4, 128], BF16)
        e4tf = sbuf("e4tf", [128, 4], F32)
        e16b = sbuf("e16b", [E, 128], BF16)
        nt_max = maxchunk // 128
        xnat = sbuf("xnat", [128, nt_max * D], BF16)
        x1t = sbuf("x1t", [128, KT * maxchunk], BF16)
        wtb = [sbuf("wtb0", [128, (KT + 1) * NCW], BF16),
               sbuf("wtb1", [128, (KT + 1) * NCW], BF16)]
        at = sbuf("at", [128, KT * 128], BF16)
        bmft = sbuf("bmft", [128, KT * 128], BF16)
        wrt = sbuf("wrt", [128, KT * E], BF16)
        wstg = [sbuf("wstg0", [128, D], BF16), sbuf("wstg1", [128, D], BF16)]
        mstg = [sbuf(f"mstg{i}", [128, D // 2], BF16) for i in range(2)]
        bs2t = [sbuf(f"bs2t{i}", [128, 128], F32) for i in range(4)]
        bm4t = [sbuf("bm4t0", [4, 128], BF16), sbuf("bm4t1", [4, 128], BF16)]
        q2full = sbuf("q2full", [128, KT * 128], BF16)
        outstg = [sbuf("outstg0", [128, D // 2], F32),
                  sbuf("outstg1", [128, D // 2], F32)]
        g_t = sbuf("g_t", [128, maxchunk], BF16)
        wselt = sbuf("wselt", [E, maxchunk], BF16)
        wsele = sbuf("wsele", [128, maxchunk], BF16)
        bnbt = sbuf("bnbt", [128, nt_max * 96], F32)
        mv = [sbuf("mv0", [128, 2], F32), sbuf("mv1", [128, 2], F32)]
        vpe = [sbuf("vpe0", [128, 1], F32), sbuf("vpe1", [128, 1], F32)]
        stdt = [sbuf("stdt0", [128, 1], F32), sbuf("stdt1", [128, 1], F32)]
        rstdt = sbuf("rstdt", [128, 8], F32)
        nmbt = sbuf("nmbt", [128, 8], F32)
        tlr = [sbuf("tlr0", [128, E], F32), sbuf("tlr1", [128, E], F32)]
        m1 = [sbuf("m10", [128, 1], F32), sbuf("m11", [128, 1], F32)]
        nm1 = [sbuf("nm10", [128, 1], F32), sbuf("nm11", [128, 1], F32)]
        lm = [sbuf("lm0", [128, E], F32), sbuf("lm1", [128, E], F32)]
        m2 = [sbuf("m20", [128, 1], F32), sbuf("m21", [128, 1], F32)]
        exr = [sbuf("exr0", [128, E], F32), sbuf("exr1", [128, E], F32)]
        mskr = [sbuf("mskr0", [128, E], F32), sbuf("mskr1", [128, E], F32)]
        wvr = [sbuf("wvr0", [128, E], F32), sbuf("wvr1", [128, E], F32)]
        smr = [sbuf("smr0", [128, 1], F32), sbuf("smr1", [128, 1], F32)]
        rsr = [sbuf("rsr0", [128, 1], F32), sbuf("rsr1", [128, 1], F32)]
        wselb = [sbuf("wselb0", [128, E], BF16), sbuf("wselb1", [128, E], BF16)]

        pT = [psum("pT0", [128, 1024], BF16), psum("pT1", [128, 1024], BF16)]
        pmm = [psum("pmm0", [128, 512], F32), psum("pmm1", [128, 512], F32),
               psum("pmm2", [128, 512], F32)]
        pRE = psum("pRE", [128, 512], F32)
        pWT = psum("pWT", [128, 1024], BF16)
        pH = psum("pH", [128, 512], F32)
        pQ = pWT

        block = st.enter_context(nc.Block())

        @block.scalar
        def _(sc):
            from concourse.hw_specs import get_activation_tables
            for idx, (nm, funcs) in enumerate(
                    get_activation_tables(nc.m.arch).items()):
                if nm == "natural_log_exp_and_others":
                    sc.add_instruction(mybir.InstLoadActFuncSet(
                        name=nc.get_next_instruction_name(),
                        act_func_set_id=idx, ins=[], outs=[]))
                    break

        state = {
            "pt_free": [None, None],
            "pt_i": 0,
            "wstg_free": [None, None],
            "mstg_free": [0, 0],
            "maskpipe_free": [0, 0],
            "wtb_free": [None, None],
            "wtb_scrw": [None, None],
            "bank_free": [0, 0, 0],
            "xnat_free": [],
            "x1t_ready": 0,
            "pq_free": 0,
            "scr_done": [None, None],
            "ln_free": [None, None],
            "lnset_free": [None, None],
            "prep_done": [0] * 8,
            "rt_free": [None, None],
            "pre_free": 0,
            "pwt_free": 0,
            "ph_free": 0,
            "g_free": 0,
            "wsel_free": 0,
            "wsele_free": 0,
        }

        def tpool_take(t):
            k = state["pt_i"] % 2
            state["pt_i"] += 1
            pf = state["pt_free"][k]
            if pf is not None:
                t.wait_ge(pf[0].h, pf[1])
            return k

        gres = [None]

        @block.gpsimd
        def _(g):
            gres[0] = gdma(g, wstg[0][0:E, :], wr)
        wr_s, wr_v = gres[0]

        @block.sync
        def _(sy):
            hdma(sy, ident[:, :], identc)
            hdma(sy, ones1[:, :], ones1c)
            hdma(sy, e4b[:, :], e4c)
            hdma(sy, e4tf[:, :], e4tc)
            hdma(sy, e16b[:, :], e16c)
        consts_done = tuple((s, s.n) for s in sHs if s.n)

        @block.tensor
        def _(t):
            for s, vv in consts_done:
                t.wait_ge(s.h, vv)
            t.wait_ge(wr_s.h, wr_v)
            k = tpool_take(t)
            mm = None
            for kt in range(KT):
                mm = t.transpose(pT[k][0:128, kt * E:(kt + 1) * E],
                                 wstg[0][0:E, kt * 128:(kt + 1) * 128],
                                 ident[0:E, 0:E])
            sT.inc(mm)
        tv = sT.n

        @block.vector
        def _(v, tv=tv):
            v.wait_ge(sT.h, tv)
            sV.inc(v.tensor_copy(wrt[:, :], pT[0][:, 0:KT * E]))
        state["pt_free"][0] = (sV, sV.n)
        wrt_done = sV.n

        gres = [None]

        @block.gpsimd
        def _(g):
            gres[0] = gdma(g, wstg[1][:, :], am)
        am_s, am_v = gres[0]

        for gp in range(4):
            @block.tensor
            def _(t, gp=gp, am_s=am_s, am_v=am_v):
                if gp == 0:
                    t.wait_ge(am_s.h, am_v)
                k = tpool_take(t)
                mm = None
                for j in range(8):
                    kt = gp * 8 + j
                    mm = t.transpose(pT[k][:, j * 128:(j + 1) * 128],
                                     wstg[1][:, kt * 128:(kt + 1) * 128],
                                     ident[:, :])
                sT.inc(mm)
            tv = sT.n
            kk = (state["pt_i"] - 1) % 2

            @block.vector
            def _(v, gp=gp, tv=tv, kk=kk):
                v.wait_ge(sT.h, tv)
                sV.inc(v.tensor_copy(at[:, gp * 1024:(gp + 1) * 1024],
                                     pT[kk][:, :]))
            state["pt_free"][kk] = (sV, sV.n)
        at_done = sV.n

        for e in range(E):
            @block.gpsimd
            def _(g, e=e):
                if e == 0:
                    g.wait_ge(sV.h, wrt_done)
                gdma(g,
                     wstg[0][:, :].rearrange("p (ot c) -> p ot c", c=128)
                     [:, :, e * R:(e + 1) * R],
                     bmm[e, :, :].rearrange("(ot p) r -> p ot r", p=128))
        bmfb_waits = tuple((s, s.n) for s in sGs if s.n)

        for gp in range(4):
            @block.tensor
            def _(t, gp=gp, bw=bmfb_waits):
                if gp == 0:
                    for s, vv in bw:
                        t.wait_ge(s.h, vv)
                k = tpool_take(t)
                mm = None
                for j in range(8):
                    ot = gp * 8 + j
                    mm = t.transpose(pT[k][:, j * 128:(j + 1) * 128],
                                     wstg[0][:, ot * 128:(ot + 1) * 128],
                                     ident[:, :])
                sT.inc(mm)
            tv = sT.n
            kk = (state["pt_i"] - 1) % 2

            @block.vector
            def _(v, gp=gp, tv=tv, kk=kk):
                v.wait_ge(sT.h, tv)
                sV.inc(v.tensor_scalar(bmft[:, gp * 1024:(gp + 1) * 1024],
                                       pT[kk][:, :], 2.0, None, ALU.mult))
            state["pt_free"][kk] = (sV, sV.n)
        smalls_done_T = sT.n

        state["wstg_free"] = [(sT, smalls_done_T), (sT, smalls_done_T)]

        @block.vector
        def _(v):
            sV.inc(v.memset(wtb[0][:, KT * NCW:(KT + 1) * NCW], 0.0))
            sV.inc(v.memset(wtb[1][:, KT * NCW:(KT + 1) * NCW], 0.0))
        wtb_ms = sV.n


        q2_ready = [0] * KT
        red_ready = [0] * KT
        bs2_free = [None] * 4

        def maskpipe_a(ntt):
            s4 = ntt % 4
            for hf in range(2):
                sl = (ntt * 2 + hf) % 2
                gres = [None]

                @block.gpsimd
                def _(g, ntt=ntt, hf=hf, sl=sl):
                    if state["mstg_free"][sl]:
                        g.wait_ge(sV.h, state["mstg_free"][sl])
                    gres[0] = gdma(
                        g, mstg[sl][:, :],
                        mask[ntt * 128:(ntt + 1) * 128,
                             hf * (D // 2):(hf + 1) * (D // 2)])
                gs, gv = gres[0]

                @block.vector
                def _(v, hf=hf, sl=sl, s4=s4, gs=gs, gv=gv,
                      bf2=(bs2_free[s4] if hf == 0 else None)):
                    v.wait_ge(gs.h, gv)
                    if bf2 is not None:
                        v.wait_ge(bf2[0].h, bf2[1])
                    sV.inc(v.tensor_reduce(
                        bs2t[s4][:, hf * 64:(hf + 1) * 64].unsqueeze(2),
                        mstg[sl][:, :].rearrange("p (kb j) -> p kb j", j=32),
                        AX, ALU.add))
                state["mstg_free"][sl] = sV.n
            red_ready[ntt] = sV.n

        def maskpipe_b(ntt):
            sl = ntt % 2
            s4 = ntt % 4

            @block.tensor
            def _(t, s4=s4, red_v=red_ready[ntt]):
                t.wait_ge(sV.h, red_v)
                if state["pq_free"]:
                    t.wait_ge(sV.h, state["pq_free"])
                sT.inc(t.matmul(pQ[:, :].bitcast(F32)[0:4, 0:128],
                                e4tf[:, :],
                                bs2t[s4][:, :], start=True, stop=True))
            tv = sT.n
            bs2_free[s4] = (sT, sT.n)

            @block.vector
            def _(v, sl=sl, tv=tv):
                v.wait_ge(sT.h, tv)
                sV.inc(v.tensor_scalar(bm4t[sl][0:4, :],
                                       pQ[:, :].bitcast(F32)[0:4, 0:128],
                                       0.0, None, ALU.is_gt))
            bm_v = sV.n

            @block.tensor
            def _(t, sl=sl, bm_v=bm_v):
                t.wait_ge(sV.h, bm_v)
                sT.inc(t.matmul(pQ[:, :].bitcast(F32)[:, 128:256],
                                e4b[0:4, :],
                                bm4t[sl][0:4, :], start=True, stop=True))
            tv = sT.n

            @block.vector
            def _(v, ntt=ntt, sl=sl, tv=tv):
                v.wait_ge(sT.h, tv)
                sV.inc(v.tensor_copy(q2full[:, ntt * 128:(ntt + 1) * 128],
                                     pQ[:, :].bitcast(F32)[:, 128:256]))
            state["pq_free"] = sV.n
            q2_ready[ntt] = sV.n

        def gen_subtile(L, ntt):
            sl = ntt % 2
            wsrc = w0 if L == 0 else wb
            fw = state["wstg_free"][sl]
            gres = [None]

            @block.gpsimd
            def _(g, ntt=ntt, sl=sl, wsrc=wsrc, fw=fw):
                if fw is not None:
                    g.wait_ge(fw[0].h, fw[1])
                gres[0] = gdma(g, wstg[sl][:, :],
                               wsrc[ntt * 128:(ntt + 1) * 128, :])
            gs, gv = gres[0]
            if L == 0:
                @block.vector
                def _(v, ntt=ntt, sl=sl, gs=gs, gv=gv, q2v=q2_ready[ntt]):
                    v.wait_ge(gs.h, gv)
                    v.wait_ge(sV.h, q2v)
                    sV.inc(v.tensor_tensor(
                        wstg[sl][:, :].rearrange("p (kb j) -> p kb j", j=32),
                        wstg[sl][:, :].rearrange("p (kb j) -> p kb j", j=32),
                        q2full[:, ntt * 128:(ntt + 1) * 128]
                        .unsqueeze(2).broadcast_to([128, 128, 32]),
                        ALU.mult))
                return (sV, sV.n)
            return (gs, gv)

        def gen_transposes(nci, sub, ready):
            sl2 = (nci * 2 + sub) % 2
            slw = nci % 2
            for gp in range(4):
                @block.tensor
                def _(t, gp=gp, sl2=sl2, ready=ready):
                    if gp == 0:
                        t.wait_ge(ready[0].h, ready[1])
                    k = tpool_take(t)
                    mm = None
                    for j in range(8):
                        kt = gp * 8 + j
                        mm = t.transpose(pT[k][:, j * 128:(j + 1) * 128],
                                         wstg[sl2][:, kt * 128:(kt + 1) * 128],
                                         ident[:, :])
                    sT.inc(mm)
                tv = sT.n
                kk = (state["pt_i"] - 1) % 2

                @block.scalar
                def _(sc, gp=gp, sub=sub, tv=tv, kk=kk, slw=slw,
                      scrw=state["wtb_scrw"][slw]):
                    sc.wait_ge(sT.h, tv)
                    if scrw is not None:
                        sc.wait_ge(scrw[0].h, scrw[1])
                    sC.inc(sc.activation(
                        wtb[slw][:, :].rearrange("p (kt n) -> p kt n", n=NCW)
                        [:, gp * 8:(gp + 1) * 8, sub * 128:sub * 128 + 128],
                        pT[kk][:, :].rearrange("p (j n) -> p j n", n=128),
                        AF.Identity))
                state["pt_free"][kk] = (sC, sC.n)
            state["wstg_free"][sl2] = (sT, sT.n)
            return sC.n

        def nci_matmuls(L, nci, nt, lora, wt_ready, extra_t_waits=(),
                        lora_gate=None):
            sl = nci % 2
            for i in range(nt):
                bk = (state.setdefault("bk_i", [0]))[0] % 3
                state["bk_i"][0] += 1

                @block.tensor
                def _(t, nci=nci, sl=sl, i=i, bk=bk, lora=lora,
                      wt_ready=wt_ready, bf_val=state["bank_free"][bk],
                      xr=state["x1t_ready"], ew=tuple(extra_t_waits),
                      lora_gate=lora_gate):
                    if i == 0 and wt_ready is not None:
                        t.wait_ge(wt_ready[0].h, wt_ready[1])
                    if i == 0:
                        for s, vv in ew:
                            t.wait_ge(s.h, vv)
                        t.wait_ge(sV.h, xr)
                    if bf_val:
                        t.wait_ge(sV.h, bf_val)
                    t.matmul(pmm[bk][:, 0:NCW], ones1[0:1, :],
                             wtb[sl][0:1, KT * NCW:KT * NCW + NCW],
                             start=True, stop=False)
                    mm = None
                    for kt in range(KT):
                        mm = t.matmul(
                            pmm[bk][:, 0:NCW],
                            x1t[:, kt * maxchunk + i * 128:
                                kt * maxchunk + (i + 1) * 128],
                            wtb[sl][:, kt * NCW:(kt + 1) * NCW],
                            start=False, stop=(kt == KT - 1 and lora is None))
                    if lora is not None:
                        if lora_gate is not None and nci == 0 and i == 0:
                            t.wait_ge(lora_gate[0].h, lora_gate[1])
                        mm = t.matmul(pmm[bk][:, 0:NCW],
                                      lora[:, i * 128:(i + 1) * 128],
                                      bmft[:, nci * NCW:(nci + 1) * NCW],
                                      start=False, stop=True)
                    sT.inc(mm)
                tv = sT.n

                @block.vector
                def _(v, nci=nci, i=i, bk=bk, tv=tv):
                    v.wait_ge(sT.h, tv)
                    sl_ap = xnat[:, i * D + nci * NCW:i * D + (nci + 1) * NCW]
                    sV.inc(v.scalar_tensor_tensor(
                        sl_ap, pmm[bk][:, 0:NCW], 1.0, sl_ap,
                        ALU.mult, ALU.add))
                    v.wait_ge(sV.h, sV.n)
                    sV.inc(v.bn_stats(
                        bnbt[:, i * 96 + nci * 6:i * 96 + (nci + 1) * 6],
                        sl_ap))
                state["bank_free"][bk] = sV.n - 1
                if nci == NNC - 1:
                    ln_prep(i, i % 2)
            state["wtb_free"][sl] = (sT, sT.n)

        def layer_pass(L, c, nt, lora, lora_gate=None):
            if c == 0 and L == 0:
                for ntt in range(4):
                    maskpipe_a(ntt)
                maskpipe_b(0)
                maskpipe_a(4)
                maskpipe_b(1)
                maskpipe_a(5)
                maskpipe_b(2)
                maskpipe_a(6)
                maskpipe_b(3)
                maskpipe_a(7)
                maskpipe_b(4)
                maskpipe_b(5)
            for nci in range(NNC):
                sl = nci % 2
                if c == 0:
                    subs = []
                    for sub in range(2):
                        ntt = nci * 2 + sub
                        ready = gen_subtile(L, ntt)
                        ev = gen_transposes(nci, sub, ready)
                        subs.append(ev)
                    bsrc = ba if L == 0 else bf_
                    gres = [None]

                    @block.gpsimd
                    def _(g, nci=nci, sl=sl, bsrc=bsrc,
                          scrw=state["wtb_scrw"][sl],
                          wfw=state["wtb_free"][sl]):
                        if scrw is not None:
                            g.wait_ge(scrw[0].h, scrw[1])
                        else:
                            g.wait_ge(sV.h, wtb_ms)
                        if wfw is not None:
                            g.wait_ge(wfw[0].h, wfw[1])
                        gres[0] = gdma(
                            g, wtb[sl][0:1, KT * NCW:KT * NCW + NCW],
                            bsrc[0:1, nci * NCW:(nci + 1) * NCW])
                    bias_rdy = gres[0]
                    ev_last = subs[-1]
                    nci_matmuls(L, nci, nt, lora, bias_rdy,
                                extra_t_waits=[(sC, ev_last)],
                                lora_gate=lora_gate)
                    hres = [None]

                    @block.sync
                    def _(sy, L=L, nci=nci, sl=sl, ev=ev_last,
                          br=bias_rdy):
                        sy.wait_ge(sC.h, ev)
                        sy.wait_ge(br[0].h, br[1])
                        hres[0] = hdma(
                            sy, wts[L, nci, :, :, :].transpose([1, 0, 2]),
                            wtb[sl][:, :].rearrange("p (kt n) -> p kt n",
                                                    n=NCW))
                    state["wtb_scrw"][sl] = hres[0]
                    state["scr_done"][L] = hres[0]
                    if L == 0:
                        for ntt in (nci * 2 + 6, nci * 2 + 7):
                            if ntt < KT:
                                maskpipe_b(ntt)
                        for ntt in (nci * 2 + 8, nci * 2 + 9):
                            if ntt < KT:
                                maskpipe_a(ntt)
                else:
                    fw = state["wtb_free"][sl]
                    sd = state["scr_done"][L]
                    scrw = state["wtb_scrw"][sl]
                    hres = [None]

                    @block.sync
                    def _(sy, L=L, nci=nci, sl=sl, fw=fw, sd=sd, scrw=scrw):
                        if nci < 2 and sd is not None:
                            sy.wait_ge(sd[0].h, sd[1])
                        if scrw is not None:
                            sy.wait_ge(scrw[0].h, scrw[1])
                        if fw is not None:
                            sy.wait_ge(fw[0].h, fw[1])
                        hres[0] = hdma(
                            sy,
                            wtb[sl][:, :].rearrange("p (kt n) -> p kt n",
                                                    n=NCW),
                            wts[L, nci, :, :, :].transpose([1, 0, 2]))
                    nci_matmuls(L, nci, nt, lora, hres[0],
                                lora_gate=lora_gate)

        def transpose_tile(i, waits):
            for gp in range(4):
                @block.tensor
                def _(t, i=i, gp=gp, waits=tuple(waits)):
                    if gp == 0:
                        for s, vv in waits:
                            t.wait_ge(s.h, vv)
                    k = tpool_take(t)
                    mm = None
                    for j in range(8):
                        kt = gp * 8 + j
                        mm = t.transpose(
                            pT[k][:, j * 128:(j + 1) * 128],
                            xnat[:, i * D + kt * 128:i * D + (kt + 1) * 128],
                            ident[:, :])
                    sT.inc(mm)
                tv = sT.n
                kk = (state["pt_i"] - 1) % 2

                @block.vector
                def _(v, i=i, gp=gp, tv=tv, kk=kk):
                    v.wait_ge(sT.h, tv)
                    sV.inc(v.tensor_copy(
                        x1t[:, :].rearrange("p (kt t) -> p kt t", t=maxchunk)
                        [:, gp * 8:(gp + 1) * 8, i * 128:(i + 1) * 128],
                        pT[kk][:, :].rearrange("p (j t) -> p j t", t=128)))
                state["pt_free"][kk] = (sV, sV.n)
            return sV.n

        def input_chunk(c, t0, nt):
            for i in range(nt):
                in_gv = [None]

                @block.gpsimd
                def _(g, i=i, t0=t0):
                    xf = (state["xnat_free"][i]
                          if i < len(state["xnat_free"]) else None)
                    if xf is not None:
                        g.wait_ge(xf[0].h, xf[1])
                    in_gv[0] = gdma(
                        g, xnat[:, i * D:(i + 1) * D],
                        xin[t0 + i * 128:t0 + (i + 1) * 128, :])
                transpose_tile(i, [in_gv[0]])
            state["x1t_ready"] = sV.n

        def ln_prep(i, iset):
            lf = state["lnset_free"][iset]
            st_v = sV.n

            @block.vector
            def _(v, i=i, iset=iset, lf=lf, st_v=st_v):
                v.wait_ge(sV.h, st_v)
                if lf is not None:
                    v.wait_ge(lf[0].h, lf[1])
                sV.inc(v.bn_aggr(mv[iset][:, :],
                                 bnbt[:, i * 96:(i + 1) * 96]))
                v.wait_ge(sV.h, sV.n)
                sV.inc(v.tensor_scalar(vpe[iset][:, :], mv[iset][:, 1:2],
                                       EPS, None, ALU.add))
            ve_v = sV.n

            @block.scalar
            def _(sc, i=i, iset=iset, ve_v=ve_v):
                sc.wait_ge(sV.h, ve_v)
                sC.inc(sc.activation(stdt[iset][:, :], vpe[iset][:, :],
                                     AF.Ln))
                sc.wait_ge(sC.h, sC.n)
                sC.inc(sc.activation(rstdt[:, i:i + 1], stdt[iset][:, :],
                                     AF.Exp, scale=-0.5))
            sq_v = sC.n

            @block.vector
            def _(v, i=i, iset=iset, sq_v=sq_v):
                v.wait_ge(sC.h, sq_v)
                sV.inc(v.tensor_scalar(nmbt[:, i:i + 1], mv[iset][:, 0:1],
                                       rstdt[:, i:i + 1], -1.0,
                                       ALU.mult, ALU.mult))
            state["lnset_free"][iset] = (sV, sV.n)
            state["prep_done"][i] = sV.n

        def ln_apply(i, write_out=None, t0=0):
            pd = state["prep_done"][i]
            if write_out is not None:
                ofree = state.setdefault("outstg_free", [None, None])
                of0, of1 = ofree[0], ofree[1]

                @block.scalar
                def _(sc, i=i, pd=pd, of=of0):
                    sc.wait_ge(sV.h, pd)
                    if of is not None:
                        sc.wait_ge(of[0].h, of[1])
                    sC.inc(sc.activation(
                        outstg[0][:, :],
                        xnat[:, i * D:i * D + 2048],
                        AF.Identity, bias=nmbt[:, i:i + 1],
                        scale=rstdt[:, i:i + 1]))
                    sc.wait_ge(sC.h, sC.n)
                    sdma(sc,
                         write_out[t0 + i * 128:t0 + (i + 1) * 128, 0:2048],
                         outstg[0][:, :])
                ofree[0] = (sSs[(gidx2[0] - 1) % NS],
                            sSs[(gidx2[0] - 1) % NS].n)

                @block.vector
                def _(v, i=i, pd=pd, of=of1):
                    v.wait_ge(sV.h, pd)
                    if of is not None:
                        v.wait_ge(of[0].h, of[1])
                    sV.inc(v.tensor_scalar(
                        outstg[1][:, :],
                        xnat[:, i * D + 2048:i * D + 4096],
                        rstdt[:, i:i + 1], nmbt[:, i:i + 1],
                        ALU.mult, ALU.add))
                vv = sV.n
                hres = [None]

                @block.sync
                def _(sy, i=i, vv=vv):
                    sy.wait_ge(sV.h, vv)
                    hres[0] = hdma(
                        sy,
                        write_out[t0 + i * 128:t0 + (i + 1) * 128, 2048:4096],
                        outstg[1][:, :])
                ofree[1] = hres[0]
                return [(sC, sC.n), (sV, sV.n)]
            else:
                @block.scalar
                def _(sc, i=i, pd=pd):
                    sc.wait_ge(sV.h, pd)
                    sC.inc(sc.activation(
                        xnat[:, i * D:i * D + 2048],
                        xnat[:, i * D:i * D + 2048],
                        AF.Identity, bias=nmbt[:, i:i + 1],
                        scale=rstdt[:, i:i + 1]))

                @block.vector
                def _(v, i=i, pd=pd):
                    v.wait_ge(sV.h, pd)
                    sV.inc(v.tensor_scalar(
                        xnat[:, i * D + 2048:i * D + 4096],
                        xnat[:, i * D + 2048:i * D + 4096],
                        rstdt[:, i:i + 1], nmbt[:, i:i + 1],
                        ALU.mult, ALU.add))
                return [(sC, sC.n), (sV, sV.n)]

        def router_tile(i, iset, x2t_ok):
            rf = state["rt_free"][iset]

            @block.tensor
            def _(t, i=i, x2t_ok=x2t_ok, pref=state["pre_free"]):
                t.wait_ge(sV.h, x2t_ok)
                if pref:
                    t.wait_ge(sV.h, pref)
                mm = None
                for kt in range(KT):
                    mm = t.matmul(pRE[:, 0:E],
                                  x1t[:, kt * maxchunk + i * 128:
                                      kt * maxchunk + (i + 1) * 128],
                                  wrt[:, kt * E:(kt + 1) * E],
                                  start=(kt == 0), stop=(kt == KT - 1))
                sT.inc(mm)
            tv = sT.n

            @block.vector
            def _(v, iset=iset, tv=tv, rf=rf):
                v.wait_ge(sT.h, tv)
                if rf is not None:
                    v.wait_ge(rf[0].h, rf[1])
                sV.inc(v.tensor_copy(tlr[iset][:, :], pRE[:, 0:E]))
            state["pre_free"] = sV.n
            cp_v = sV.n

            @block.scalar
            def _(sc, iset=iset, cp_v=cp_v):
                sc.wait_ge(sV.h, cp_v)
                sC.inc(sc.activation(exr[iset][:, :], tlr[iset][:, :],
                                     AF.Exp))
            ex_v = sC.n

            @block.vector
            def _(v, iset=iset, ex_v=ex_v, cp_v=cp_v):
                def stepv(inst):
                    sV.inc(inst)
                    v.wait_ge(sV.h, sV.n)
                v.wait_ge(sV.h, cp_v)
                stepv(v.reduce_max(m1[iset][:, :], tlr[iset][:, :], axis=AX))
                stepv(v.tensor_scalar(mskr[iset][:, :], tlr[iset][:, :],
                                      m1[iset][:, 0:1], None, ALU.is_ge))
                stepv(v.scalar_tensor_tensor(lm[iset][:, :], mskr[iset][:, :],
                                             -1e30, tlr[iset][:, :],
                                             ALU.mult, ALU.add))
                stepv(v.reduce_max(m2[iset][:, :], lm[iset][:, :], axis=AX))
                stepv(v.tensor_scalar(mskr[iset][:, :], tlr[iset][:, :],
                                      m2[iset][:, 0:1], None, ALU.is_ge))
                v.wait_ge(sC.h, ex_v)
                stepv(v.tensor_tensor(wvr[iset][:, :], exr[iset][:, :],
                                      mskr[iset][:, :], ALU.mult))
                stepv(v.reduce_sum(smr[iset][:, :], wvr[iset][:, :], axis=AX))
                stepv(v.reciprocal(rsr[iset][:, :], smr[iset][:, :]))
                stepv(v.tensor_scalar(wselb[iset][:, :], wvr[iset][:, :],
                                      rsr[iset][:, 0:1], None, ALU.mult))
            wb_v = sV.n
            state["rt_free"][iset] = (sV, sV.n)

            @block.tensor
            def _(t, iset=iset, wb_v=wb_v, pwf=state["pwt_free"]):
                t.wait_ge(sV.h, wb_v)
                if pwf:
                    t.wait_ge(sV.h, pwf)
                sT.inc(t.transpose(pWT[0:E, 0:128], wselb[iset][:, :],
                                   ident[:, :]))
            tw_v = sT.n

            @block.vector
            def _(v, i=i, tw_v=tw_v):
                v.wait_ge(sT.h, tw_v)
                sV.inc(v.tensor_copy(wselt[0:E, i * 128:(i + 1) * 128],
                                     pWT[0:E, 0:128]))
            state["pwt_free"] = sV.n

        t0s = []
        acc = 0
        for cs in chunk_sizes:
            t0s.append(acc)
            acc += cs

        input_chunk(0, t0s[0], chunk_sizes[0] // 128)
        state["x1t_ready"] = sV.n
        for c, chunk in enumerate(chunk_sizes):
            nt = chunk // 128
            t0 = t0s[c]

            layer_pass(0, c, nt, None)
            attn_done_v = sV.n

            ph_evac = state["ph_free"]
            for i in range(nt):
                apw = ln_apply(i)
                x2t_i = transpose_tile(i, apw)
                router_tile(i, i % 2, x2t_i)
                wt_v = sV.n

                @block.tensor
                def _(t, i=i, wt_v=wt_v, ph_evac=ph_evac):
                    t.wait_ge(sV.h, wt_v)
                    if ph_evac:
                        t.wait_ge(sV.h, ph_evac)
                    sT.inc(t.matmul(
                        pH[:, (i % 4) * 128:(i % 4 + 1) * 128],
                        e16b[0:E, :],
                        wselt[0:E, i * 128:(i + 1) * 128],
                        start=True, stop=True))
                tv = sT.n
                if i % 4 == 3 or i == nt - 1:
                    g0 = (i // 4) * 4

                    @block.vector
                    def _(v, i=i, g0=g0, tv=tv):
                        v.wait_ge(sT.h, tv)
                        sV.inc(v.tensor_copy(
                            wsele[:, g0 * 128:(i + 1) * 128],
                            pH[:, 0:(i + 1 - g0) * 128]))
                    ph_evac = sV.n
            state["x1t_ready"] = sV.n
            wsele_done = sV.n

            hws = []
            off = 0
            while off < chunk:
                w_ = min(512, chunk - off)
                hws.append((off, w_))
                off += w_

            for off, w_ in hws:
                @block.tensor
                def _(t, off=off, w_=w_, phf=state["ph_free"],
                      wd=wsele_done):
                    t.wait_ge(sV.h, wd)
                    if phf:
                        t.wait_ge(sV.h, phf)
                    mm = None
                    for kt in range(KT):
                        mm = t.matmul(pH[:, 0:w_],
                                      at[:, kt * 128:(kt + 1) * 128],
                                      x1t[:, kt * maxchunk + off:
                                          kt * maxchunk + off + w_],
                                      start=(kt == 0), stop=(kt == KT - 1))
                    sT.inc(mm)
                tv = sT.n

                @block.vector
                def _(v, off=off, w_=w_, tv=tv, wd=wsele_done):
                    v.wait_ge(sT.h, tv)
                    v.wait_ge(sV.h, wd)
                    sV.inc(v.tensor_tensor(g_t[:, off:off + w_],
                                           pH[:, 0:w_],
                                           wsele[:, off:off + w_], ALU.mult))
                state["ph_free"] = sV.n
            g_done = sV.n
            state["x1t_ready"] = wsele_done

            layer_pass(1, c, nt, g_t, lora_gate=(sV, g_done))
            state["g_free"] = sT.n
            ffn_done_v = sV.n

            nxt_nt = (chunk_sizes[c + 1] // 128
                      if c + 1 < len(chunk_sizes) else 0)
            assert nxt_nt <= nt
            for i in range(nt):
                apw = ln_apply(i, write_out=out, t0=t0)
                if i < nxt_nt:
                    in_g = [None]

                    @block.gpsimd
                    def _(g, i=i, apw=tuple(apw), nt0=t0s[c + 1]):
                        for ss, vv in apw:
                            g.wait_ge(ss.h, vv)
                        in_g[0] = gdma(
                            g, xnat[:, i * D:(i + 1) * D],
                            xin[nt0 + i * 128:nt0 + (i + 1) * 128, :])
                    transpose_tile(i, [in_g[0]])
            if nxt_nt:
                state["x1t_ready"] = sV.n

        fin = tuple((s, s.n) for s in (sHs + sGs + sSs) if s.n)

        @block.sync
        def _(sy):
            for s, vv in fin:
                sy.wait_ge(s.h, vv)

        @block.gpsimd
        def _(g):
            for s, vv in fin:
                g.wait_ge(s.h, vv)

    return nc


def kernel(L0_attn_data, L0_attn_mask, L0_attn_weight, L0_attn_bias,
           L0_ffn_Wb, L0_ffn_bias, L0_ffn_Wr, L0_ffn_A, L0_ffn_Bm):
    t_total = L0_attn_data.shape[0] * L0_attn_data.shape[1]
    t_core = t_total // NCORES
    x = np.ascontiguousarray(
        np.asarray(L0_attn_data, dtype=np.float32).reshape(t_total, D))
    common = {
        "mask": np.ascontiguousarray(np.asarray(L0_attn_mask,
                                                dtype=np.int32)),
        "w0": np.ascontiguousarray(np.asarray(L0_attn_weight,
                                              dtype=np.float32)),
        "ba": np.ascontiguousarray(
            np.asarray(L0_attn_bias, dtype=np.float32).reshape(1, D)),
        "wb": np.ascontiguousarray(np.asarray(L0_ffn_Wb, dtype=np.float32)),
        "bf": np.ascontiguousarray(
            np.asarray(L0_ffn_bias, dtype=np.float32).reshape(1, D)),
        "wr": np.ascontiguousarray(np.asarray(L0_ffn_Wr, dtype=np.float32)),
        "am": np.ascontiguousarray(
            np.asarray(L0_ffn_A, dtype=np.float32).reshape(ER, D)),
        "bm": np.ascontiguousarray(np.asarray(L0_ffn_Bm, dtype=np.float32)),
    }
    in_maps = []
    for c in range(NCORES):
        m = dict(common)
        m["xin"] = np.ascontiguousarray(x[c * t_core:(c + 1) * t_core])
        in_maps.append(m)
    nc = build_program(t_core, [768, 768, 512])
    res = run_bass_kernel_spmd(nc, in_maps, core_ids=list(range(NCORES)))
    shards = [np.asarray(r["out"]) for r in res.results]
    full = np.concatenate(shards, axis=0)
    return full.reshape(L0_attn_data.shape[0], L0_attn_data.shape[1], D)


# version 14
# speedup vs baseline: 1.0690x; 1.0690x over previous
import contextlib

import numpy as np
import ml_dtypes

import concourse.bass as bass
import concourse.mybir as mybir
from concourse.bass_utils import run_bass_kernel_spmd

F32 = mybir.dt.float32
BF16 = mybir.dt.bfloat16
I32 = mybir.dt.int32
AX = mybir.AxisListType.X
ALU = mybir.AluOpType
AF = mybir.ActivationFunctionType
BF = ml_dtypes.bfloat16

D = 4096
KT = D // 128
NCW = 256
NNC = D // NCW
E = 8
R = 16
ER = E * R
EPS = 1e-5
NCORES = 8


class Sem:
    def __init__(self, stack, nc, name):
        self.h = stack.enter_context(nc.semaphore(name))
        self.n = 0

    def inc(self, inst, k=1):
        inst.then_inc(self.h, k)
        self.n += k
        return self.n


def build_program(t_core, chunk_sizes):
    assert sum(chunk_sizes) == t_core
    maxchunk = max(chunk_sizes)

    nc = bass.Bass("TRN2", target_bir_lowering=False)

    xin = nc.dram_tensor("xin", [t_core, D], F32, kind="ExternalInput").ap()
    mask = nc.dram_tensor("mask", [D, D], I32, kind="ExternalInput").ap()
    w0 = nc.dram_tensor("w0", [D, D], F32, kind="ExternalInput").ap()
    ba = nc.dram_tensor("ba", [1, D], F32, kind="ExternalInput").ap()
    wb = nc.dram_tensor("wb", [D, D], F32, kind="ExternalInput").ap()
    bf_ = nc.dram_tensor("bf", [1, D], F32, kind="ExternalInput").ap()
    wr = nc.dram_tensor("wr", [E, D], F32, kind="ExternalInput").ap()
    am = nc.dram_tensor("am", [ER, D], F32, kind="ExternalInput").ap()
    bmm = nc.dram_tensor("bm", [E, D, R], F32, kind="ExternalInput").ap()
    out = nc.dram_tensor("out", [t_core, D], F32, kind="ExternalOutput").ap()
    wts = nc.dram_tensor("wts", [2, NNC, KT + 1, 128, NCW], BF16,
                         kind="Internal").ap()

    identc = nc.inline_tensor(np.eye(128, dtype=BF), name="identc").ap()
    e4 = np.zeros((4, 128), dtype=BF)
    for b in range(4):
        e4[b, b * 32:(b + 1) * 32] = 1
    e4c = nc.inline_tensor(e4, name="e4c").ap()
    e4t = np.zeros((128, 4), dtype=np.float32)
    for b in range(4):
        e4t[b * 32:(b + 1) * 32, b] = 1
    e4tc = nc.inline_tensor(e4t, name="e4tc").ap()
    e16 = np.zeros((E, 128), dtype=BF)
    for e in range(E):
        e16[e, e * R:(e + 1) * R] = 1
    e16c = nc.inline_tensor(e16, name="e16c").ap()
    ones1c = nc.inline_tensor(np.ones((1, 128), dtype=BF), name="ones1c").ap()

    with contextlib.ExitStack() as st:
        sem = lambda name: Sem(st, nc, name)
        NG = 6
        sGs = [sem(f"sG{i}") for i in range(NG)]
        NH = 6
        sHs = [sem(f"sH{i}") for i in range(NH)]
        NS = 4
        sSs = [sem(f"sS{i}") for i in range(NS)]
        sT = sem("sT")
        sV = sem("sV")
        sC = sem("sC")
        sP = sem("sP")
        gidx = [0]
        hidx = [0]

        def _qdma(pool, idx, q, dst, src):
            s = pool[idx[0] % len(pool)]
            idx[0] += 1
            if s.n:
                q.wait_ge(s.h, s.n)
            q.dma_start(dst, src).then_inc(s.h, 16)
            s.n += 16
            return (s, s.n)

        gidx2 = [0]

        def gdma(g, dst, src):
            return _qdma(sGs, gidx, g, dst, src)

        def hdma(q, dst, src):
            return _qdma(sHs, hidx, q, dst, src)

        def sdma(q, dst, src):
            return _qdma(sSs, gidx2, q, dst, src)

        sbuf = lambda name, shape, dt: st.enter_context(
            nc.sbuf_tensor(name, shape, dt))
        psum = lambda name, shape, dt: st.enter_context(
            nc.psum_tensor(name, shape, dt))

        ident = sbuf("ident", [128, 128], BF16)
        ones1 = sbuf("ones1", [1, 128], BF16)
        e4b = sbuf("e4b", [4, 128], BF16)
        e4tf = sbuf("e4tf", [128, 4], F32)
        e16b = sbuf("e16b", [E, 128], BF16)
        nt_max = maxchunk // 128
        xnat = sbuf("xnat", [128, nt_max * D], BF16)
        x1t = sbuf("x1t", [128, KT * maxchunk], BF16)
        wtb = [sbuf("wtb0", [128, (KT + 1) * NCW], BF16),
               sbuf("wtb1", [128, (KT + 1) * NCW], BF16)]
        at = sbuf("at", [128, KT * 128], BF16)
        bmft = sbuf("bmft", [128, KT * 128], BF16)
        wrt = sbuf("wrt", [128, KT * E], BF16)
        wstg = [sbuf("wstg0", [128, D], BF16), sbuf("wstg1", [128, D], BF16)]
        mstg = [sbuf(f"mstg{i}", [128, D // 2], BF16) for i in range(2)]
        bs2t = [sbuf(f"bs2t{i}", [128, 128], F32) for i in range(4)]
        bm4t = [sbuf("bm4t0", [4, 128], BF16), sbuf("bm4t1", [4, 128], BF16)]
        sgt = [sbuf("sgt0", [4, 128], F32), sbuf("sgt1", [4, 128], F32)]
        q2full = sbuf("q2full", [128, KT * 128], BF16)
        outstg = [sbuf("outstg0", [128, D // 2], F32),
                  sbuf("outstg1", [128, D // 2], F32)]
        g_t = sbuf("g_t", [128, maxchunk], BF16)
        wselt = sbuf("wselt", [E, maxchunk], BF16)
        wsele = sbuf("wsele", [128, maxchunk], BF16)
        bnbt = sbuf("bnbt", [128, nt_max * 96], F32)
        mv = [sbuf("mv0", [128, 2], F32), sbuf("mv1", [128, 2], F32)]
        vpe = [sbuf("vpe0", [128, 1], F32), sbuf("vpe1", [128, 1], F32)]
        stdt = [sbuf("stdt0", [128, 1], F32), sbuf("stdt1", [128, 1], F32)]
        rstdt = sbuf("rstdt", [128, 8], F32)
        nmbt = sbuf("nmbt", [128, 8], F32)
        tlr = [sbuf("tlr0", [128, E], F32), sbuf("tlr1", [128, E], F32)]
        m1 = [sbuf("m10", [128, 1], F32), sbuf("m11", [128, 1], F32)]
        nm1 = [sbuf("nm10", [128, 1], F32), sbuf("nm11", [128, 1], F32)]
        lm = [sbuf("lm0", [128, E], F32), sbuf("lm1", [128, E], F32)]
        m2 = [sbuf("m20", [128, 1], F32), sbuf("m21", [128, 1], F32)]
        exr = [sbuf("exr0", [128, E], F32), sbuf("exr1", [128, E], F32)]
        mskr = [sbuf("mskr0", [128, E], F32), sbuf("mskr1", [128, E], F32)]
        wvr = [sbuf("wvr0", [128, E], F32), sbuf("wvr1", [128, E], F32)]
        smr = [sbuf("smr0", [128, 1], F32), sbuf("smr1", [128, 1], F32)]
        rsr = [sbuf("rsr0", [128, 1], F32), sbuf("rsr1", [128, 1], F32)]
        wselb = [sbuf("wselb0", [128, E], BF16), sbuf("wselb1", [128, E], BF16)]

        pT = [psum("pT0", [128, 1024], BF16), psum("pT1", [128, 1024], BF16)]
        pmm = [psum("pmm0", [128, 512], F32), psum("pmm1", [128, 512], F32),
               psum("pmm2", [128, 512], F32)]
        pRE = psum("pRE", [128, 512], F32)
        pWT = psum("pWT", [128, 1024], BF16)
        pH = psum("pH", [128, 512], F32)
        pQ = pWT
        pTa = [pT[0][:, :], pT[1][:, :], pH[:, :].bitcast(BF16)]

        block = st.enter_context(nc.Block())

        @block.scalar
        def _(sc):
            from concourse.hw_specs import get_activation_tables
            for idx, (nm, funcs) in enumerate(
                    get_activation_tables(nc.m.arch).items()):
                if nm == "natural_log_exp_and_others":
                    sc.add_instruction(mybir.InstLoadActFuncSet(
                        name=nc.get_next_instruction_name(),
                        act_func_set_id=idx, ins=[], outs=[]))
                    break

        state = {
            "pt_free": [None, None, None],
            "pt_i": 0,
            "wstg_free": [None, None],
            "mstg_free": [0, 0],
            "maskpipe_free": [0, 0],
            "wtb_free": [None, None],
            "wtb_scrw": [None, None],
            "bank_free": [0, 0, 0],
            "xnat_free": [],
            "x1t_ready": 0,
            "pq_free": None,
            "pq2_free": None,
            "bm_v": [0, 0],
            "scr_done": [None, None],
            "ln_free": [None, None],
            "bm4_free": [None, None],
            "lnset_free": [None, None],
            "prep_done": [0] * 8,
            "rt_free": [None, None],
            "pre_free": 0,
            "pwt_free": 0,
            "ph_free": 0,
            "g_free": 0,
            "wsel_free": 0,
            "wsele_free": 0,
        }

        tpool_last = [0]

        def tpool_take(t, allow3=True):
            while True:
                k = state["pt_i"] % 3
                state["pt_i"] += 1
                if k == 2 and not allow3:
                    continue
                break
            pf = state["pt_free"][k]
            if pf is not None:
                t.wait_ge(pf[0].h, pf[1])
            tpool_last[0] = k
            return k

        gres = [None]

        @block.gpsimd
        def _(g):
            gres[0] = gdma(g, wstg[0][0:E, :], wr)
        wr_s, wr_v = gres[0]

        @block.sync
        def _(sy):
            hdma(sy, ident[:, :], identc)
            hdma(sy, ones1[:, :], ones1c)
            hdma(sy, e4b[:, :], e4c)
            hdma(sy, e4tf[:, :], e4tc)
            hdma(sy, e16b[:, :], e16c)
        consts_done = tuple((s, s.n) for s in sHs if s.n)

        @block.tensor
        def _(t):
            for s, vv in consts_done:
                t.wait_ge(s.h, vv)
            t.wait_ge(wr_s.h, wr_v)
            k = tpool_take(t)
            mm = None
            for kt in range(KT):
                mm = t.transpose(pTa[k][0:128, kt * E:(kt + 1) * E],
                                 wstg[0][0:E, kt * 128:(kt + 1) * 128],
                                 ident[0:E, 0:E])
            sT.inc(mm)
        tv = sT.n

        @block.vector
        def _(v, tv=tv):
            v.wait_ge(sT.h, tv)
            sV.inc(v.tensor_copy(wrt[:, :], pTa[0][:, 0:KT * E]))
        state["pt_free"][0] = (sV, sV.n)
        wrt_done = sV.n

        gres = [None]

        @block.gpsimd
        def _(g):
            gres[0] = gdma(g, wstg[1][:, :], am)
        am_s, am_v = gres[0]

        for gp in range(4):
            @block.tensor
            def _(t, gp=gp, am_s=am_s, am_v=am_v):
                if gp == 0:
                    t.wait_ge(am_s.h, am_v)
                k = tpool_take(t)
                mm = None
                for j in range(8):
                    kt = gp * 8 + j
                    mm = t.transpose(pTa[k][:, j * 128:(j + 1) * 128],
                                     wstg[1][:, kt * 128:(kt + 1) * 128],
                                     ident[:, :])
                sT.inc(mm)
            tv = sT.n
            kk = tpool_last[0]

            @block.vector
            def _(v, gp=gp, tv=tv, kk=kk):
                v.wait_ge(sT.h, tv)
                sV.inc(v.tensor_copy(at[:, gp * 1024:(gp + 1) * 1024],
                                     pTa[kk][:, :]))
            state["pt_free"][kk] = (sV, sV.n)
        at_done = sV.n

        for e in range(E):
            @block.gpsimd
            def _(g, e=e):
                if e == 0:
                    g.wait_ge(sV.h, wrt_done)
                gdma(g,
                     wstg[0][:, :].rearrange("p (ot c) -> p ot c", c=128)
                     [:, :, e * R:(e + 1) * R],
                     bmm[e, :, :].rearrange("(ot p) r -> p ot r", p=128))
        bmfb_waits = tuple((s, s.n) for s in sGs if s.n)

        for gp in range(4):
            @block.tensor
            def _(t, gp=gp, bw=bmfb_waits):
                if gp == 0:
                    for s, vv in bw:
                        t.wait_ge(s.h, vv)
                k = tpool_take(t)
                mm = None
                for j in range(8):
                    ot = gp * 8 + j
                    mm = t.transpose(pTa[k][:, j * 128:(j + 1) * 128],
                                     wstg[0][:, ot * 128:(ot + 1) * 128],
                                     ident[:, :])
                sT.inc(mm)
            tv = sT.n
            kk = tpool_last[0]

            @block.vector
            def _(v, gp=gp, tv=tv, kk=kk):
                v.wait_ge(sT.h, tv)
                sV.inc(v.tensor_scalar(bmft[:, gp * 1024:(gp + 1) * 1024],
                                       pTa[kk][:, :], 2.0, None, ALU.mult))
            state["pt_free"][kk] = (sV, sV.n)
        smalls_done_T = sT.n

        state["wstg_free"] = [(sT, smalls_done_T), (sT, smalls_done_T)]

        @block.vector
        def _(v):
            sV.inc(v.memset(wtb[0][:, KT * NCW:(KT + 1) * NCW], 0.0))
            sV.inc(v.memset(wtb[1][:, KT * NCW:(KT + 1) * NCW], 0.0))
        wtb_ms = sV.n


        q2_ready = [None] * KT
        red_ready = [0] * KT
        bs2_free = [None] * 4

        def maskpipe_a(ntt):
            s4 = ntt % 4
            for hf in range(2):
                sl = (ntt * 2 + hf) % 2
                gres = [None]

                @block.gpsimd
                def _(g, ntt=ntt, hf=hf, sl=sl):
                    if state["mstg_free"][sl]:
                        g.wait_ge(sV.h, state["mstg_free"][sl])
                    gres[0] = gdma(
                        g, mstg[sl][:, :],
                        mask[ntt * 128:(ntt + 1) * 128,
                             hf * (D // 2):(hf + 1) * (D // 2)])
                gs, gv = gres[0]

                @block.vector
                def _(v, hf=hf, sl=sl, s4=s4, gs=gs, gv=gv,
                      bf2=(bs2_free[s4] if hf == 0 else None)):
                    v.wait_ge(gs.h, gv)
                    if bf2 is not None:
                        v.wait_ge(bf2[0].h, bf2[1])
                    sV.inc(v.tensor_reduce(
                        bs2t[s4][:, hf * 64:(hf + 1) * 64].unsqueeze(2),
                        mstg[sl][:, :].rearrange("p (kb j) -> p kb j", j=32),
                        AX, ALU.add))
                state["mstg_free"][sl] = sV.n
            red_ready[ntt] = sV.n

        def maskpipe_b1(ntt):
            sl = ntt % 2
            s4 = ntt % 4
            pQf = pQ[:, :].bitcast(F32)

            @block.tensor
            def _(t, s4=s4, red_v=red_ready[ntt]):
                t.wait_ge(sV.h, red_v)
                pqf = state["pq_free"]
                if pqf is not None:
                    t.wait_ge(pqf[0].h, pqf[1])
                sT.inc(t.matmul(pQf[0:4, 0:128], e4tf[:, :],
                                bs2t[s4][:, :], start=True, stop=True))
            tv = sT.n
            bs2_free[s4] = (sT, sT.n)

            @block.scalar
            def _(sc, sl=sl, tv=tv, bmf=state["bm4_free"][sl]):
                sc.wait_ge(sT.h, tv)
                if bmf is not None:
                    sc.wait_ge(bmf[0].h, bmf[1])
                sC.inc(sc.activation(sgt[sl][0:4, :], pQf[0:4, 0:128],
                                     AF.Sign))
                sc.wait_ge(sC.h, sC.n)
                sC.inc(sc.activation(bm4t[sl][0:4, :], sgt[sl][0:4, :],
                                     AF.Relu))
            state["bm_v"][sl] = sC.n
            state["pq_free"] = (sC, sC.n)

        def maskpipe_b2(ntt):
            sl = ntt % 2
            pQf = pQ[:, :].bitcast(F32)

            @block.tensor
            def _(t, sl=sl, bm_v=state["bm_v"][sl]):
                t.wait_ge(sC.h, bm_v)
                pqf = state["pq2_free"]
                if pqf is not None:
                    t.wait_ge(pqf[0].h, pqf[1])
                sT.inc(t.matmul(pQf[:, 128:256], e4b[0:4, :],
                                bm4t[sl][0:4, :], start=True, stop=True))
            tv = sT.n
            state["bm4_free"][sl] = (sT, sT.n)

            @block.scalar
            def _(sc, ntt=ntt, tv=tv):
                sc.wait_ge(sT.h, tv)
                sC.inc(sc.activation(q2full[:, ntt * 128:(ntt + 1) * 128],
                                     pQf[:, 128:256], AF.Identity))
            state["pq2_free"] = (sC, sC.n)
            q2_ready[ntt] = (sC, sC.n)

        def gen_subtile(L, ntt):
            sl = ntt % 2
            wsrc = w0 if L == 0 else wb
            fw = state["wstg_free"][sl]
            gres = [None]

            @block.gpsimd
            def _(g, ntt=ntt, sl=sl, wsrc=wsrc, fw=fw):
                if fw is not None:
                    g.wait_ge(fw[0].h, fw[1])
                gres[0] = gdma(g, wstg[sl][:, :],
                               wsrc[ntt * 128:(ntt + 1) * 128, :])
            gs, gv = gres[0]
            if L == 0:
                @block.vector
                def _(v, ntt=ntt, sl=sl, gs=gs, gv=gv, q2v=q2_ready[ntt]):
                    v.wait_ge(gs.h, gv)
                    v.wait_ge(q2v[0].h, q2v[1])
                    sV.inc(v.tensor_tensor(
                        wstg[sl][:, :].rearrange("p (kb j) -> p kb j", j=32),
                        wstg[sl][:, :].rearrange("p (kb j) -> p kb j", j=32),
                        q2full[:, ntt * 128:(ntt + 1) * 128]
                        .unsqueeze(2).broadcast_to([128, 128, 32]),
                        ALU.mult))
                return (sV, sV.n)
            return (gs, gv)

        def gen_transpose_batch(nci, sub, gp, ready):
            sl2 = (nci * 2 + sub) % 2
            slw = nci % 2

            @block.tensor
            def _(t, gp=gp, sl2=sl2, ready=ready):
                if gp == 0:
                    t.wait_ge(ready[0].h, ready[1])
                k = tpool_take(t)
                mm = None
                for j in range(8):
                    kt = gp * 8 + j
                    mm = t.transpose(pTa[k][:, j * 128:(j + 1) * 128],
                                     wstg[sl2][:, kt * 128:(kt + 1) * 128],
                                     ident[:, :])
                sT.inc(mm)
            tv = sT.n
            kk = tpool_last[0]

            @block.scalar
            def _(sc, gp=gp, sub=sub, tv=tv, kk=kk, slw=slw,
                  scrw=state["wtb_scrw"][slw]):
                sc.wait_ge(sT.h, tv)
                if scrw is not None:
                    sc.wait_ge(scrw[0].h, scrw[1])
                sC.inc(sc.activation(
                    wtb[slw][:, :].rearrange("p (kt n) -> p kt n", n=NCW)
                    [:, gp * 8:(gp + 1) * 8, sub * 128:sub * 128 + 128],
                    pTa[kk][:, :].rearrange("p (j n) -> p j n", n=128),
                    AF.Identity))
            state["pt_free"][kk] = (sC, sC.n)
            if gp == 3:
                state["wstg_free"][sl2] = (sT, tv)

        def prep_nci(L, nci):
            sl = nci % 2
            batches = []
            for sub in range(2):
                ntt = nci * 2 + sub
                ready = gen_subtile(L, ntt)
                for gp in range(4):
                    batches.append(
                        lambda nci=nci, sub=sub, gp=gp, ready=ready:
                        gen_transpose_batch(nci, sub, gp, ready))
            bsrc = ba if L == 0 else bf_
            gres = [None]

            @block.gpsimd
            def _(g, nci=nci, sl=sl, bsrc=bsrc,
                  scrw=state["wtb_scrw"][sl],
                  wfw=state["wtb_free"][sl]):
                if scrw is not None:
                    g.wait_ge(scrw[0].h, scrw[1])
                else:
                    g.wait_ge(sV.h, wtb_ms)
                if wfw is not None:
                    g.wait_ge(wfw[0].h, wfw[1])
                gres[0] = gdma(
                    g, wtb[sl][0:1, KT * NCW:KT * NCW + NCW],
                    bsrc[0:1, nci * NCW:(nci + 1) * NCW])
            return batches, gres[0]

        def mm_tile(L, nci, nt, i, lora, wt_ready, extra_t_waits=(),
                    lora_gate=None):
            sl = nci % 2
            bk = (state.setdefault("bk_i", [0]))[0] % 3
            state["bk_i"][0] += 1

            @block.tensor
            def _(t, nci=nci, sl=sl, i=i, bk=bk, lora=lora,
                  wt_ready=wt_ready, bf_val=state["bank_free"][bk],
                  xr=state["x1t_ready"], ew=tuple(extra_t_waits),
                  lora_gate=lora_gate):
                if i == 0 and wt_ready is not None:
                    t.wait_ge(wt_ready[0].h, wt_ready[1])
                if i == 0:
                    for s, vv in ew:
                        t.wait_ge(s.h, vv)
                    t.wait_ge(sV.h, xr)
                if bf_val:
                    t.wait_ge(sV.h, bf_val)
                t.matmul(pmm[bk][:, 0:NCW], ones1[0:1, :],
                         wtb[sl][0:1, KT * NCW:KT * NCW + NCW],
                         start=True, stop=False)
                mm = None
                for kt in range(KT):
                    mm = t.matmul(
                        pmm[bk][:, 0:NCW],
                        x1t[:, kt * maxchunk + i * 128:
                            kt * maxchunk + (i + 1) * 128],
                        wtb[sl][:, kt * NCW:(kt + 1) * NCW],
                        start=False, stop=(kt == KT - 1 and lora is None))
                if lora is not None:
                    if lora_gate is not None and nci == 0 and i == 0:
                        t.wait_ge(lora_gate[0].h, lora_gate[1])
                    mm = t.matmul(pmm[bk][:, 0:NCW],
                                  lora[:, i * 128:(i + 1) * 128],
                                  bmft[:, nci * NCW:(nci + 1) * NCW],
                                  start=False, stop=True)
                sT.inc(mm)
            tv = sT.n

            @block.vector
            def _(v, nci=nci, i=i, bk=bk, tv=tv):
                v.wait_ge(sT.h, tv)
                sl_ap = xnat[:, i * D + nci * NCW:i * D + (nci + 1) * NCW]
                sV.inc(v.scalar_tensor_tensor(
                    sl_ap, pmm[bk][:, 0:NCW], 1.0, sl_ap,
                    ALU.mult, ALU.add))
                v.wait_ge(sV.h, sV.n)
                sV.inc(v.bn_stats(
                    bnbt[:, i * 96 + nci * 6:i * 96 + (nci + 1) * 6],
                    sl_ap))
            state["bank_free"][bk] = sV.n - 1
            if nci == NNC - 1:
                ln_prep(i, i % 2)
            if i == nt - 1:
                state["wtb_free"][sl] = (sT, tv)

        def nci_matmuls(L, nci, nt, lora, wt_ready, extra_t_waits=(),
                        lora_gate=None):
            for i in range(nt):
                mm_tile(L, nci, nt, i, lora, wt_ready, extra_t_waits,
                        lora_gate)

        def layer_pass(L, c, nt, lora, lora_gate=None):
            if c == 0 and L == 0:
                for ntt in range(4):
                    maskpipe_a(ntt)
                for ntt in range(2):
                    maskpipe_b1(ntt)
                    maskpipe_b2(ntt)
                    maskpipe_a(ntt + 4)
                for ntt in range(2, 6):
                    maskpipe_b1(ntt)
                    maskpipe_b2(ntt)
                    maskpipe_a(ntt + 4)
            if c == 0:
                batches, bias_rdy = prep_nci(L, 0)
                for b in batches:
                    b()
                ev = sC.n
            for nci in range(NNC):
                sl = nci % 2
                if c == 0:
                    if L == 0:
                        for ntt in (nci * 2 + 6, nci * 2 + 7):
                            if ntt < KT:
                                maskpipe_b1(ntt)
                    if nci + 1 < NNC:
                        nbatches, nbias = prep_nci(L, nci + 1)
                    else:
                        nbatches, nbias = [], None
                    done = 0
                    for i in range(nt):
                        mm_tile(L, nci, nt, i, lora, bias_rdy,
                                extra_t_waits=[(sC, ev)],
                                lora_gate=lora_gate)
                        want = len(nbatches) * (i + 1) // nt
                        while done < want:
                            nbatches[done]()
                            done += 1
                    while done < len(nbatches):
                        nbatches[done]()
                        done += 1
                    nev = sC.n
                    hres = [None]

                    @block.sync
                    def _(sy, L=L, nci=nci, sl=sl, ev=ev, br=bias_rdy):
                        sy.wait_ge(sC.h, ev)
                        sy.wait_ge(br[0].h, br[1])
                        hres[0] = hdma(
                            sy, wts[L, nci, :, :, :].transpose([1, 0, 2]),
                            wtb[sl][:, :].rearrange("p (kt n) -> p kt n",
                                                    n=NCW))
                    state["wtb_scrw"][sl] = hres[0]
                    state["scr_done"][L] = hres[0]
                    ev = nev
                    bias_rdy = nbias
                    if L == 0:
                        for ntt in (nci * 2 + 6, nci * 2 + 7):
                            if ntt < KT:
                                maskpipe_b2(ntt)
                        for ntt in (nci * 2 + 8, nci * 2 + 9):
                            if ntt < KT:
                                maskpipe_a(ntt)
                else:
                    fw = state["wtb_free"][sl]
                    sd = state["scr_done"][L]
                    scrw = state["wtb_scrw"][sl]
                    hres = [None]

                    @block.sync
                    def _(sy, L=L, nci=nci, sl=sl, fw=fw, sd=sd, scrw=scrw):
                        if nci < 2 and sd is not None:
                            sy.wait_ge(sd[0].h, sd[1])
                        if scrw is not None:
                            sy.wait_ge(scrw[0].h, scrw[1])
                        if fw is not None:
                            sy.wait_ge(fw[0].h, fw[1])
                        hres[0] = hdma(
                            sy,
                            wtb[sl][:, :].rearrange("p (kt n) -> p kt n",
                                                    n=NCW),
                            wts[L, nci, :, :, :].transpose([1, 0, 2]))
                    nci_matmuls(L, nci, nt, lora, hres[0],
                                lora_gate=lora_gate)

        def transpose_tile(i, waits, allow3=True):
            for gp in range(4):
                @block.tensor
                def _(t, i=i, gp=gp, waits=tuple(waits), allow3=allow3):
                    if gp == 0:
                        for s, vv in waits:
                            t.wait_ge(s.h, vv)
                    k = tpool_take(t, allow3)
                    mm = None
                    for j in range(8):
                        kt = gp * 8 + j
                        mm = t.transpose(
                            pTa[k][:, j * 128:(j + 1) * 128],
                            xnat[:, i * D + kt * 128:i * D + (kt + 1) * 128],
                            ident[:, :])
                    sT.inc(mm)
                tv = sT.n
                kk = tpool_last[0]

                @block.vector
                def _(v, i=i, gp=gp, tv=tv, kk=kk):
                    v.wait_ge(sT.h, tv)
                    sV.inc(v.tensor_copy(
                        x1t[:, :].rearrange("p (kt t) -> p kt t", t=maxchunk)
                        [:, gp * 8:(gp + 1) * 8, i * 128:(i + 1) * 128],
                        pTa[kk][:, :].rearrange("p (j t) -> p j t", t=128)))
                state["pt_free"][kk] = (sV, sV.n)
            return sV.n

        def input_chunk(c, t0, nt):
            for i in range(nt):
                in_gv = [None]

                @block.gpsimd
                def _(g, i=i, t0=t0):
                    xf = (state["xnat_free"][i]
                          if i < len(state["xnat_free"]) else None)
                    if xf is not None:
                        g.wait_ge(xf[0].h, xf[1])
                    in_gv[0] = gdma(
                        g, xnat[:, i * D:(i + 1) * D],
                        xin[t0 + i * 128:t0 + (i + 1) * 128, :])
                transpose_tile(i, [in_gv[0]])
            state["x1t_ready"] = sV.n

        def ln_prep(i, iset):
            lf = state["lnset_free"][iset]
            st_v = sV.n

            @block.vector
            def _(v, i=i, iset=iset, lf=lf, st_v=st_v):
                v.wait_ge(sV.h, st_v)
                if lf is not None:
                    v.wait_ge(lf[0].h, lf[1])
                sV.inc(v.bn_aggr(mv[iset][:, :],
                                 bnbt[:, i * 96:(i + 1) * 96]))
                v.wait_ge(sV.h, sV.n)
                sV.inc(v.tensor_scalar(vpe[iset][:, :], mv[iset][:, 1:2],
                                       EPS, None, ALU.add))
            ve_v = sV.n

            @block.scalar
            def _(sc, i=i, iset=iset, ve_v=ve_v):
                sc.wait_ge(sV.h, ve_v)
                sC.inc(sc.activation(stdt[iset][:, :], vpe[iset][:, :],
                                     AF.Ln))
                sc.wait_ge(sC.h, sC.n)
                sC.inc(sc.activation(rstdt[:, i:i + 1], stdt[iset][:, :],
                                     AF.Exp, scale=-0.5))
            sq_v = sC.n

            @block.vector
            def _(v, i=i, iset=iset, sq_v=sq_v):
                v.wait_ge(sC.h, sq_v)
                sV.inc(v.tensor_scalar(nmbt[:, i:i + 1], mv[iset][:, 0:1],
                                       rstdt[:, i:i + 1], -1.0,
                                       ALU.mult, ALU.mult))
            state["lnset_free"][iset] = (sV, sV.n)
            state["prep_done"][i] = sV.n

        def ln_apply(i, write_out=None, t0=0):
            pd = state["prep_done"][i]
            if write_out is not None:
                ofree = state.setdefault("outstg_free", [None, None])
                of0, of1 = ofree[0], ofree[1]

                @block.scalar
                def _(sc, i=i, pd=pd, of=of0):
                    sc.wait_ge(sV.h, pd)
                    if of is not None:
                        sc.wait_ge(of[0].h, of[1])
                    sC.inc(sc.activation(
                        outstg[0][:, :],
                        xnat[:, i * D:i * D + 2048],
                        AF.Identity, bias=nmbt[:, i:i + 1],
                        scale=rstdt[:, i:i + 1]))
                    sc.wait_ge(sC.h, sC.n)
                    sdma(sc,
                         write_out[t0 + i * 128:t0 + (i + 1) * 128, 0:2048],
                         outstg[0][:, :])
                ofree[0] = (sSs[(gidx2[0] - 1) % NS],
                            sSs[(gidx2[0] - 1) % NS].n)

                @block.vector
                def _(v, i=i, pd=pd, of=of1):
                    v.wait_ge(sV.h, pd)
                    if of is not None:
                        v.wait_ge(of[0].h, of[1])
                    sV.inc(v.tensor_scalar(
                        outstg[1][:, :],
                        xnat[:, i * D + 2048:i * D + 4096],
                        rstdt[:, i:i + 1], nmbt[:, i:i + 1],
                        ALU.mult, ALU.add))
                vv = sV.n
                hres = [None]

                @block.sync
                def _(sy, i=i, vv=vv):
                    sy.wait_ge(sV.h, vv)
                    hres[0] = hdma(
                        sy,
                        write_out[t0 + i * 128:t0 + (i + 1) * 128, 2048:4096],
                        outstg[1][:, :])
                ofree[1] = hres[0]
                return [(sC, sC.n), (sV, sV.n)]
            else:
                @block.scalar
                def _(sc, i=i, pd=pd):
                    sc.wait_ge(sV.h, pd)
                    sC.inc(sc.activation(
                        xnat[:, i * D:i * D + 2048],
                        xnat[:, i * D:i * D + 2048],
                        AF.Identity, bias=nmbt[:, i:i + 1],
                        scale=rstdt[:, i:i + 1]))

                @block.vector
                def _(v, i=i, pd=pd):
                    v.wait_ge(sV.h, pd)
                    sV.inc(v.tensor_scalar(
                        xnat[:, i * D + 2048:i * D + 4096],
                        xnat[:, i * D + 2048:i * D + 4096],
                        rstdt[:, i:i + 1], nmbt[:, i:i + 1],
                        ALU.mult, ALU.add))
                return [(sC, sC.n), (sV, sV.n)]

        def router_tile(i, iset, x2t_ok):
            rf = state["rt_free"][iset]

            @block.tensor
            def _(t, i=i, x2t_ok=x2t_ok, pref=state["pre_free"]):
                t.wait_ge(sV.h, x2t_ok)
                if pref:
                    t.wait_ge(sV.h, pref)
                mm = None
                for kt in range(KT):
                    mm = t.matmul(pRE[:, 0:E],
                                  x1t[:, kt * maxchunk + i * 128:
                                      kt * maxchunk + (i + 1) * 128],
                                  wrt[:, kt * E:(kt + 1) * E],
                                  start=(kt == 0), stop=(kt == KT - 1))
                sT.inc(mm)
            tv = sT.n

            @block.vector
            def _(v, iset=iset, tv=tv, rf=rf):
                v.wait_ge(sT.h, tv)
                if rf is not None:
                    v.wait_ge(rf[0].h, rf[1])
                sV.inc(v.tensor_copy(tlr[iset][:, :], pRE[:, 0:E]))
            state["pre_free"] = sV.n
            cp_v = sV.n

            @block.scalar
            def _(sc, iset=iset, cp_v=cp_v):
                sc.wait_ge(sV.h, cp_v)
                sC.inc(sc.activation(exr[iset][:, :], tlr[iset][:, :],
                                     AF.Exp))
            ex_v = sC.n

            @block.vector
            def _(v, iset=iset, ex_v=ex_v, cp_v=cp_v):
                def stepv(inst):
                    sV.inc(inst)
                    v.wait_ge(sV.h, sV.n)
                v.wait_ge(sV.h, cp_v)
                stepv(v.reduce_max(m1[iset][:, :], tlr[iset][:, :], axis=AX))
                stepv(v.tensor_scalar(mskr[iset][:, :], tlr[iset][:, :],
                                      m1[iset][:, 0:1], None, ALU.is_ge))
                stepv(v.scalar_tensor_tensor(lm[iset][:, :], mskr[iset][:, :],
                                             -1e30, tlr[iset][:, :],
                                             ALU.mult, ALU.add))
                stepv(v.reduce_max(m2[iset][:, :], lm[iset][:, :], axis=AX))
                stepv(v.tensor_scalar(mskr[iset][:, :], tlr[iset][:, :],
                                      m2[iset][:, 0:1], None, ALU.is_ge))
                v.wait_ge(sC.h, ex_v)
                stepv(v.tensor_tensor(wvr[iset][:, :], exr[iset][:, :],
                                      mskr[iset][:, :], ALU.mult))
                stepv(v.reduce_sum(smr[iset][:, :], wvr[iset][:, :], axis=AX))
                stepv(v.reciprocal(rsr[iset][:, :], smr[iset][:, :]))
                stepv(v.tensor_scalar(wselb[iset][:, :], wvr[iset][:, :],
                                      rsr[iset][:, 0:1], None, ALU.mult))
            wb_v = sV.n
            state["rt_free"][iset] = (sV, sV.n)
            return wb_v

        def router_finish(i, iset, wb_v):
            @block.tensor
            def _(t, iset=iset, wb_v=wb_v, pwf=state["pwt_free"],
                  pqf=state["pq2_free"]):
                t.wait_ge(sV.h, wb_v)
                if pwf:
                    t.wait_ge(sV.h, pwf)
                if pqf is not None:
                    t.wait_ge(pqf[0].h, pqf[1])
                sT.inc(t.transpose(pWT[0:E, 0:128], wselb[iset][:, :],
                                   ident[:, :]))
            tw_v = sT.n

            @block.vector
            def _(v, i=i, tw_v=tw_v):
                v.wait_ge(sT.h, tw_v)
                sV.inc(v.tensor_copy(wselt[0:E, i * 128:(i + 1) * 128],
                                     pWT[0:E, 0:128]))
            state["pwt_free"] = sV.n

        t0s = []
        acc = 0
        for cs in chunk_sizes:
            t0s.append(acc)
            acc += cs

        input_chunk(0, t0s[0], chunk_sizes[0] // 128)
        state["x1t_ready"] = sV.n
        for c, chunk in enumerate(chunk_sizes):
            nt = chunk // 128
            t0 = t0s[c]

            layer_pass(0, c, nt, None)
            attn_done_v = sV.n

            ph_evac = state["ph_free"]
            pending = None

            def finish_one(pending):
                i_, iset_, wbv_ = pending
                router_finish(i_, iset_, wbv_)
                wt_v = sV.n

                @block.tensor
                def _(t, i=i_, wt_v=wt_v, ph_evac=ph_evac,
                      p2f=state["pt_free"][2]):
                    t.wait_ge(sV.h, wt_v)
                    if ph_evac:
                        t.wait_ge(sV.h, ph_evac)
                    if p2f is not None:
                        t.wait_ge(p2f[0].h, p2f[1])
                    sT.inc(t.matmul(
                        pH[:, (i % 4) * 128:(i % 4 + 1) * 128],
                        e16b[0:E, :],
                        wselt[0:E, i * 128:(i + 1) * 128],
                        start=True, stop=True))
                return sT.n

            evac_group = []
            for i in range(nt):
                apw = ln_apply(i)
                x2t_i = transpose_tile(i, apw, allow3=False)
                wb_v = router_tile(i, i % 2, x2t_i)
                if pending is not None:
                    tv = finish_one(pending)
                    evac_group.append((pending[0], tv))
                pending = (i, i % 2, wb_v)
                if evac_group and (evac_group[-1][0] % 4 == 3):
                    g0 = evac_group[0][0]
                    gi = evac_group[-1][0]
                    tv = evac_group[-1][1]

                    @block.vector
                    def _(v, g0=g0, gi=gi, tv=tv):
                        v.wait_ge(sT.h, tv)
                        sV.inc(v.tensor_copy(
                            wsele[:, g0 * 128:(gi + 1) * 128],
                            pH[:, (g0 % 4) * 128:(gi % 4 + 1) * 128]))
                    ph_evac = sV.n
                    evac_group = []
            if pending is not None:
                tv = finish_one(pending)
                evac_group.append((pending[0], tv))
            if evac_group:
                g0 = evac_group[0][0]
                gi = evac_group[-1][0]
                tv = evac_group[-1][1]

                @block.vector
                def _(v, g0=g0, gi=gi, tv=tv):
                    v.wait_ge(sT.h, tv)
                    sV.inc(v.tensor_copy(
                        wsele[:, g0 * 128:(gi + 1) * 128],
                        pH[:, (g0 % 4) * 128:(gi % 4 + 1) * 128]))
                ph_evac = sV.n
            state["x1t_ready"] = sV.n
            wsele_done = sV.n

            hws = []
            off = 0
            while off < chunk:
                w_ = min(512, chunk - off)
                hws.append((off, w_))
                off += w_

            for off, w_ in hws:
                @block.tensor
                def _(t, off=off, w_=w_, phf=state["ph_free"],
                      wd=wsele_done):
                    t.wait_ge(sV.h, wd)
                    if phf:
                        t.wait_ge(sV.h, phf)
                    mm = None
                    for kt in range(KT):
                        mm = t.matmul(pH[:, 0:w_],
                                      at[:, kt * 128:(kt + 1) * 128],
                                      x1t[:, kt * maxchunk + off:
                                          kt * maxchunk + off + w_],
                                      start=(kt == 0), stop=(kt == KT - 1))
                    sT.inc(mm)
                tv = sT.n

                @block.vector
                def _(v, off=off, w_=w_, tv=tv, wd=wsele_done):
                    v.wait_ge(sT.h, tv)
                    v.wait_ge(sV.h, wd)
                    sV.inc(v.tensor_tensor(g_t[:, off:off + w_],
                                           pH[:, 0:w_],
                                           wsele[:, off:off + w_], ALU.mult))
                state["ph_free"] = sV.n
            g_done = sV.n
            state["pt_free"][2] = (sV, g_done)
            state["x1t_ready"] = wsele_done

            layer_pass(1, c, nt, g_t, lora_gate=(sV, g_done))
            state["g_free"] = sT.n
            ffn_done_v = sV.n

            nxt_nt = (chunk_sizes[c + 1] // 128
                      if c + 1 < len(chunk_sizes) else 0)
            assert nxt_nt <= nt
            for i in range(nt):
                apw = ln_apply(i, write_out=out, t0=t0)
                if i < nxt_nt:
                    in_g = [None]

                    @block.gpsimd
                    def _(g, i=i, apw=tuple(apw), nt0=t0s[c + 1]):
                        for ss, vv in apw:
                            g.wait_ge(ss.h, vv)
                        in_g[0] = gdma(
                            g, xnat[:, i * D:(i + 1) * D],
                            xin[nt0 + i * 128:nt0 + (i + 1) * 128, :])
                    transpose_tile(i, [in_g[0]])
            if nxt_nt:
                state["x1t_ready"] = sV.n

        fin = tuple((s, s.n) for s in (sHs + sGs + sSs) if s.n)

        @block.sync
        def _(sy):
            for s, vv in fin:
                sy.wait_ge(s.h, vv)

        @block.gpsimd
        def _(g):
            for s, vv in fin:
                g.wait_ge(s.h, vv)

    return nc


def kernel(L0_attn_data, L0_attn_mask, L0_attn_weight, L0_attn_bias,
           L0_ffn_Wb, L0_ffn_bias, L0_ffn_Wr, L0_ffn_A, L0_ffn_Bm):
    t_total = L0_attn_data.shape[0] * L0_attn_data.shape[1]
    t_core = t_total // NCORES
    x = np.ascontiguousarray(
        np.asarray(L0_attn_data, dtype=np.float32).reshape(t_total, D))
    common = {
        "mask": np.ascontiguousarray(np.asarray(L0_attn_mask,
                                                dtype=np.int32)),
        "w0": np.ascontiguousarray(np.asarray(L0_attn_weight,
                                              dtype=np.float32)),
        "ba": np.ascontiguousarray(
            np.asarray(L0_attn_bias, dtype=np.float32).reshape(1, D)),
        "wb": np.ascontiguousarray(np.asarray(L0_ffn_Wb, dtype=np.float32)),
        "bf": np.ascontiguousarray(
            np.asarray(L0_ffn_bias, dtype=np.float32).reshape(1, D)),
        "wr": np.ascontiguousarray(np.asarray(L0_ffn_Wr, dtype=np.float32)),
        "am": np.ascontiguousarray(
            np.asarray(L0_ffn_A, dtype=np.float32).reshape(ER, D)),
        "bm": np.ascontiguousarray(np.asarray(L0_ffn_Bm, dtype=np.float32)),
    }
    in_maps = []
    for c in range(NCORES):
        m = dict(common)
        m["xin"] = np.ascontiguousarray(x[c * t_core:(c + 1) * t_core])
        in_maps.append(m)
    nc = build_program(t_core, [768, 768, 512])
    res = run_bass_kernel_spmd(nc, in_maps, core_ids=list(range(NCORES)))
    shards = [np.asarray(r["out"]) for r in res.results]
    full = np.concatenate(shards, axis=0)
    return full.reshape(L0_attn_data.shape[0], L0_attn_data.shape[1], D)


# version 15
# speedup vs baseline: 1.0910x; 1.0205x over previous
import contextlib

import numpy as np
import ml_dtypes

import concourse.bass as bass
import concourse.mybir as mybir
from concourse.bass_utils import run_bass_kernel_spmd

F32 = mybir.dt.float32
BF16 = mybir.dt.bfloat16
I32 = mybir.dt.int32
AX = mybir.AxisListType.X
ALU = mybir.AluOpType
AF = mybir.ActivationFunctionType
BF = ml_dtypes.bfloat16

D = 4096
KT = D // 128
NCW = 256
NNC = D // NCW
E = 8
R = 16
ER = E * R
EPS = 1e-5
NCORES = 8


class Sem:
    def __init__(self, stack, nc, name):
        self.h = stack.enter_context(nc.semaphore(name))
        self.n = 0

    def inc(self, inst, k=1):
        inst.then_inc(self.h, k)
        self.n += k
        return self.n


def build_program(t_core, chunk_sizes):
    assert sum(chunk_sizes) == t_core
    maxchunk = max(chunk_sizes)

    nc = bass.Bass("TRN2", target_bir_lowering=False)

    xin = nc.dram_tensor("xin", [t_core, D], F32, kind="ExternalInput").ap()
    mask = nc.dram_tensor("mask", [D, D], I32, kind="ExternalInput").ap()
    w0 = nc.dram_tensor("w0", [D, D], F32, kind="ExternalInput").ap()
    ba = nc.dram_tensor("ba", [1, D], F32, kind="ExternalInput").ap()
    wb = nc.dram_tensor("wb", [D, D], F32, kind="ExternalInput").ap()
    bf_ = nc.dram_tensor("bf", [1, D], F32, kind="ExternalInput").ap()
    wr = nc.dram_tensor("wr", [E, D], F32, kind="ExternalInput").ap()
    am = nc.dram_tensor("am", [ER, D], F32, kind="ExternalInput").ap()
    bmm = nc.dram_tensor("bm", [E, D, R], F32, kind="ExternalInput").ap()
    out = nc.dram_tensor("out", [t_core, D], F32, kind="ExternalOutput").ap()
    wts = nc.dram_tensor("wts", [2, NNC, KT + 1, 128, NCW], BF16,
                         kind="Internal").ap()

    identc = nc.inline_tensor(np.eye(128, dtype=BF), name="identc").ap()
    e4 = np.zeros((4, 128), dtype=BF)
    for b in range(4):
        e4[b, b * 32:(b + 1) * 32] = 1
    e4c = nc.inline_tensor(e4, name="e4c").ap()
    e4t = np.zeros((128, 4), dtype=np.float32)
    for b in range(4):
        e4t[b * 32:(b + 1) * 32, b] = 1
    e4tc = nc.inline_tensor(e4t, name="e4tc").ap()
    e16 = np.zeros((E, 128), dtype=BF)
    for e in range(E):
        e16[e, e * R:(e + 1) * R] = 1
    e16c = nc.inline_tensor(e16, name="e16c").ap()
    ones1c = nc.inline_tensor(np.ones((1, 128), dtype=BF), name="ones1c").ap()

    with contextlib.ExitStack() as st:
        sem = lambda name: Sem(st, nc, name)
        NG = 6
        sGs = [sem(f"sG{i}") for i in range(NG)]
        NH = 6
        sHs = [sem(f"sH{i}") for i in range(NH)]
        NS = 4
        sSs = [sem(f"sS{i}") for i in range(NS)]
        sT = sem("sT")
        sV = sem("sV")
        sC = sem("sC")
        sP = sem("sP")
        gidx = [0]
        hidx = [0]

        def _qdma(pool, idx, q, dst, src):
            s = pool[idx[0] % len(pool)]
            idx[0] += 1
            if s.n:
                q.wait_ge(s.h, s.n)
            q.dma_start(dst, src).then_inc(s.h, 16)
            s.n += 16
            return (s, s.n)

        gidx2 = [0]

        def gdma(g, dst, src):
            return _qdma(sGs, gidx, g, dst, src)

        def hdma(q, dst, src):
            return _qdma(sHs, hidx, q, dst, src)

        def sdma(q, dst, src):
            return _qdma(sSs, gidx2, q, dst, src)

        sbuf = lambda name, shape, dt: st.enter_context(
            nc.sbuf_tensor(name, shape, dt))
        psum = lambda name, shape, dt: st.enter_context(
            nc.psum_tensor(name, shape, dt))

        ident = sbuf("ident", [128, 128], BF16)
        ones1 = sbuf("ones1", [1, 128], BF16)
        e4b = sbuf("e4b", [4, 128], BF16)
        e4tf = sbuf("e4tf", [128, 4], F32)
        e16b = sbuf("e16b", [E, 128], BF16)
        nt_max = maxchunk // 128
        xnat = sbuf("xnat", [128, nt_max * D], BF16)
        x1t = sbuf("x1t", [128, KT * maxchunk], BF16)
        wtb = [sbuf("wtb0", [128, (KT + 1) * NCW], BF16),
               sbuf("wtb1", [128, (KT + 1) * NCW], BF16)]
        at = sbuf("at", [128, KT * 128], BF16)
        bmft = sbuf("bmft", [128, KT * 128], BF16)
        wrt = sbuf("wrt", [128, KT * E], BF16)
        wstg = [sbuf("wstg0", [128, D], BF16), sbuf("wstg1", [128, D], BF16)]
        mstg = [sbuf(f"mstg{i}", [128, D // 2], BF16) for i in range(3)]
        bs2t = [sbuf(f"bs2t{i}", [128, 128], F32) for i in range(3)]
        bm4t = [sbuf("bm4t0", [4, 128], BF16), sbuf("bm4t1", [4, 128], BF16)]
        sgt0 = sbuf("sgt0", [4, 128], F32)
        sgt = [sgt0, sgt0]
        q2full = sbuf("q2full", [128, KT * 128], BF16)
        outstg = [sbuf("outstg0", [128, D // 2], F32),
                  sbuf("outstg1", [128, D // 2], F32)]
        g_t = sbuf("g_t", [128, maxchunk], BF16)
        wselt = sbuf("wselt", [E, maxchunk], BF16)
        wsele = sbuf("wsele", [128, maxchunk], BF16)
        bnbt = sbuf("bnbt", [128, nt_max * 48], F32)
        mv = [sbuf("mv0", [128, 2], F32), sbuf("mv1", [128, 2], F32)]
        vpe = [sbuf("vpe0", [128, 1], F32), sbuf("vpe1", [128, 1], F32)]
        stdt = [sbuf("stdt0", [128, 1], F32), sbuf("stdt1", [128, 1], F32)]
        rstdt = sbuf("rstdt", [128, 8], F32)
        nmbt = sbuf("nmbt", [128, 8], F32)
        tlr = [sbuf("tlr0", [128, E], F32), sbuf("tlr1", [128, E], F32)]
        m1 = [sbuf("m10", [128, 1], F32), sbuf("m11", [128, 1], F32)]
        lm = [sbuf("lm0", [128, E], F32), sbuf("lm1", [128, E], F32)]
        m2 = [sbuf("m20", [128, 1], F32), sbuf("m21", [128, 1], F32)]
        exr = [sbuf("exr0", [128, E], F32), sbuf("exr1", [128, E], F32)]
        mskr = [sbuf("mskr0", [128, E], F32), sbuf("mskr1", [128, E], F32)]
        wvr = [sbuf("wvr0", [128, E], F32), sbuf("wvr1", [128, E], F32)]
        smr = [sbuf("smr0", [128, 1], F32), sbuf("smr1", [128, 1], F32)]
        rsr = [sbuf("rsr0", [128, 1], F32), sbuf("rsr1", [128, 1], F32)]
        wselb = [sbuf("wselb0", [128, E], BF16), sbuf("wselb1", [128, E], BF16)]

        pT = [psum("pT0", [128, 1024], BF16), psum("pT1", [128, 1024], BF16)]
        pmm = [psum("pmm0", [128, 512], F32), psum("pmm1", [128, 512], F32),
               psum("pmm2", [128, 512], F32)]
        pRE = psum("pRE", [128, 512], F32)
        pWT = psum("pWT", [128, 1024], BF16)
        pH = psum("pH", [128, 512], F32)
        pQ = pWT
        pQb = [pWT, pRE]
        pTa = [pT[0][:, :], pT[1][:, :], pH[:, :].bitcast(BF16)]

        block = st.enter_context(nc.Block())

        @block.scalar
        def _(sc):
            from concourse.hw_specs import get_activation_tables
            for idx, (nm, funcs) in enumerate(
                    get_activation_tables(nc.m.arch).items()):
                if nm == "natural_log_exp_and_others":
                    sc.add_instruction(mybir.InstLoadActFuncSet(
                        name=nc.get_next_instruction_name(),
                        act_func_set_id=idx, ins=[], outs=[]))
                    break

        state = {
            "pt_free": [None, None, None],
            "pt_i": 0,
            "wstg_free": [None, None],
            "mstg_free": [0, 0, 0],
            "maskpipe_free": [0, 0],
            "wtb_free": [None, None],
            "wtb_scrw": [None, None],
            "bank_free": [0, 0, 0],
            "xnat_free": [],
            "x1t_ready": [],
            "pq_free": [None, None],
            "pq2_free": [None, None],
            "bm_v": [0, 0],
            "scr_done": [None, None],
            "ln_free": [None, None],
            "bm4_free": [None, None],
            "lnset_free": [None, None],
            "prep_done": [0] * 8,
            "rt_free": [None, None],
            "pre_free": 0,
            "pwt_free": None,
            "ph_free": 0,
            "g_free": 0,
            "wsel_free": 0,
            "wsele_free": 0,
        }

        tpool_last = [0]

        def tpool_take(t, allow3=True):
            while True:
                k = state["pt_i"] % 3
                state["pt_i"] += 1
                if k == 2 and not allow3:
                    continue
                break
            pf = state["pt_free"][k]
            if pf is not None:
                t.wait_ge(pf[0].h, pf[1])
            tpool_last[0] = k
            return k

        gres = [None]

        @block.gpsimd
        def _(g):
            gres[0] = gdma(g, wstg[0][0:E, :], wr)
        wr_s, wr_v = gres[0]

        @block.sync
        def _(sy):
            hdma(sy, ident[:, :], identc)
            hdma(sy, ones1[:, :], ones1c)
            hdma(sy, e4b[:, :], e4c)
            hdma(sy, e4tf[:, :], e4tc)
            hdma(sy, e16b[:, :], e16c)
        consts_done = tuple((s, s.n) for s in sHs if s.n)

        @block.tensor
        def _(t):
            for s, vv in consts_done:
                t.wait_ge(s.h, vv)
            t.wait_ge(wr_s.h, wr_v)
            k = tpool_take(t)
            mm = None
            for kt in range(KT):
                mm = t.transpose(pTa[k][0:128, kt * E:(kt + 1) * E],
                                 wstg[0][0:E, kt * 128:(kt + 1) * 128],
                                 ident[0:E, 0:E])
            sT.inc(mm)
        tv = sT.n

        @block.vector
        def _(v, tv=tv):
            v.wait_ge(sT.h, tv)
            sV.inc(v.tensor_copy(wrt[:, :], pTa[0][:, 0:KT * E]))
        state["pt_free"][0] = (sV, sV.n)
        wrt_done = sV.n

        gres = [None]

        @block.gpsimd
        def _(g):
            gres[0] = gdma(g, wstg[1][:, :], am)
        am_s, am_v = gres[0]

        for gp in range(4):
            @block.tensor
            def _(t, gp=gp, am_s=am_s, am_v=am_v):
                if gp == 0:
                    t.wait_ge(am_s.h, am_v)
                k = tpool_take(t)
                mm = None
                for j in range(8):
                    kt = gp * 8 + j
                    mm = t.transpose(pTa[k][:, j * 128:(j + 1) * 128],
                                     wstg[1][:, kt * 128:(kt + 1) * 128],
                                     ident[:, :])
                sT.inc(mm)
            tv = sT.n
            kk = tpool_last[0]

            @block.vector
            def _(v, gp=gp, tv=tv, kk=kk):
                v.wait_ge(sT.h, tv)
                sV.inc(v.tensor_copy(at[:, gp * 1024:(gp + 1) * 1024],
                                     pTa[kk][:, :]))
            state["pt_free"][kk] = (sV, sV.n)
        at_done = sV.n

        for e in range(E):
            @block.gpsimd
            def _(g, e=e):
                if e == 0:
                    g.wait_ge(sV.h, wrt_done)
                gdma(g,
                     wstg[0][:, :].rearrange("p (ot c) -> p ot c", c=128)
                     [:, :, e * R:(e + 1) * R],
                     bmm[e, :, :].rearrange("(ot p) r -> p ot r", p=128))
        bmfb_waits = tuple((s, s.n) for s in sGs if s.n)

        for gp in range(4):
            @block.tensor
            def _(t, gp=gp, bw=bmfb_waits):
                if gp == 0:
                    for s, vv in bw:
                        t.wait_ge(s.h, vv)
                k = tpool_take(t)
                mm = None
                for j in range(8):
                    ot = gp * 8 + j
                    mm = t.transpose(pTa[k][:, j * 128:(j + 1) * 128],
                                     wstg[0][:, ot * 128:(ot + 1) * 128],
                                     ident[:, :])
                sT.inc(mm)
            tv = sT.n
            kk = tpool_last[0]

            @block.vector
            def _(v, gp=gp, tv=tv, kk=kk):
                v.wait_ge(sT.h, tv)
                sV.inc(v.tensor_scalar(bmft[:, gp * 1024:(gp + 1) * 1024],
                                       pTa[kk][:, :], 2.0, None, ALU.mult))
            state["pt_free"][kk] = (sV, sV.n)
        smalls_done_T = sT.n

        state["wstg_free"] = [(sT, smalls_done_T), (sT, smalls_done_T)]

        @block.vector
        def _(v):
            sV.inc(v.memset(wtb[0][:, KT * NCW:(KT + 1) * NCW], 0.0))
            sV.inc(v.memset(wtb[1][:, KT * NCW:(KT + 1) * NCW], 0.0))
        wtb_ms = sV.n


        q2_ready = [None] * KT
        red_ready = [0] * KT
        bs2_free = [None] * 3

        def maskpipe_a(ntt):
            s4 = ntt % 3
            for hf in range(2):
                sl = (ntt * 2 + hf) % 3
                gres = [None]

                @block.gpsimd
                def _(g, ntt=ntt, hf=hf, sl=sl):
                    if state["mstg_free"][sl]:
                        g.wait_ge(sV.h, state["mstg_free"][sl])
                    gres[0] = gdma(
                        g, mstg[sl][:, :],
                        mask[ntt * 128:(ntt + 1) * 128,
                             hf * (D // 2):(hf + 1) * (D // 2)])
                gs, gv = gres[0]

                @block.vector
                def _(v, hf=hf, sl=sl, s4=s4, gs=gs, gv=gv,
                      bf2=(bs2_free[s4] if hf == 0 else None)):
                    v.wait_ge(gs.h, gv)
                    if bf2 is not None:
                        v.wait_ge(bf2[0].h, bf2[1])
                    sV.inc(v.tensor_reduce(
                        bs2t[s4][:, hf * 64:(hf + 1) * 64].unsqueeze(2),
                        mstg[sl][:, :].rearrange("p (kb j) -> p kb j", j=32),
                        AX, ALU.add))
                state["mstg_free"][sl] = sV.n
            red_ready[ntt] = sV.n

        def maskpipe_b1(ntt):
            sl = ntt % 2
            s4 = ntt % 3
            pQf = pQb[sl][:, :] if sl else pQb[0][:, :].bitcast(F32)
            if sl == 0:
                pQf = pQb[0][:, :].bitcast(F32)
            else:
                pQf = pQb[1][:, :]

            @block.tensor
            def _(t, s4=s4, sl=sl, pQf=pQf, red_v=red_ready[ntt]):
                t.wait_ge(sV.h, red_v)
                pqf = state["pq_free"][sl]
                if pqf is not None:
                    t.wait_ge(pqf[0].h, pqf[1])
                sT.inc(t.matmul(pQf[0:4, 0:128], e4tf[:, :],
                                bs2t[s4][:, :], start=True, stop=True))
            tv = sT.n
            bs2_free[s4] = (sT, sT.n)

            @block.scalar
            def _(sc, sl=sl, tv=tv, pQf=pQf, bmf=state["bm4_free"][sl],
                  sgf=sC.n):
                sc.wait_ge(sT.h, tv)
                if bmf is not None:
                    sc.wait_ge(bmf[0].h, bmf[1])
                if sgf:
                    sc.wait_ge(sC.h, sgf)
                sC.inc(sc.activation(sgt[sl][0:4, :], pQf[0:4, 0:128],
                                     AF.Sign))
                sc.wait_ge(sC.h, sC.n)
                sC.inc(sc.activation(bm4t[sl][0:4, :], sgt[sl][0:4, :],
                                     AF.Relu))
            state["bm_v"][sl] = sC.n
            state["pq_free"][sl] = (sC, sC.n)

        def maskpipe_b2(ntt):
            sl = ntt % 2
            if sl == 0:
                pQf = pQb[0][:, :].bitcast(F32)
            else:
                pQf = pQb[1][:, :]

            @block.tensor
            def _(t, sl=sl, pQf=pQf, bm_v=state["bm_v"][sl]):
                t.wait_ge(sC.h, bm_v)
                pqf = state["pq2_free"][sl]
                if pqf is not None:
                    t.wait_ge(pqf[0].h, pqf[1])
                sT.inc(t.matmul(pQf[:, 128:256], e4b[0:4, :],
                                bm4t[sl][0:4, :], start=True, stop=True))
            tv = sT.n
            state["bm4_free"][sl] = (sT, sT.n)

            @block.scalar
            def _(sc, ntt=ntt, tv=tv, pQf=pQf):
                sc.wait_ge(sT.h, tv)
                sC.inc(sc.activation(q2full[:, ntt * 128:(ntt + 1) * 128],
                                     pQf[:, 128:256], AF.Identity))
            state["pq2_free"][sl] = (sC, sC.n)
            q2_ready[ntt] = (sC, sC.n)

        def gen_subtile(L, ntt):
            sl = ntt % 2
            wsrc = w0 if L == 0 else wb
            fw = state["wstg_free"][sl]
            gres = [None]

            @block.gpsimd
            def _(g, ntt=ntt, sl=sl, wsrc=wsrc, fw=fw):
                if fw is not None:
                    g.wait_ge(fw[0].h, fw[1])
                gres[0] = gdma(g, wstg[sl][:, :],
                               wsrc[ntt * 128:(ntt + 1) * 128, :])
            gs, gv = gres[0]
            if L == 0:
                @block.vector
                def _(v, ntt=ntt, sl=sl, gs=gs, gv=gv, q2v=q2_ready[ntt]):
                    v.wait_ge(gs.h, gv)
                    v.wait_ge(q2v[0].h, q2v[1])
                    sV.inc(v.tensor_tensor(
                        wstg[sl][:, :].rearrange("p (kb j) -> p kb j", j=32),
                        wstg[sl][:, :].rearrange("p (kb j) -> p kb j", j=32),
                        q2full[:, ntt * 128:(ntt + 1) * 128]
                        .unsqueeze(2).broadcast_to([128, 128, 32]),
                        ALU.mult))
                return (sV, sV.n)
            return (gs, gv)

        def gen_transpose_batch(nci, sub, gp, ready):
            sl2 = (nci * 2 + sub) % 2
            slw = nci % 2

            @block.tensor
            def _(t, gp=gp, sl2=sl2, ready=ready):
                if gp == 0:
                    t.wait_ge(ready[0].h, ready[1])
                k = tpool_take(t)
                mm = None
                for j in range(8):
                    kt = gp * 8 + j
                    mm = t.transpose(pTa[k][:, j * 128:(j + 1) * 128],
                                     wstg[sl2][:, kt * 128:(kt + 1) * 128],
                                     ident[:, :])
                sT.inc(mm)
            tv = sT.n
            kk = tpool_last[0]

            @block.scalar
            def _(sc, gp=gp, sub=sub, tv=tv, kk=kk, slw=slw,
                  scrw=state["wtb_scrw"][slw]):
                sc.wait_ge(sT.h, tv)
                if scrw is not None:
                    sc.wait_ge(scrw[0].h, scrw[1])
                sC.inc(sc.activation(
                    wtb[slw][:, :].rearrange("p (kt n) -> p kt n", n=NCW)
                    [:, gp * 8:(gp + 1) * 8, sub * 128:sub * 128 + 128],
                    pTa[kk][:, :].rearrange("p (j n) -> p j n", n=128),
                    AF.Identity))
            state["pt_free"][kk] = (sC, sC.n)
            if gp == 3:
                state["wstg_free"][sl2] = (sT, tv)

        def prep_nci(L, nci):
            sl = nci % 2
            bsrc = ba if L == 0 else bf_
            gres = [None]

            @block.gpsimd
            def _(g, nci=nci, sl=sl, bsrc=bsrc,
                  scrw=state["wtb_scrw"][sl],
                  wfw=state["wtb_free"][sl]):
                if scrw is not None:
                    g.wait_ge(scrw[0].h, scrw[1])
                else:
                    g.wait_ge(sV.h, wtb_ms)
                if wfw is not None:
                    g.wait_ge(wfw[0].h, wfw[1])
                gres[0] = gdma(
                    g, wtb[sl][0:1, KT * NCW:KT * NCW + NCW],
                    bsrc[0:1, nci * NCW:(nci + 1) * NCW])
            batches = []
            for sub in range(2):
                ntt = nci * 2 + sub
                ready = gen_subtile(L, ntt)
                for gp in range(4):
                    batches.append(
                        lambda nci=nci, sub=sub, gp=gp, ready=ready:
                        gen_transpose_batch(nci, sub, gp, ready))
            return batches, gres[0]

        def mm_tile(L, nci, nt, i, lora, wt_ready, extra_t_waits=(),
                    lora_gate=None):
            sl = nci % 2
            bk = (state.setdefault("bk_i", [0]))[0] % 3
            state["bk_i"][0] += 1

            @block.tensor
            def _(t, nci=nci, sl=sl, i=i, bk=bk, lora=lora,
                  wt_ready=wt_ready, bf_val=state["bank_free"][bk],
                  xr=state["x1t_ready"], ew=tuple(extra_t_waits),
                  lora_gate=lora_gate):
                if i == 0 and wt_ready is not None:
                    t.wait_ge(wt_ready[0].h, wt_ready[1])
                if i == 0:
                    for s, vv in ew:
                        t.wait_ge(s.h, vv)
                    for s, vv in xr:
                        t.wait_ge(s.h, vv)
                if bf_val:
                    t.wait_ge(sV.h, bf_val)
                t.matmul(pmm[bk][:, 0:NCW], ones1[0:1, :],
                         wtb[sl][0:1, KT * NCW:KT * NCW + NCW],
                         start=True, stop=False)
                mm = None
                for kt in range(KT):
                    mm = t.matmul(
                        pmm[bk][:, 0:NCW],
                        x1t[:, kt * maxchunk + i * 128:
                            kt * maxchunk + (i + 1) * 128],
                        wtb[sl][:, kt * NCW:(kt + 1) * NCW],
                        start=False, stop=(kt == KT - 1 and lora is None))
                if lora is not None:
                    if lora_gate is not None and nci == 0 and i == 0:
                        t.wait_ge(lora_gate[0].h, lora_gate[1])
                    mm = t.matmul(pmm[bk][:, 0:NCW],
                                  lora[:, i * 128:(i + 1) * 128],
                                  bmft[:, nci * NCW:(nci + 1) * NCW],
                                  start=False, stop=True)
                sT.inc(mm)
            tv = sT.n

            evv = [0]

            @block.vector
            def _(v, nci=nci, i=i, bk=bk, tv=tv):
                v.wait_ge(sT.h, tv)
                sl_ap = xnat[:, i * D + nci * NCW:i * D + (nci + 1) * NCW]
                sV.inc(v.scalar_tensor_tensor(
                    sl_ap, pmm[bk][:, 0:NCW], 1.0, sl_ap,
                    ALU.mult, ALU.add))
                evv[0] = sV.n
                if nci % 2 == 1:
                    v.wait_ge(sV.h, sV.n)
                    sV.inc(v.bn_stats(
                        bnbt[:, i * 48 + (nci // 2) * 6:
                             i * 48 + (nci // 2 + 1) * 6],
                        xnat[:, i * D + (nci - 1) * NCW:
                             i * D + (nci + 1) * NCW]))
            state["bank_free"][bk] = evv[0]
            if nci == NNC - 1:
                ln_prep(i, i % 2)
            if i == nt - 1:
                state["wtb_free"][sl] = (sT, tv)

        def nci_matmuls(L, nci, nt, lora, wt_ready, extra_t_waits=(),
                        lora_gate=None):
            for i in range(nt):
                mm_tile(L, nci, nt, i, lora, wt_ready, extra_t_waits,
                        lora_gate)

        def layer_pass(L, c, nt, lora, lora_gate=None):
            if c == 0 and L == 0:
                for ntt in (0, 1, 2):
                    maskpipe_a(ntt)
                maskpipe_b1(0)
                maskpipe_b1(1)
                maskpipe_b2(0)
                maskpipe_b2(1)
                maskpipe_a(3)
                maskpipe_a(4)
                maskpipe_b1(2)
                maskpipe_b1(3)
                maskpipe_b2(2)
                maskpipe_b2(3)
                maskpipe_a(5)
                maskpipe_a(6)
                maskpipe_b1(4)
                maskpipe_b1(5)
                maskpipe_b2(4)
                maskpipe_b2(5)
                maskpipe_a(7)
                maskpipe_a(8)
                maskpipe_b1(6)
                maskpipe_b1(7)
                maskpipe_a(9)
            if c == 0:
                batches, bias_rdy = prep_nci(L, 0)
                for b in batches:
                    b()
                ev = sC.n
            for nci in range(NNC):
                sl = nci % 2
                if c == 0:
                    if L == 0:
                        for ntt in (nci * 2 + 6, nci * 2 + 7):
                            if ntt < KT:
                                maskpipe_b2(ntt)
                        for ntt in (nci * 2 + 8, nci * 2 + 9):
                            if ntt < KT:
                                maskpipe_b1(ntt)
                    if nci + 1 < NNC:
                        nbatches, nbias = prep_nci(L, nci + 1)
                    else:
                        nbatches, nbias = [], None
                    done = 0
                    for i in range(nt):
                        mm_tile(L, nci, nt, i, lora, bias_rdy,
                                extra_t_waits=[(sC, ev)],
                                lora_gate=lora_gate)
                        want = len(nbatches) * (i + 1) // nt
                        while done < want:
                            nbatches[done]()
                            done += 1
                    while done < len(nbatches):
                        nbatches[done]()
                        done += 1
                    nev = sC.n
                    hres = [None]

                    @block.sync
                    def _(sy, L=L, nci=nci, sl=sl, ev=ev, br=bias_rdy):
                        sy.wait_ge(sC.h, ev)
                        sy.wait_ge(br[0].h, br[1])
                        hres[0] = hdma(
                            sy, wts[L, nci, :, :, :].transpose([1, 0, 2]),
                            wtb[sl][:, :].rearrange("p (kt n) -> p kt n",
                                                    n=NCW))
                    state["wtb_scrw"][sl] = hres[0]
                    state["scr_done"][L] = hres[0]
                    ev = nev
                    bias_rdy = nbias
                    if L == 0:
                        for ntt in (nci * 2 + 10, nci * 2 + 11):
                            if ntt < KT:
                                maskpipe_a(ntt)
                else:
                    fw = state["wtb_free"][sl]
                    sd = state["scr_done"][L]
                    scrw = state["wtb_scrw"][sl]
                    hres = [None]

                    @block.sync
                    def _(sy, L=L, nci=nci, sl=sl, fw=fw, sd=sd, scrw=scrw):
                        if nci < 2 and sd is not None:
                            sy.wait_ge(sd[0].h, sd[1])
                        if scrw is not None:
                            sy.wait_ge(scrw[0].h, scrw[1])
                        if fw is not None:
                            sy.wait_ge(fw[0].h, fw[1])
                        hres[0] = hdma(
                            sy,
                            wtb[sl][:, :].rearrange("p (kt n) -> p kt n",
                                                    n=NCW),
                            wts[L, nci, :, :, :].transpose([1, 0, 2]))
                    nci_matmuls(L, nci, nt, lora, hres[0],
                                lora_gate=lora_gate)

        def transpose_tile(i, waits, allow3=True, waits_hi=None):
            for gp in range(4):
                @block.tensor
                def _(t, i=i, gp=gp, waits=tuple(waits),
                      waits_hi=tuple(waits_hi if waits_hi is not None
                                     else waits), allow3=allow3):
                    if gp == 0:
                        for s, vv in waits:
                            t.wait_ge(s.h, vv)
                    if gp == 2 and waits_hi != waits:
                        for s, vv in waits_hi:
                            t.wait_ge(s.h, vv)
                    k = tpool_take(t, allow3)
                    mm = None
                    for j in range(8):
                        kt = gp * 8 + j
                        mm = t.transpose(
                            pTa[k][:, j * 128:(j + 1) * 128],
                            xnat[:, i * D + kt * 128:i * D + (kt + 1) * 128],
                            ident[:, :])
                    sT.inc(mm)
                tv = sT.n
                kk = tpool_last[0]
                dst = x1t[:, :].rearrange("p (kt t) -> p kt t", t=maxchunk)[
                    :, gp * 8:(gp + 1) * 8, i * 128:(i + 1) * 128]
                src = pTa[kk][:, :].rearrange("p (j t) -> p j t", t=128)
                if gp % 2 == 0:
                    @block.vector
                    def _(v, tv=tv, dst=dst, src=src):
                        v.wait_ge(sT.h, tv)
                        sV.inc(v.tensor_copy(dst, src))
                    state["pt_free"][kk] = (sV, sV.n)
                else:
                    @block.scalar
                    def _(sc, tv=tv, dst=dst, src=src):
                        sc.wait_ge(sT.h, tv)
                        sC.inc(sc.activation(dst, src, AF.Identity))
                    state["pt_free"][kk] = (sC, sC.n)
            return (sV.n, sC.n)

        def input_chunk(c, t0, nt):
            for i in range(nt):
                in_gv = [None, None]

                @block.gpsimd
                def _(g, i=i, t0=t0):
                    xf = (state["xnat_free"][i]
                          if i < len(state["xnat_free"]) else None)
                    if xf is not None:
                        g.wait_ge(xf[0].h, xf[1])
                    in_gv[0] = gdma(
                        g, xnat[:, i * D:i * D + 2048],
                        xin[t0 + i * 128:t0 + (i + 1) * 128, 0:2048])
                    in_gv[1] = gdma(
                        g, xnat[:, i * D + 2048:(i + 1) * D],
                        xin[t0 + i * 128:t0 + (i + 1) * 128, 2048:D])
                transpose_tile(i, [in_gv[0]], waits_hi=[in_gv[1]])
            state["x1t_ready"] = [(sV, sV.n), (sC, sC.n)]

        def ln_prep(i, iset):
            lf = state["lnset_free"][iset]
            st_v = sV.n

            @block.vector
            def _(v, i=i, iset=iset, lf=lf, st_v=st_v):
                v.wait_ge(sV.h, st_v)
                if lf is not None:
                    v.wait_ge(lf[0].h, lf[1])
                sV.inc(v.bn_aggr(mv[iset][:, :],
                                 bnbt[:, i * 48:(i + 1) * 48]))
                v.wait_ge(sV.h, sV.n)
                sV.inc(v.tensor_scalar(vpe[iset][:, :], mv[iset][:, 1:2],
                                       EPS, None, ALU.add))
            ve_v = sV.n

            @block.scalar
            def _(sc, i=i, iset=iset, ve_v=ve_v):
                sc.wait_ge(sV.h, ve_v)
                sC.inc(sc.activation(stdt[iset][:, :], vpe[iset][:, :],
                                     AF.Ln))
                sc.wait_ge(sC.h, sC.n)
                sC.inc(sc.activation(rstdt[:, i:i + 1], stdt[iset][:, :],
                                     AF.Exp, scale=-0.5))
            sq_v = sC.n

            @block.vector
            def _(v, i=i, iset=iset, sq_v=sq_v):
                v.wait_ge(sC.h, sq_v)
                sV.inc(v.tensor_scalar(nmbt[:, i:i + 1], mv[iset][:, 0:1],
                                       rstdt[:, i:i + 1], -1.0,
                                       ALU.mult, ALU.mult))
            state["lnset_free"][iset] = (sV, sV.n)
            state["prep_done"][i] = sV.n

        def ln_apply(i, write_out=None, t0=0):
            pd = state["prep_done"][i]
            if write_out is not None:
                ofree = state.setdefault("outstg_free", [None, None])
                of0, of1 = ofree[0], ofree[1]

                @block.scalar
                def _(sc, i=i, pd=pd, of=of0):
                    sc.wait_ge(sV.h, pd)
                    if of is not None:
                        sc.wait_ge(of[0].h, of[1])
                    sC.inc(sc.activation(
                        outstg[0][:, :],
                        xnat[:, i * D:i * D + 2048],
                        AF.Identity, bias=nmbt[:, i:i + 1],
                        scale=rstdt[:, i:i + 1]))
                    sc.wait_ge(sC.h, sC.n)
                    sdma(sc,
                         write_out[t0 + i * 128:t0 + (i + 1) * 128, 0:2048],
                         outstg[0][:, :])
                ofree[0] = (sSs[(gidx2[0] - 1) % NS],
                            sSs[(gidx2[0] - 1) % NS].n)

                @block.vector
                def _(v, i=i, pd=pd, of=of1):
                    v.wait_ge(sV.h, pd)
                    if of is not None:
                        v.wait_ge(of[0].h, of[1])
                    sV.inc(v.tensor_scalar(
                        outstg[1][:, :],
                        xnat[:, i * D + 2048:i * D + 4096],
                        rstdt[:, i:i + 1], nmbt[:, i:i + 1],
                        ALU.mult, ALU.add))
                vv = sV.n
                hres = [None]

                @block.sync
                def _(sy, i=i, vv=vv):
                    sy.wait_ge(sV.h, vv)
                    hres[0] = hdma(
                        sy,
                        write_out[t0 + i * 128:t0 + (i + 1) * 128, 2048:4096],
                        outstg[1][:, :])
                ofree[1] = hres[0]
                return [(sC, sC.n), (sV, sV.n)]
            else:
                @block.scalar
                def _(sc, i=i, pd=pd):
                    sc.wait_ge(sV.h, pd)
                    sC.inc(sc.activation(
                        xnat[:, i * D:i * D + 2048],
                        xnat[:, i * D:i * D + 2048],
                        AF.Identity, bias=nmbt[:, i:i + 1],
                        scale=rstdt[:, i:i + 1]))

                @block.vector
                def _(v, i=i, pd=pd):
                    v.wait_ge(sV.h, pd)
                    sV.inc(v.tensor_scalar(
                        xnat[:, i * D + 2048:i * D + 4096],
                        xnat[:, i * D + 2048:i * D + 4096],
                        rstdt[:, i:i + 1], nmbt[:, i:i + 1],
                        ALU.mult, ALU.add))
                return [(sC, sC.n), (sV, sV.n)]

        def router_tile(i, iset, x2t_ok):
            rf = state["rt_free"][iset]

            @block.tensor
            def _(t, i=i, x2t_ok=tuple(x2t_ok), pref=state["pre_free"],
                  pq1=state["pq_free"][1], pq2=state["pq2_free"][1]):
                for s, vv in x2t_ok:
                    t.wait_ge(s.h, vv)
                if pref:
                    t.wait_ge(sV.h, pref)
                for q in (pq1, pq2):
                    if q is not None:
                        t.wait_ge(q[0].h, q[1])
                mm = None
                for kt in range(KT):
                    mm = t.matmul(pRE[:, 0:E],
                                  x1t[:, kt * maxchunk + i * 128:
                                      kt * maxchunk + (i + 1) * 128],
                                  wrt[:, kt * E:(kt + 1) * E],
                                  start=(kt == 0), stop=(kt == KT - 1))
                sT.inc(mm)
            tv = sT.n

            @block.vector
            def _(v, iset=iset, tv=tv, rf=rf):
                v.wait_ge(sT.h, tv)
                if rf is not None:
                    v.wait_ge(rf[0].h, rf[1])
                sV.inc(v.tensor_copy(tlr[iset][:, :], pRE[:, 0:E]))
            state["pre_free"] = sV.n
            cp_v = sV.n

            @block.scalar
            def _(sc, iset=iset, cp_v=cp_v):
                sc.wait_ge(sV.h, cp_v)
                sC.inc(sc.activation(exr[iset][:, :], tlr[iset][:, :],
                                     AF.Exp))
            ex_v = sC.n

            @block.vector
            def _(v, iset=iset, ex_v=ex_v, cp_v=cp_v):
                def stepv(inst):
                    sV.inc(inst)
                    v.wait_ge(sV.h, sV.n)
                v.wait_ge(sV.h, cp_v)
                stepv(v.reduce_max(m1[iset][:, :], tlr[iset][:, :], axis=AX))
                stepv(v.tensor_scalar(mskr[iset][:, :], tlr[iset][:, :],
                                      m1[iset][:, 0:1], None, ALU.is_ge))
                stepv(v.scalar_tensor_tensor(lm[iset][:, :], mskr[iset][:, :],
                                             -1e30, tlr[iset][:, :],
                                             ALU.mult, ALU.add))
                stepv(v.reduce_max(m2[iset][:, :], lm[iset][:, :], axis=AX))
                stepv(v.tensor_scalar(mskr[iset][:, :], tlr[iset][:, :],
                                      m2[iset][:, 0:1], None, ALU.is_ge))
                v.wait_ge(sC.h, ex_v)
                stepv(v.tensor_tensor(wvr[iset][:, :], exr[iset][:, :],
                                      mskr[iset][:, :], ALU.mult))
                stepv(v.reduce_sum(smr[iset][:, :], wvr[iset][:, :], axis=AX))
                stepv(v.reciprocal(rsr[iset][:, :], smr[iset][:, :]))
                stepv(v.tensor_scalar(wselb[iset][:, :], wvr[iset][:, :],
                                      rsr[iset][:, 0:1], None, ALU.mult))
            wb_v = sV.n
            state["rt_free"][iset] = (sV, sV.n)
            return wb_v

        def router_finish(i, iset, wb_v):
            @block.tensor
            def _(t, iset=iset, wb_v=wb_v, pwf=state["pwt_free"],
                  pqf=state["pq2_free"][0]):
                t.wait_ge(sV.h, wb_v)
                if pwf is not None:
                    t.wait_ge(pwf[0].h, pwf[1])
                if pqf is not None:
                    t.wait_ge(pqf[0].h, pqf[1])
                sT.inc(t.transpose(pWT[0:E, 0:128], wselb[iset][:, :],
                                   ident[:, :]))
            tw_v = sT.n

            @block.scalar
            def _(sc, i=i, tw_v=tw_v):
                sc.wait_ge(sT.h, tw_v)
                sC.inc(sc.activation(wselt[0:E, i * 128:(i + 1) * 128],
                                     pWT[0:E, 0:128], AF.Identity))
            state["pwt_free"] = (sC, sC.n)

        t0s = []
        acc = 0
        for cs in chunk_sizes:
            t0s.append(acc)
            acc += cs

        input_chunk(0, t0s[0], chunk_sizes[0] // 128)
        for c, chunk in enumerate(chunk_sizes):
            nt = chunk // 128
            t0 = t0s[c]

            layer_pass(0, c, nt, None)
            attn_done_v = sV.n

            ph_evac = state["ph_free"]
            pending = None

            def finish_one(pending):
                i_, iset_, wbv_ = pending
                router_finish(i_, iset_, wbv_)
                wt_v = sC.n

                @block.tensor
                def _(t, i=i_, wt_v=wt_v, ph_evac=ph_evac,
                      p2f=state["pt_free"][2]):
                    t.wait_ge(sC.h, wt_v)
                    if ph_evac:
                        t.wait_ge(sV.h, ph_evac)
                    if p2f is not None:
                        t.wait_ge(p2f[0].h, p2f[1])
                    sT.inc(t.matmul(
                        pH[:, (i % 4) * 128:(i % 4 + 1) * 128],
                        e16b[0:E, :],
                        wselt[0:E, i * 128:(i + 1) * 128],
                        start=True, stop=True))
                return sT.n

            evac_group = []
            for i in range(nt):
                apw = ln_apply(i)
                x2t_i = transpose_tile(i, [apw[0]], allow3=False,
                                       waits_hi=[apw[1]])
                wb_v = router_tile(i, i % 2, [(sV, x2t_i[0]),
                                              (sC, x2t_i[1])])
                if pending is not None:
                    tv = finish_one(pending)
                    evac_group.append((pending[0], tv))
                pending = (i, i % 2, wb_v)
                if evac_group and (evac_group[-1][0] % 4 == 3):
                    g0 = evac_group[0][0]
                    gi = evac_group[-1][0]
                    tv = evac_group[-1][1]

                    @block.vector
                    def _(v, g0=g0, gi=gi, tv=tv):
                        v.wait_ge(sT.h, tv)
                        sV.inc(v.tensor_copy(
                            wsele[:, g0 * 128:(gi + 1) * 128],
                            pH[:, (g0 % 4) * 128:(gi % 4 + 1) * 128]))
                    ph_evac = sV.n
                    evac_group = []
            if pending is not None:
                tv = finish_one(pending)
                evac_group.append((pending[0], tv))
            if evac_group:
                g0 = evac_group[0][0]
                gi = evac_group[-1][0]
                tv = evac_group[-1][1]

                @block.vector
                def _(v, g0=g0, gi=gi, tv=tv):
                    v.wait_ge(sT.h, tv)
                    sV.inc(v.tensor_copy(
                        wsele[:, g0 * 128:(gi + 1) * 128],
                        pH[:, (g0 % 4) * 128:(gi % 4 + 1) * 128]))
                ph_evac = sV.n
            state["x1t_ready"] = [(sV, sV.n), (sC, sC.n)]
            wsele_done = sV.n

            hws = []
            off = 0
            while off < chunk:
                w_ = min(512, chunk - off)
                hws.append((off, w_))
                off += w_

            for off, w_ in hws:
                @block.tensor
                def _(t, off=off, w_=w_, phf=state["ph_free"],
                      wd=wsele_done):
                    t.wait_ge(sV.h, wd)
                    if phf:
                        t.wait_ge(sV.h, phf)
                    mm = None
                    for kt in range(KT):
                        mm = t.matmul(pH[:, 0:w_],
                                      at[:, kt * 128:(kt + 1) * 128],
                                      x1t[:, kt * maxchunk + off:
                                          kt * maxchunk + off + w_],
                                      start=(kt == 0), stop=(kt == KT - 1))
                    sT.inc(mm)
                tv = sT.n

                @block.vector
                def _(v, off=off, w_=w_, tv=tv, wd=wsele_done):
                    v.wait_ge(sT.h, tv)
                    v.wait_ge(sV.h, wd)
                    sV.inc(v.tensor_tensor(g_t[:, off:off + w_],
                                           pH[:, 0:w_],
                                           wsele[:, off:off + w_], ALU.mult))
                state["ph_free"] = sV.n
            g_done = sV.n
            state["pt_free"][2] = (sV, g_done)

            layer_pass(1, c, nt, g_t, lora_gate=(sV, g_done))
            state["g_free"] = sT.n
            ffn_done_v = sV.n

            nxt_nt = (chunk_sizes[c + 1] // 128
                      if c + 1 < len(chunk_sizes) else 0)
            assert nxt_nt <= nt
            for i in range(nt):
                apw = ln_apply(i, write_out=out, t0=t0)
                if i < nxt_nt:
                    in_g = [None, None]

                    @block.gpsimd
                    def _(g, i=i, apw=tuple(apw), nt0=t0s[c + 1]):
                        g.wait_ge(apw[0][0].h, apw[0][1])
                        in_g[0] = gdma(
                            g, xnat[:, i * D:i * D + 2048],
                            xin[nt0 + i * 128:nt0 + (i + 1) * 128, 0:2048])
                        g.wait_ge(apw[1][0].h, apw[1][1])
                        in_g[1] = gdma(
                            g, xnat[:, i * D + 2048:(i + 1) * D],
                            xin[nt0 + i * 128:nt0 + (i + 1) * 128, 2048:D])
                    transpose_tile(i, [in_g[0]], waits_hi=[in_g[1]])
            if nxt_nt:
                state["x1t_ready"] = [(sV, sV.n), (sC, sC.n)]

        fin = tuple((s, s.n) for s in (sHs + sGs + sSs) if s.n)

        @block.sync
        def _(sy):
            for s, vv in fin:
                sy.wait_ge(s.h, vv)

        @block.gpsimd
        def _(g):
            for s, vv in fin:
                g.wait_ge(s.h, vv)

    return nc


def kernel(L0_attn_data, L0_attn_mask, L0_attn_weight, L0_attn_bias,
           L0_ffn_Wb, L0_ffn_bias, L0_ffn_Wr, L0_ffn_A, L0_ffn_Bm):
    t_total = L0_attn_data.shape[0] * L0_attn_data.shape[1]
    t_core = t_total // NCORES
    x = np.ascontiguousarray(
        np.asarray(L0_attn_data, dtype=np.float32).reshape(t_total, D))
    common = {
        "mask": np.ascontiguousarray(np.asarray(L0_attn_mask,
                                                dtype=np.int32)),
        "w0": np.ascontiguousarray(np.asarray(L0_attn_weight,
                                              dtype=np.float32)),
        "ba": np.ascontiguousarray(
            np.asarray(L0_attn_bias, dtype=np.float32).reshape(1, D)),
        "wb": np.ascontiguousarray(np.asarray(L0_ffn_Wb, dtype=np.float32)),
        "bf": np.ascontiguousarray(
            np.asarray(L0_ffn_bias, dtype=np.float32).reshape(1, D)),
        "wr": np.ascontiguousarray(np.asarray(L0_ffn_Wr, dtype=np.float32)),
        "am": np.ascontiguousarray(
            np.asarray(L0_ffn_A, dtype=np.float32).reshape(ER, D)),
        "bm": np.ascontiguousarray(np.asarray(L0_ffn_Bm, dtype=np.float32)),
    }
    in_maps = []
    for c in range(NCORES):
        m = dict(common)
        m["xin"] = np.ascontiguousarray(x[c * t_core:(c + 1) * t_core])
        in_maps.append(m)
    nc = build_program(t_core, [768, 768, 512])
    res = run_bass_kernel_spmd(nc, in_maps, core_ids=list(range(NCORES)))
    shards = [np.asarray(r["out"]) for r in res.results]
    full = np.concatenate(shards, axis=0)
    return full.reshape(L0_attn_data.shape[0], L0_attn_data.shape[1], D)


# version 17
# speedup vs baseline: 1.1227x; 1.0291x over previous
import contextlib

import numpy as np
import ml_dtypes

import concourse.bass as bass
import concourse.mybir as mybir
from concourse.bass_utils import run_bass_kernel_spmd

F32 = mybir.dt.float32
BF16 = mybir.dt.bfloat16
I32 = mybir.dt.int32
AX = mybir.AxisListType.X
ALU = mybir.AluOpType
AF = mybir.ActivationFunctionType
BF = ml_dtypes.bfloat16

D = 4096
KT = D // 128
NCW = 256
NNC = D // NCW
E = 8
R = 16
ER = E * R
EPS = 1e-5
NCORES = 8


class Sem:
    def __init__(self, stack, nc, name):
        self.h = stack.enter_context(nc.semaphore(name))
        self.n = 0

    def inc(self, inst, k=1):
        inst.then_inc(self.h, k)
        self.n += k
        return self.n


def build_program(t_core, chunk_sizes):
    assert sum(chunk_sizes) == t_core
    maxchunk = max(chunk_sizes)

    nc = bass.Bass("TRN2", target_bir_lowering=False)

    xin = nc.dram_tensor("xin", [t_core, D], F32, kind="ExternalInput").ap()
    mask = nc.dram_tensor("mask", [D, D], I32, kind="ExternalInput").ap()
    w0 = nc.dram_tensor("w0", [D, D], F32, kind="ExternalInput").ap()
    ba = nc.dram_tensor("ba", [1, D], F32, kind="ExternalInput").ap()
    wb = nc.dram_tensor("wb", [D, D], F32, kind="ExternalInput").ap()
    bf_ = nc.dram_tensor("bf", [1, D], F32, kind="ExternalInput").ap()
    wr = nc.dram_tensor("wr", [E, D], F32, kind="ExternalInput").ap()
    am = nc.dram_tensor("am", [ER, D], F32, kind="ExternalInput").ap()
    bmm = nc.dram_tensor("bm", [E, D, R], F32, kind="ExternalInput").ap()
    out = nc.dram_tensor("out", [t_core, D], F32, kind="ExternalOutput").ap()
    wts = nc.dram_tensor("wts", [2, NNC, KT + 1, 128, NCW], BF16,
                         kind="Internal").ap()
    xscr = nc.dram_tensor("xscr", [t_core, D], BF16, kind="Internal").ap()

    identc = nc.inline_tensor(np.eye(128, dtype=BF), name="identc").ap()
    e4 = np.zeros((4, 128), dtype=BF)
    for b in range(4):
        e4[b, b * 32:(b + 1) * 32] = 1
    e4c = nc.inline_tensor(e4, name="e4c").ap()
    e4t = np.zeros((128, 4), dtype=BF)
    for b in range(4):
        e4t[b * 32:(b + 1) * 32, b] = 1
    e4tc = nc.inline_tensor(e4t, name="e4tc").ap()
    e16 = np.zeros((E, 128), dtype=BF)
    for e in range(E):
        e16[e, e * R:(e + 1) * R] = 1
    e16c = nc.inline_tensor(e16, name="e16c").ap()
    ones1c = nc.inline_tensor(np.ones((1, 128), dtype=BF), name="ones1c").ap()

    with contextlib.ExitStack() as st:
        sem = lambda name: Sem(st, nc, name)
        NG = 6
        sGs = [sem(f"sG{i}") for i in range(NG)]
        NH = 6
        sHs = [sem(f"sH{i}") for i in range(NH)]
        NS = 4
        sSs = [sem(f"sS{i}") for i in range(NS)]
        sGx = sem("sGx")
        sT = sem("sT")
        sV = sem("sV")
        sC = sem("sC")
        sP = sem("sP")
        gidx = [0]
        hidx = [0]

        def _qdma(pool, idx, q, dst, src):
            s = pool[idx[0] % len(pool)]
            idx[0] += 1
            if s.n:
                q.wait_ge(s.h, s.n)
            q.dma_start(dst, src).then_inc(s.h, 16)
            s.n += 16
            return (s, s.n)

        gidx2 = [0]

        def gdma(g, dst, src):
            return _qdma(sGs, gidx, g, dst, src)

        def hdma(q, dst, src):
            return _qdma(sHs, hidx, q, dst, src)

        def sdma(q, dst, src):
            return _qdma(sSs, gidx2, q, dst, src)

        sbuf = lambda name, shape, dt: st.enter_context(
            nc.sbuf_tensor(name, shape, dt))
        psum = lambda name, shape, dt: st.enter_context(
            nc.psum_tensor(name, shape, dt))

        ident = sbuf("ident", [128, 128], BF16)
        ones1 = sbuf("ones1", [1, 128], BF16)
        e4b = sbuf("e4b", [4, 128], BF16)
        e4tf = sbuf("e4tf", [128, 4], BF16)
        e16b = sbuf("e16b", [E, 128], BF16)
        nt_max = maxchunk // 128
        xnat = sbuf("xnat", [128, nt_max * D], BF16)
        x1t = sbuf("x1t", [128, KT * maxchunk], BF16)
        wtb = [sbuf("wtb0", [128, (KT + 1) * NCW], BF16),
               sbuf("wtb1", [128, (KT + 1) * NCW], BF16)]
        at = sbuf("at", [128, KT * 128], BF16)
        bmft = sbuf("bmft", [128, KT * 128], BF16)
        wrt = sbuf("wrt", [128, KT * E], BF16)
        wstg = [sbuf("wstg0", [128, D], BF16), sbuf("wstg1", [128, D], BF16)]
        mstg = [sbuf(f"mstg{i}", [128, D // 2], BF16) for i in range(4)]
        bs2t = [sbuf(f"bs2t{i}", [128, 128], BF16) for i in range(3)]
        bm4t = [sbuf("bm4t0", [4, 128], BF16), sbuf("bm4t1", [4, 128], BF16)]
        sgt0 = sbuf("sgt0", [4, 128], F32)
        sgt = [sgt0, sgt0]
        q2full = sbuf("q2full", [128, 8 * 128], BF16)
        outstg = [sbuf("outstg0", [128, D // 2], F32),
                  sbuf("outstg1", [128, D // 2], F32)]
        g_t = sbuf("g_t", [128, maxchunk], BF16)
        wselt = sbuf("wselt", [E, maxchunk], BF16)
        wsele = sbuf("wsele", [128, maxchunk], BF16)
        bnbt = sbuf("bnbt", [128, nt_max * 48], F32)
        mv = [sbuf("mv0", [128, 2], F32), sbuf("mv1", [128, 2], F32)]
        vpe = [sbuf("vpe0", [128, 1], F32), sbuf("vpe1", [128, 1], F32)]
        stdt = [sbuf("stdt0", [128, 1], F32), sbuf("stdt1", [128, 1], F32)]
        rstdt = sbuf("rstdt", [128, 8], F32)
        nmbt = sbuf("nmbt", [128, 8], F32)
        tlr = [sbuf("tlr0", [128, E], F32), sbuf("tlr1", [128, E], F32)]
        m1 = [sbuf("m10", [128, 1], F32), sbuf("m11", [128, 1], F32)]
        lm = [sbuf("lm0", [128, E], F32), sbuf("lm1", [128, E], F32)]
        m2 = [sbuf("m20", [128, 1], F32), sbuf("m21", [128, 1], F32)]
        exr = [sbuf("exr0", [128, E], F32), sbuf("exr1", [128, E], F32)]
        mskr = [sbuf("mskr0", [128, E], F32), sbuf("mskr1", [128, E], F32)]
        wvr = [sbuf("wvr0", [128, E], F32), sbuf("wvr1", [128, E], F32)]
        smr = [sbuf("smr0", [128, 1], F32), sbuf("smr1", [128, 1], F32)]
        rsr = [sbuf("rsr0", [128, 1], F32), sbuf("rsr1", [128, 1], F32)]
        wselb = [sbuf("wselb0", [128, E], BF16), sbuf("wselb1", [128, E], BF16)]

        pT = [psum("pT0", [128, 1024], BF16), psum("pT1", [128, 1024], BF16)]
        pmm = [psum("pmm0", [128, 512], F32), psum("pmm1", [128, 512], F32),
               psum("pmm2", [128, 512], F32)]
        pRE = psum("pRE", [128, 512], F32)
        pWT = psum("pWT", [128, 1024], BF16)
        pH = psum("pH", [128, 512], F32)
        pQ = pWT
        pQb = [pWT, pRE]
        pTa = [pT[0][:, :], pT[1][:, :], pH[:, :].bitcast(BF16)]

        block = st.enter_context(nc.Block())

        @block.scalar
        def _(sc):
            from concourse.hw_specs import get_activation_tables
            for idx, (nm, funcs) in enumerate(
                    get_activation_tables(nc.m.arch).items()):
                if nm == "natural_log_exp_and_others":
                    sc.add_instruction(mybir.InstLoadActFuncSet(
                        name=nc.get_next_instruction_name(),
                        act_func_set_id=idx, ins=[], outs=[]))
                    break

        state = {
            "pt_free": [None, None, None],
            "pt_i": 0,
            "wstg_free": [None, None],
            "mstg_free": [0, 0, 0, 0],
            "maskpipe_free": [0, 0],
            "wtb_free": [None, None],
            "wtb_scrw": [None, None],
            "bank_free": [0, 0, 0],
            "xnat_free": [],
            "x1t_tile": [[] for _ in range(8)],
            "pq_free": [None, None],
            "pq2_free": [None, None],
            "bm_v": [0, 0],
            "scr_done": [None, None],
            "ln_free": [None, None],
            "bm4_free": [None, None],
            "lnset_free": [None, None],
            "prep_done": [0] * 8,
            "rt_free": [None, None],
            "pre_free": 0,
            "pwt_free": None,
            "ph_free": 0,
            "g_free": 0,
            "wsel_free": 0,
            "wsele_free": 0,
        }

        xscr_done = {}
        tpool_last = [0]

        def tpool_take(t, allow3=True):
            while True:
                k = state["pt_i"] % 3
                state["pt_i"] += 1
                if k == 2 and not allow3:
                    continue
                break
            pf = state["pt_free"][k]
            if pf is not None:
                t.wait_ge(pf[0].h, pf[1])
            tpool_last[0] = k
            return k

        gres = [None]

        @block.gpsimd
        def _(g):
            gres[0] = gdma(g, wstg[0][0:E, :], wr)
        wr_s, wr_v = gres[0]

        @block.sync
        def _(sy):
            hdma(sy, ident[:, :], identc)
            hdma(sy, ones1[:, :], ones1c)
            hdma(sy, e4b[:, :], e4c)
            hdma(sy, e4tf[:, :], e4tc)
            hdma(sy, e16b[:, :], e16c)
        consts_done = tuple((s, s.n) for s in sHs if s.n)

        @block.tensor
        def _(t):
            for s, vv in consts_done:
                t.wait_ge(s.h, vv)
            t.wait_ge(wr_s.h, wr_v)
            k = tpool_take(t)
            mm = None
            for kt in range(KT):
                mm = t.transpose(pTa[k][0:128, kt * E:(kt + 1) * E],
                                 wstg[0][0:E, kt * 128:(kt + 1) * 128],
                                 ident[0:E, 0:E])
            sT.inc(mm)
        tv = sT.n

        @block.vector
        def _(v, tv=tv):
            v.wait_ge(sT.h, tv)
            sV.inc(v.tensor_copy(wrt[:, :], pTa[0][:, 0:KT * E]))
        state["pt_free"][0] = (sV, sV.n)
        wrt_done = sV.n

        gres = [None]

        @block.gpsimd
        def _(g):
            gres[0] = gdma(g, wstg[1][:, :], am)
        am_s, am_v = gres[0]

        for gp in range(4):
            @block.tensor
            def _(t, gp=gp, am_s=am_s, am_v=am_v):
                if gp == 0:
                    t.wait_ge(am_s.h, am_v)
                k = tpool_take(t)
                mm = None
                for j in range(8):
                    kt = gp * 8 + j
                    mm = t.transpose(pTa[k][:, j * 128:(j + 1) * 128],
                                     wstg[1][:, kt * 128:(kt + 1) * 128],
                                     ident[:, :])
                sT.inc(mm)
            tv = sT.n
            kk = tpool_last[0]

            @block.vector
            def _(v, gp=gp, tv=tv, kk=kk):
                v.wait_ge(sT.h, tv)
                sV.inc(v.tensor_copy(at[:, gp * 1024:(gp + 1) * 1024],
                                     pTa[kk][:, :]))
            state["pt_free"][kk] = (sV, sV.n)
        at_done = sV.n

        for e in range(E):
            @block.gpsimd
            def _(g, e=e):
                if e == 0:
                    g.wait_ge(sV.h, wrt_done)
                gdma(g,
                     wstg[0][:, :].rearrange("p (ot c) -> p ot c", c=128)
                     [:, :, e * R:(e + 1) * R],
                     bmm[e, :, :].rearrange("(ot p) r -> p ot r", p=128))
        bmfb_waits = tuple((s, s.n) for s in sGs if s.n)

        for gp in range(4):
            @block.tensor
            def _(t, gp=gp, bw=bmfb_waits):
                if gp == 0:
                    for s, vv in bw:
                        t.wait_ge(s.h, vv)
                k = tpool_take(t)
                mm = None
                for j in range(8):
                    ot = gp * 8 + j
                    mm = t.transpose(pTa[k][:, j * 128:(j + 1) * 128],
                                     wstg[0][:, ot * 128:(ot + 1) * 128],
                                     ident[:, :])
                sT.inc(mm)
            tv = sT.n
            kk = tpool_last[0]

            @block.vector
            def _(v, gp=gp, tv=tv, kk=kk):
                v.wait_ge(sT.h, tv)
                sV.inc(v.tensor_scalar(bmft[:, gp * 1024:(gp + 1) * 1024],
                                       pTa[kk][:, :], 2.0, None, ALU.mult))
            state["pt_free"][kk] = (sV, sV.n)
        smalls_done_T = sT.n

        state["wstg_free"] = [(sT, smalls_done_T), (sT, smalls_done_T)]

        @block.vector
        def _(v):
            sV.inc(v.memset(wtb[0][:, KT * NCW:(KT + 1) * NCW], 0.0))
            sV.inc(v.memset(wtb[1][:, KT * NCW:(KT + 1) * NCW], 0.0))
        wtb_ms = sV.n


        q2_ready = [None] * KT
        q2_consumed = [0] * KT
        red_ready = [0] * KT
        bs2_free = [None] * 3

        mdma_done = {}

        def maskpipe_a_dma(ntt):
            for hf in range(2):
                sl = (ntt * 2 + hf) % 4
                gres = [None]

                @block.gpsimd
                def _(g, ntt=ntt, hf=hf, sl=sl):
                    if state["mstg_free"][sl]:
                        g.wait_ge(sV.h, state["mstg_free"][sl])
                    gres[0] = gdma(
                        g, mstg[sl][:, :],
                        mask[ntt * 128:(ntt + 1) * 128,
                             hf * (D // 2):(hf + 1) * (D // 2)])
                mdma_done[(ntt, hf)] = gres[0]

        def maskpipe_a_red(ntt):
            s4 = ntt % 3
            for hf in range(2):
                sl = (ntt * 2 + hf) % 4
                gs, gv = mdma_done[(ntt, hf)]

                @block.vector
                def _(v, hf=hf, sl=sl, s4=s4, gs=gs, gv=gv,
                      bf2=(bs2_free[s4] if hf == 0 else None)):
                    v.wait_ge(gs.h, gv)
                    if bf2 is not None:
                        v.wait_ge(bf2[0].h, bf2[1])
                    with nc.allow_low_precision(reason="0/1 counts <= 32"):
                        sV.inc(v.tensor_reduce(
                            bs2t[s4][:, hf * 64:(hf + 1) * 64].unsqueeze(2),
                            mstg[sl][:, :].rearrange("p (kb j) -> p kb j",
                                                     j=32),
                            AX, ALU.add))
                state["mstg_free"][sl] = sV.n
            red_ready[ntt] = sV.n

        def maskpipe_b1(ntt):
            sl = ntt % 2
            s4 = ntt % 3
            pQf = pQb[sl][:, :] if sl else pQb[0][:, :].bitcast(F32)
            if sl == 0:
                pQf = pQb[0][:, :].bitcast(F32)
            else:
                pQf = pQb[1][:, :]

            @block.tensor
            def _(t, s4=s4, sl=sl, pQf=pQf, red_v=red_ready[ntt]):
                t.wait_ge(sV.h, red_v)
                pqf = state["pq_free"][sl]
                if pqf is not None:
                    t.wait_ge(pqf[0].h, pqf[1])
                sT.inc(t.matmul(pQf[0:4, 0:128], e4tf[:, :],
                                bs2t[s4][:, :], start=True, stop=True))
            tv = sT.n
            bs2_free[s4] = (sT, sT.n)

            @block.scalar
            def _(sc, sl=sl, tv=tv, pQf=pQf, bmf=state["bm4_free"][sl],
                  sgf=sC.n):
                sc.wait_ge(sT.h, tv)
                if bmf is not None:
                    sc.wait_ge(bmf[0].h, bmf[1])
                if sgf:
                    sc.wait_ge(sC.h, sgf)
                sC.inc(sc.activation(sgt[sl][0:4, :], pQf[0:4, 0:128],
                                     AF.Sign))
                sc.wait_ge(sC.h, sC.n)
                sC.inc(sc.activation(bm4t[sl][0:4, :], sgt[sl][0:4, :],
                                     AF.Relu))
            state["bm_v"][sl] = sC.n
            state["pq_free"][sl] = (sC, sC.n)

        def maskpipe_b2(ntt):
            sl = ntt % 2
            if sl == 0:
                pQf = pQb[0][:, :].bitcast(F32)
            else:
                pQf = pQb[1][:, :]

            @block.tensor
            def _(t, sl=sl, pQf=pQf, bm_v=state["bm_v"][sl]):
                t.wait_ge(sC.h, bm_v)
                pqf = state["pq2_free"][sl]
                if pqf is not None:
                    t.wait_ge(pqf[0].h, pqf[1])
                sT.inc(t.matmul(pQf[:, 128:256], e4b[0:4, :],
                                bm4t[sl][0:4, :], start=True, stop=True))
            tv = sT.n
            state["bm4_free"][sl] = (sT, sT.n)

            @block.scalar
            def _(sc, ntt=ntt, tv=tv, pQf=pQf,
                  qcf=(q2_consumed[ntt - 8] if ntt >= 8 else 0)):
                sc.wait_ge(sT.h, tv)
                if qcf:
                    sc.wait_ge(sV.h, qcf)
                sC.inc(sc.activation(
                    q2full[:, (ntt % 8) * 128:(ntt % 8 + 1) * 128],
                    pQf[:, 128:256], AF.Identity))
            state["pq2_free"][sl] = (sC, sC.n)
            q2_ready[ntt] = (sC, sC.n)

        def gen_subtile(L, ntt):
            sl = ntt % 2
            wsrc = w0 if L == 0 else wb
            fw = state["wstg_free"][sl]
            gres = [None]

            @block.gpsimd
            def _(g, ntt=ntt, sl=sl, wsrc=wsrc, fw=fw):
                if fw is not None:
                    g.wait_ge(fw[0].h, fw[1])
                gres[0] = gdma(g, wstg[sl][:, :],
                               wsrc[ntt * 128:(ntt + 1) * 128, :])
            gs, gv = gres[0]
            if L == 0:
                vs = []
                for pc in range(4):
                    @block.vector
                    def _(v, ntt=ntt, sl=sl, gs=gs, gv=gv, pc=pc,
                          q2v=q2_ready[ntt]):
                        v.wait_ge(gs.h, gv)
                        if pc == 0:
                            v.wait_ge(q2v[0].h, q2v[1])
                        sV.inc(v.tensor_tensor(
                            wstg[sl][:, pc * 1024:(pc + 1) * 1024]
                            .rearrange("p (kb j) -> p kb j", j=32),
                            wstg[sl][:, pc * 1024:(pc + 1) * 1024]
                            .rearrange("p (kb j) -> p kb j", j=32),
                            q2full[:, (ntt % 8) * 128 + pc * 32:
                                   (ntt % 8) * 128 + (pc + 1) * 32]
                            .unsqueeze(2).broadcast_to([128, 32, 32]),
                            ALU.mult))
                    vs.append((sV, sV.n))
                q2_consumed[ntt] = sV.n
                return vs
            return [(gs, gv)] * 4

        def gen_transpose_batch(nci, sub, gp, ready):
            sl2 = (nci * 2 + sub) % 2
            slw = nci % 2

            @block.tensor
            def _(t, gp=gp, sl2=sl2, ready=ready[gp]):
                t.wait_ge(ready[0].h, ready[1])
                k = tpool_take(t)
                mm = None
                for j in range(8):
                    kt = gp * 8 + j
                    mm = t.transpose(pTa[k][:, j * 128:(j + 1) * 128],
                                     wstg[sl2][:, kt * 128:(kt + 1) * 128],
                                     ident[:, :])
                sT.inc(mm)
            tv = sT.n
            kk = tpool_last[0]

            @block.scalar
            def _(sc, gp=gp, sub=sub, tv=tv, kk=kk, slw=slw,
                  scrw=state["wtb_scrw"][slw]):
                sc.wait_ge(sT.h, tv)
                if scrw is not None:
                    sc.wait_ge(scrw[0].h, scrw[1])
                sC.inc(sc.activation(
                    wtb[slw][:, :].rearrange("p (kt n) -> p kt n", n=NCW)
                    [:, gp * 8:(gp + 1) * 8, sub * 128:sub * 128 + 128],
                    pTa[kk][:, :].rearrange("p (j n) -> p j n", n=128),
                    AF.Identity))
            state["pt_free"][kk] = (sC, sC.n)
            if gp == 3:
                state["wstg_free"][sl2] = (sT, tv)

        def prep_nci(L, nci):
            sl = nci % 2
            bsrc = ba if L == 0 else bf_
            gres = [None]

            @block.gpsimd
            def _(g, nci=nci, sl=sl, bsrc=bsrc,
                  scrw=state["wtb_scrw"][sl],
                  wfw=state["wtb_free"][sl]):
                if scrw is not None:
                    g.wait_ge(scrw[0].h, scrw[1])
                else:
                    g.wait_ge(sV.h, wtb_ms)
                if wfw is not None:
                    g.wait_ge(wfw[0].h, wfw[1])
                gres[0] = gdma(
                    g, wtb[sl][0:1, KT * NCW:KT * NCW + NCW],
                    bsrc[0:1, nci * NCW:(nci + 1) * NCW])
            batches = []
            for sub in range(2):
                ntt = nci * 2 + sub
                ready = gen_subtile(L, ntt)
                for gp in range(4):
                    batches.append(
                        lambda nci=nci, sub=sub, gp=gp, ready=ready:
                        gen_transpose_batch(nci, sub, gp, ready))
            return batches, gres[0]

        def mm_tile(L, nci, nt, i, lora, wt_ready, extra_t_waits=(),
                    lora_gate=None):
            sl = nci % 2
            bk = (state.setdefault("bk_i", [0]))[0] % 3
            state["bk_i"][0] += 1

            @block.tensor
            def _(t, nci=nci, sl=sl, i=i, bk=bk, lora=lora,
                  wt_ready=wt_ready, bf_val=state["bank_free"][bk],
                  xr=tuple(state["x1t_tile"][i]) if nci == 0 else (),
                  ew=tuple(extra_t_waits), lora_gate=lora_gate):
                if i == 0 and wt_ready is not None:
                    wl = (wt_ready if isinstance(wt_ready, list)
                          else [wt_ready])
                    for s, vv in wl:
                        t.wait_ge(s.h, vv)
                if i == 0:
                    for s, vv in ew:
                        t.wait_ge(s.h, vv)
                for s, vv in xr:
                    t.wait_ge(s.h, vv)
                if bf_val:
                    t.wait_ge(sV.h, bf_val)
                t.matmul(pmm[bk][:, 0:NCW], ones1[0:1, :],
                         wtb[sl][0:1, KT * NCW:KT * NCW + NCW],
                         start=True, stop=False)
                mm = None
                for kt in range(KT):
                    mm = t.matmul(
                        pmm[bk][:, 0:NCW],
                        x1t[:, kt * maxchunk + i * 128:
                            kt * maxchunk + (i + 1) * 128],
                        wtb[sl][:, kt * NCW:(kt + 1) * NCW],
                        start=False, stop=(kt == KT - 1 and lora is None))
                if lora is not None:
                    if lora_gate is not None and nci == 0 and i == 0:
                        t.wait_ge(lora_gate[0].h, lora_gate[1])
                    mm = t.matmul(pmm[bk][:, 0:NCW],
                                  lora[:, i * 128:(i + 1) * 128],
                                  bmft[:, nci * NCW:(nci + 1) * NCW],
                                  start=False, stop=True)
                sT.inc(mm)
            tv = sT.n

            evv = [0]

            @block.vector
            def _(v, nci=nci, i=i, bk=bk, tv=tv):
                v.wait_ge(sT.h, tv)
                sl_ap = xnat[:, i * D + nci * NCW:i * D + (nci + 1) * NCW]
                sV.inc(v.scalar_tensor_tensor(
                    sl_ap, pmm[bk][:, 0:NCW], 1.0, sl_ap,
                    ALU.mult, ALU.add))
                evv[0] = sV.n
                if nci % 2 == 1:
                    v.wait_ge(sV.h, sV.n)
                    sV.inc(v.bn_stats(
                        bnbt[:, i * 48 + (nci // 2) * 6:
                             i * 48 + (nci // 2 + 1) * 6],
                        xnat[:, i * D + (nci - 1) * NCW:
                             i * D + (nci + 1) * NCW]))
            state["bank_free"][bk] = evv[0]
            if nci == NNC - 1:
                ln_prep(i, i % 2)
            if i == nt - 1:
                state["wtb_free"][sl] = (sT, tv)

        def nci_matmuls(L, nci, nt, lora, wt_ready, extra_t_waits=(),
                        lora_gate=None):
            for i in range(nt):
                mm_tile(L, nci, nt, i, lora, wt_ready, extra_t_waits,
                        lora_gate)

        def layer_pass(L, c, nt, lora, lora_gate=None):
            if c == 0 and L == 0:
                maskpipe_a_dma(0)
                maskpipe_a_red(0)
                maskpipe_a_dma(1)
                maskpipe_a_red(1)
                maskpipe_b1(0)
                maskpipe_a_dma(2)
                maskpipe_a_red(2)
                maskpipe_b1(1)
                maskpipe_b2(0)
                maskpipe_a_dma(3)
                maskpipe_a_red(3)
                maskpipe_b1(2)
                maskpipe_b2(1)
                maskpipe_a_dma(4)
                maskpipe_a_red(4)
                maskpipe_b1(3)
                maskpipe_b2(2)
                maskpipe_a_dma(5)
                maskpipe_a_red(5)
                maskpipe_b1(4)
                maskpipe_b2(3)
                maskpipe_a_dma(6)
                maskpipe_a_red(6)
                maskpipe_b1(5)
                maskpipe_b2(4)
                maskpipe_a_dma(7)
                maskpipe_a_red(7)
                maskpipe_b1(6)
                maskpipe_b2(5)
                maskpipe_a_dma(8)
                maskpipe_a_red(8)
                maskpipe_b1(7)
                maskpipe_a_dma(9)
                maskpipe_a_red(9)
                maskpipe_a_dma(10)
                maskpipe_a_dma(11)
            if c == 0:
                batches, bias_rdy = prep_nci(L, 0)
                for b in batches:
                    b()
                ev = sC.n
            for nci in range(NNC):
                sl = nci % 2
                if c == 0:
                    if L == 0:
                        if nci * 2 + 10 < KT:
                            maskpipe_a_red(nci * 2 + 10)
                        for ntt in (nci * 2 + 6, nci * 2 + 7):
                            if ntt < KT:
                                maskpipe_b2(ntt)
                        for ntt in (nci * 2 + 8, nci * 2 + 9):
                            if ntt < KT:
                                maskpipe_b1(ntt)
                        if nci * 2 + 11 < KT:
                            maskpipe_a_red(nci * 2 + 11)
                    if nci + 1 < NNC:
                        nbatches, nbias = prep_nci(L, nci + 1)
                    else:
                        nbatches, nbias = [], None
                    done = 0
                    for i in range(nt):
                        mm_tile(L, nci, nt, i, lora, bias_rdy,
                                extra_t_waits=[(sC, ev)],
                                lora_gate=lora_gate)
                        want = len(nbatches) * (i + 1) // nt
                        while done < want:
                            nbatches[done]()
                            done += 1
                    while done < len(nbatches):
                        nbatches[done]()
                        done += 1
                    nev = sC.n
                    hres = [None]

                    @block.sync
                    def _(sy, L=L, nci=nci, sl=sl, ev=ev, br=bias_rdy):
                        sy.wait_ge(sC.h, ev)
                        sy.wait_ge(br[0].h, br[1])
                        hres[0] = hdma(
                            sy, wts[L, nci, :, :, :].transpose([1, 0, 2]),
                            wtb[sl][:, :].rearrange("p (kt n) -> p kt n",
                                                    n=NCW))
                    state["wtb_scrw"][sl] = hres[0]
                    state["scr_done"][L] = hres[0]
                    ev = nev
                    bias_rdy = nbias
                    if L == 0:
                        for ntt in (nci * 2 + 12, nci * 2 + 13):
                            if ntt < KT:
                                maskpipe_a_dma(ntt)
                        nt0_tiles = chunk_sizes[0] // 128
                        if nci < t_core // 128 - nt0_tiles:
                            gt = nt0_tiles + nci

                            @block.gpsimd
                            def _(g, gt=gt):
                                if sGx.n:
                                    g.wait_ge(sGx.h, sGx.n)
                                g.dma_start(
                                    xscr[gt * 128:(gt + 1) * 128, :],
                                    xin[gt * 128:(gt + 1) * 128, :]
                                ).then_inc(sGx.h, 16)
                                sGx.n += 16
                                xscr_done[gt] = (sGx, sGx.n)
                else:
                    fw = state["wtb_free"][sl]
                    sd = state["scr_done"][L]
                    scrw = state["wtb_scrw"][sl]
                    hres = [None, None, None]

                    @block.sync
                    def _(sy, L=L, nci=nci, sl=sl, fw=fw, sd=sd, scrw=scrw):
                        if nci < 2 and sd is not None:
                            sy.wait_ge(sd[0].h, sd[1])
                        if scrw is not None:
                            sy.wait_ge(scrw[0].h, scrw[1])
                        if fw is not None:
                            sy.wait_ge(fw[0].h, fw[1])
                        wtb3 = wtb[sl][:, :].rearrange("p (kt n) -> p kt n",
                                                       n=NCW)
                        for qi, (k0, k1) in enumerate(
                                ((0, 11), (11, 22), (22, KT + 1))):
                            hres[qi] = hdma(
                                sy, wtb3[:, k0:k1, :],
                                wts[L, nci, k0:k1, :, :].transpose([1, 0, 2]))
                    nci_matmuls(L, nci, nt, lora, list(hres),
                                lora_gate=lora_gate)

        def transpose_tile(i, waits, allow3=True, waits_hi=None):
            for gp in range(4):
                @block.tensor
                def _(t, i=i, gp=gp, waits=tuple(waits),
                      waits_hi=tuple(waits_hi if waits_hi is not None
                                     else waits), allow3=allow3):
                    if gp == 0:
                        for s, vv in waits:
                            t.wait_ge(s.h, vv)
                    if gp == 2 and waits_hi != waits:
                        for s, vv in waits_hi:
                            t.wait_ge(s.h, vv)
                    k = tpool_take(t, allow3)
                    mm = None
                    for j in range(8):
                        kt = gp * 8 + j
                        mm = t.transpose(
                            pTa[k][:, j * 128:(j + 1) * 128],
                            xnat[:, i * D + kt * 128:i * D + (kt + 1) * 128],
                            ident[:, :])
                    sT.inc(mm)
                tv = sT.n
                kk = tpool_last[0]
                dst = x1t[:, :].rearrange("p (kt t) -> p kt t", t=maxchunk)[
                    :, gp * 8:(gp + 1) * 8, i * 128:(i + 1) * 128]
                src = pTa[kk][:, :].rearrange("p (j t) -> p j t", t=128)
                if gp % 2 == 0:
                    @block.vector
                    def _(v, tv=tv, dst=dst, src=src):
                        v.wait_ge(sT.h, tv)
                        sV.inc(v.tensor_copy(dst, src))
                    state["pt_free"][kk] = (sV, sV.n)
                else:
                    @block.scalar
                    def _(sc, tv=tv, dst=dst, src=src):
                        sc.wait_ge(sT.h, tv)
                        sC.inc(sc.activation(dst, src, AF.Identity))
                    state["pt_free"][kk] = (sC, sC.n)
            return (sV.n, sC.n)

        def input_chunk(c, t0, nt):
            for i in range(nt):
                in_gv = [None, None]

                @block.gpsimd
                def _(g, i=i, t0=t0):
                    xf = (state["xnat_free"][i]
                          if i < len(state["xnat_free"]) else None)
                    if xf is not None:
                        g.wait_ge(xf[0].h, xf[1])
                    in_gv[0] = gdma(
                        g, xnat[:, i * D:i * D + 2048],
                        xin[t0 + i * 128:t0 + (i + 1) * 128, 0:2048])
                    in_gv[1] = gdma(
                        g, xnat[:, i * D + 2048:(i + 1) * D],
                        xin[t0 + i * 128:t0 + (i + 1) * 128, 2048:D])
                tv = transpose_tile(i, [in_gv[0]], waits_hi=[in_gv[1]])
                state["x1t_tile"][i] = [(sV, tv[0]), (sC, tv[1])]

        def ln_prep(i, iset):
            lf = state["lnset_free"][iset]
            st_v = sV.n

            @block.vector
            def _(v, i=i, iset=iset, lf=lf, st_v=st_v):
                v.wait_ge(sV.h, st_v)
                if lf is not None:
                    v.wait_ge(lf[0].h, lf[1])
                sV.inc(v.bn_aggr(mv[iset][:, :],
                                 bnbt[:, i * 48:(i + 1) * 48]))
                v.wait_ge(sV.h, sV.n)
                sV.inc(v.tensor_scalar(vpe[iset][:, :], mv[iset][:, 1:2],
                                       EPS, None, ALU.add))
            ve_v = sV.n

            @block.scalar
            def _(sc, i=i, iset=iset, ve_v=ve_v):
                sc.wait_ge(sV.h, ve_v)
                sC.inc(sc.activation(stdt[iset][:, :], vpe[iset][:, :],
                                     AF.Ln))
                sc.wait_ge(sC.h, sC.n)
                sC.inc(sc.activation(rstdt[:, i:i + 1], stdt[iset][:, :],
                                     AF.Exp, scale=-0.5))
            sq_v = sC.n

            @block.vector
            def _(v, i=i, iset=iset, sq_v=sq_v):
                v.wait_ge(sC.h, sq_v)
                sV.inc(v.tensor_scalar(nmbt[:, i:i + 1], mv[iset][:, 0:1],
                                       rstdt[:, i:i + 1], -1.0,
                                       ALU.mult, ALU.mult))
            state["lnset_free"][iset] = (sV, sV.n)
            state["prep_done"][i] = sV.n

        def ln_apply(i, write_out=None, t0=0):
            pd = state["prep_done"][i]
            if write_out is not None:
                ofree = state.setdefault("outstg_free", [None, None])
                of0, of1 = ofree[0], ofree[1]

                @block.scalar
                def _(sc, i=i, pd=pd, of=of0):
                    sc.wait_ge(sV.h, pd)
                    if of is not None:
                        sc.wait_ge(of[0].h, of[1])
                    sC.inc(sc.activation(
                        outstg[0][:, :],
                        xnat[:, i * D:i * D + 2048],
                        AF.Identity, bias=nmbt[:, i:i + 1],
                        scale=rstdt[:, i:i + 1]))
                    sc.wait_ge(sC.h, sC.n)
                    sdma(sc,
                         write_out[t0 + i * 128:t0 + (i + 1) * 128, 0:2048],
                         outstg[0][:, :])
                ofree[0] = (sSs[(gidx2[0] - 1) % NS],
                            sSs[(gidx2[0] - 1) % NS].n)

                @block.vector
                def _(v, i=i, pd=pd, of=of1):
                    v.wait_ge(sV.h, pd)
                    if of is not None:
                        v.wait_ge(of[0].h, of[1])
                    sV.inc(v.tensor_scalar(
                        outstg[1][:, :],
                        xnat[:, i * D + 2048:i * D + 4096],
                        rstdt[:, i:i + 1], nmbt[:, i:i + 1],
                        ALU.mult, ALU.add))
                vv = sV.n
                hres = [None]

                @block.sync
                def _(sy, i=i, vv=vv):
                    sy.wait_ge(sV.h, vv)
                    hres[0] = hdma(
                        sy,
                        write_out[t0 + i * 128:t0 + (i + 1) * 128, 2048:4096],
                        outstg[1][:, :])
                ofree[1] = hres[0]
                return [(sC, sC.n), (sV, sV.n)]
            else:
                @block.scalar
                def _(sc, i=i, pd=pd):
                    sc.wait_ge(sV.h, pd)
                    sC.inc(sc.activation(
                        xnat[:, i * D:i * D + 2048],
                        xnat[:, i * D:i * D + 2048],
                        AF.Identity, bias=nmbt[:, i:i + 1],
                        scale=rstdt[:, i:i + 1]))

                @block.vector
                def _(v, i=i, pd=pd):
                    v.wait_ge(sV.h, pd)
                    sV.inc(v.tensor_scalar(
                        xnat[:, i * D + 2048:i * D + 4096],
                        xnat[:, i * D + 2048:i * D + 4096],
                        rstdt[:, i:i + 1], nmbt[:, i:i + 1],
                        ALU.mult, ALU.add))
                return [(sC, sC.n), (sV, sV.n)]

        def router_tile(i, iset, x2t_ok):
            rf = state["rt_free"][iset]

            @block.tensor
            def _(t, i=i, x2t_ok=tuple(x2t_ok), pref=state["pre_free"],
                  pq1=state["pq_free"][1], pq2=state["pq2_free"][1]):
                for s, vv in x2t_ok:
                    t.wait_ge(s.h, vv)
                if pref:
                    t.wait_ge(sV.h, pref)
                for q in (pq1, pq2):
                    if q is not None:
                        t.wait_ge(q[0].h, q[1])
                mm = None
                for kt in range(KT):
                    mm = t.matmul(pRE[:, 0:E],
                                  x1t[:, kt * maxchunk + i * 128:
                                      kt * maxchunk + (i + 1) * 128],
                                  wrt[:, kt * E:(kt + 1) * E],
                                  start=(kt == 0), stop=(kt == KT - 1))
                sT.inc(mm)
            tv = sT.n

            @block.vector
            def _(v, iset=iset, tv=tv, rf=rf):
                v.wait_ge(sT.h, tv)
                if rf is not None:
                    v.wait_ge(rf[0].h, rf[1])
                sV.inc(v.tensor_copy(tlr[iset][:, :], pRE[:, 0:E]))
            state["pre_free"] = sV.n
            cp_v = sV.n

            @block.scalar
            def _(sc, iset=iset, cp_v=cp_v):
                sc.wait_ge(sV.h, cp_v)
                sC.inc(sc.activation(exr[iset][:, :], tlr[iset][:, :],
                                     AF.Exp))
            ex_v = sC.n

            @block.vector
            def _(v, iset=iset, ex_v=ex_v, cp_v=cp_v):
                def stepv(inst):
                    sV.inc(inst)
                    v.wait_ge(sV.h, sV.n)
                v.wait_ge(sV.h, cp_v)
                stepv(v.reduce_max(m1[iset][:, :], tlr[iset][:, :], axis=AX))
                stepv(v.tensor_scalar(mskr[iset][:, :], tlr[iset][:, :],
                                      m1[iset][:, 0:1], None, ALU.is_ge))
                stepv(v.scalar_tensor_tensor(lm[iset][:, :], mskr[iset][:, :],
                                             -1e30, tlr[iset][:, :],
                                             ALU.mult, ALU.add))
                stepv(v.reduce_max(m2[iset][:, :], lm[iset][:, :], axis=AX))
                stepv(v.tensor_scalar(mskr[iset][:, :], tlr[iset][:, :],
                                      m2[iset][:, 0:1], None, ALU.is_ge))
                v.wait_ge(sC.h, ex_v)
                stepv(v.tensor_tensor(wvr[iset][:, :], exr[iset][:, :],
                                      mskr[iset][:, :], ALU.mult))
                stepv(v.reduce_sum(smr[iset][:, :], wvr[iset][:, :], axis=AX))
                stepv(v.reciprocal(rsr[iset][:, :], smr[iset][:, :]))
                stepv(v.tensor_scalar(wselb[iset][:, :], wvr[iset][:, :],
                                      rsr[iset][:, 0:1], None, ALU.mult))
            wb_v = sV.n
            state["rt_free"][iset] = (sV, sV.n)
            return wb_v

        def router_finish(i, iset, wb_v):
            @block.tensor
            def _(t, iset=iset, wb_v=wb_v, pwf=state["pwt_free"],
                  pqf=state["pq2_free"][0]):
                t.wait_ge(sV.h, wb_v)
                if pwf is not None:
                    t.wait_ge(pwf[0].h, pwf[1])
                if pqf is not None:
                    t.wait_ge(pqf[0].h, pqf[1])
                sT.inc(t.transpose(pWT[0:E, 0:128], wselb[iset][:, :],
                                   ident[:, :]))
            tw_v = sT.n

            @block.scalar
            def _(sc, i=i, tw_v=tw_v):
                sc.wait_ge(sT.h, tw_v)
                sC.inc(sc.activation(wselt[0:E, i * 128:(i + 1) * 128],
                                     pWT[0:E, 0:128], AF.Identity))
            state["pwt_free"] = (sC, sC.n)

        t0s = []
        acc = 0
        for cs in chunk_sizes:
            t0s.append(acc)
            acc += cs

        input_chunk(0, t0s[0], chunk_sizes[0] // 128)
        for c, chunk in enumerate(chunk_sizes):
            nt = chunk // 128
            t0 = t0s[c]

            layer_pass(0, c, nt, None)
            attn_done_v = sV.n

            ph_evac = state["ph_free"]
            pending = None

            def finish_one(pending):
                i_, iset_, wbv_ = pending
                router_finish(i_, iset_, wbv_)
                wt_v = sC.n

                @block.tensor
                def _(t, i=i_, wt_v=wt_v, ph_evac=ph_evac,
                      p2f=state["pt_free"][2]):
                    t.wait_ge(sC.h, wt_v)
                    if ph_evac:
                        t.wait_ge(sV.h, ph_evac)
                    if p2f is not None:
                        t.wait_ge(p2f[0].h, p2f[1])
                    sT.inc(t.matmul(
                        pH[:, (i % 4) * 128:(i % 4 + 1) * 128],
                        e16b[0:E, :],
                        wselt[0:E, i * 128:(i + 1) * 128],
                        start=True, stop=True))
                return sT.n

            evac_group = []
            for i in range(nt):
                apw = ln_apply(i)
                x2t_i = transpose_tile(i, [apw[0]], allow3=False,
                                       waits_hi=[apw[1]])
                state["x1t_tile"][i] = [(sV, x2t_i[0]), (sC, x2t_i[1])]
                wb_v = router_tile(i, i % 2, [(sV, x2t_i[0]),
                                              (sC, x2t_i[1])])
                if pending is not None:
                    tv = finish_one(pending)
                    evac_group.append((pending[0], tv))
                pending = (i, i % 2, wb_v)
                if evac_group and (evac_group[-1][0] % 4 == 3):
                    g0 = evac_group[0][0]
                    gi = evac_group[-1][0]
                    tv = evac_group[-1][1]

                    @block.vector
                    def _(v, g0=g0, gi=gi, tv=tv):
                        v.wait_ge(sT.h, tv)
                        sV.inc(v.tensor_copy(
                            wsele[:, g0 * 128:(gi + 1) * 128],
                            pH[:, (g0 % 4) * 128:(gi % 4 + 1) * 128]))
                    ph_evac = sV.n
                    evac_group = []
            if pending is not None:
                tv = finish_one(pending)
                evac_group.append((pending[0], tv))
            if evac_group:
                g0 = evac_group[0][0]
                gi = evac_group[-1][0]
                tv = evac_group[-1][1]

                @block.vector
                def _(v, g0=g0, gi=gi, tv=tv):
                    v.wait_ge(sT.h, tv)
                    sV.inc(v.tensor_copy(
                        wsele[:, g0 * 128:(gi + 1) * 128],
                        pH[:, (g0 % 4) * 128:(gi % 4 + 1) * 128]))
                ph_evac = sV.n
            wsele_done = sV.n

            hws = []
            off = 0
            while off < chunk:
                w_ = min(512, chunk - off)
                hws.append((off, w_))
                off += w_

            for off, w_ in hws:
                @block.tensor
                def _(t, off=off, w_=w_, phf=state["ph_free"],
                      wd=wsele_done):
                    t.wait_ge(sV.h, wd)
                    if phf:
                        t.wait_ge(sV.h, phf)
                    mm = None
                    for kt in range(KT):
                        mm = t.matmul(pH[:, 0:w_],
                                      at[:, kt * 128:(kt + 1) * 128],
                                      x1t[:, kt * maxchunk + off:
                                          kt * maxchunk + off + w_],
                                      start=(kt == 0), stop=(kt == KT - 1))
                    sT.inc(mm)
                tv = sT.n

                @block.vector
                def _(v, off=off, w_=w_, tv=tv, wd=wsele_done):
                    v.wait_ge(sT.h, tv)
                    v.wait_ge(sV.h, wd)
                    sV.inc(v.tensor_tensor(g_t[:, off:off + w_],
                                           pH[:, 0:w_],
                                           wsele[:, off:off + w_], ALU.mult))
                state["ph_free"] = sV.n
            g_done = sV.n
            state["pt_free"][2] = (sV, g_done)

            layer_pass(1, c, nt, g_t, lora_gate=(sV, g_done))
            state["g_free"] = sT.n
            ffn_done_v = sV.n

            nxt_nt = (chunk_sizes[c + 1] // 128
                      if c + 1 < len(chunk_sizes) else 0)
            assert nxt_nt <= nt
            in_gs = [None] * nt

            def emit_apply(i):
                apw = ln_apply(i, write_out=out, t0=t0)
                if i < nxt_nt:
                    in_g = [None, None]
                    gt = t0s[c + 1] // 128 + i
                    xd = xscr_done[gt]
                    r0 = gt * 128
                    r1 = (gt + 1) * 128

                    @block.gpsimd
                    def _(g, i=i, apw=tuple(apw), xd=xd, r0=r0, r1=r1):
                        g.wait_ge(xd[0].h, xd[1])
                        g.wait_ge(apw[0][0].h, apw[0][1])
                        in_g[0] = gdma(g, xnat[:, i * D:i * D + 2048],
                                       xscr[r0:r1, 0:2048])
                        g.wait_ge(apw[1][0].h, apw[1][1])
                        in_g[1] = gdma(g, xnat[:, i * D + 2048:(i + 1) * D],
                                       xscr[r0:r1, 2048:D])
                    in_gs[i] = in_g

            nlead = min(3, nt)
            for i in range(nlead):
                emit_apply(i)
            nxtj = nlead
            for i in range(nxt_nt):
                tv = transpose_tile(i, [in_gs[i][0]], waits_hi=[in_gs[i][1]])
                state["x1t_tile"][i] = [(sV, tv[0]), (sC, tv[1])]
                if nxtj < nt:
                    emit_apply(nxtj)
                    nxtj += 1
            while nxtj < nt:
                emit_apply(nxtj)
                nxtj += 1

        fin = tuple((s, s.n) for s in (sHs + sGs + sSs + [sGx]) if s.n)

        @block.sync
        def _(sy):
            for s, vv in fin:
                sy.wait_ge(s.h, vv)

        @block.gpsimd
        def _(g):
            for s, vv in fin:
                g.wait_ge(s.h, vv)

    return nc


def kernel(L0_attn_data, L0_attn_mask, L0_attn_weight, L0_attn_bias,
           L0_ffn_Wb, L0_ffn_bias, L0_ffn_Wr, L0_ffn_A, L0_ffn_Bm):
    t_total = L0_attn_data.shape[0] * L0_attn_data.shape[1]
    t_core = t_total // NCORES
    x = np.ascontiguousarray(
        np.asarray(L0_attn_data, dtype=np.float32).reshape(t_total, D))
    common = {
        "mask": np.ascontiguousarray(np.asarray(L0_attn_mask,
                                                dtype=np.int32)),
        "w0": np.ascontiguousarray(np.asarray(L0_attn_weight,
                                              dtype=np.float32)),
        "ba": np.ascontiguousarray(
            np.asarray(L0_attn_bias, dtype=np.float32).reshape(1, D)),
        "wb": np.ascontiguousarray(np.asarray(L0_ffn_Wb, dtype=np.float32)),
        "bf": np.ascontiguousarray(
            np.asarray(L0_ffn_bias, dtype=np.float32).reshape(1, D)),
        "wr": np.ascontiguousarray(np.asarray(L0_ffn_Wr, dtype=np.float32)),
        "am": np.ascontiguousarray(
            np.asarray(L0_ffn_A, dtype=np.float32).reshape(ER, D)),
        "bm": np.ascontiguousarray(np.asarray(L0_ffn_Bm, dtype=np.float32)),
    }
    in_maps = []
    for c in range(NCORES):
        m = dict(common)
        m["xin"] = np.ascontiguousarray(x[c * t_core:(c + 1) * t_core])
        in_maps.append(m)
    nc = build_program(t_core, [768, 768, 512])
    res = run_bass_kernel_spmd(nc, in_maps, core_ids=list(range(NCORES)))
    shards = [np.asarray(r["out"]) for r in res.results]
    full = np.concatenate(shards, axis=0)
    return full.reshape(L0_attn_data.shape[0], L0_attn_data.shape[1], D)


# version 18
# speedup vs baseline: 1.1290x; 1.0056x over previous
import contextlib

import numpy as np
import ml_dtypes

import concourse.bass as bass
import concourse.mybir as mybir
from concourse.bass_utils import run_bass_kernel_spmd

F32 = mybir.dt.float32
BF16 = mybir.dt.bfloat16
I32 = mybir.dt.int32
AX = mybir.AxisListType.X
ALU = mybir.AluOpType
AF = mybir.ActivationFunctionType
BF = ml_dtypes.bfloat16

D = 4096
KT = D // 128
NCW = 256
NNC = D // NCW
E = 8
R = 16
ER = E * R
EPS = 1e-5
NCORES = 8


class Sem:
    def __init__(self, stack, nc, name):
        self.h = stack.enter_context(nc.semaphore(name))
        self.n = 0

    def inc(self, inst, k=1):
        inst.then_inc(self.h, k)
        self.n += k
        return self.n


def build_program(t_core, chunk_sizes):
    assert sum(chunk_sizes) == t_core
    maxchunk = max(chunk_sizes)

    nc = bass.Bass("TRN2", target_bir_lowering=False)

    xin = nc.dram_tensor("xin", [t_core, D], F32, kind="ExternalInput").ap()
    mask = nc.dram_tensor("mask", [D, D], I32, kind="ExternalInput").ap()
    w0 = nc.dram_tensor("w0", [D, D], F32, kind="ExternalInput").ap()
    ba = nc.dram_tensor("ba", [1, D], F32, kind="ExternalInput").ap()
    wb = nc.dram_tensor("wb", [D, D], F32, kind="ExternalInput").ap()
    bf_ = nc.dram_tensor("bf", [1, D], F32, kind="ExternalInput").ap()
    wr = nc.dram_tensor("wr", [E, D], F32, kind="ExternalInput").ap()
    am = nc.dram_tensor("am", [ER, D], F32, kind="ExternalInput").ap()
    bmm = nc.dram_tensor("bm", [E, D, R], F32, kind="ExternalInput").ap()
    out = nc.dram_tensor("out", [t_core, D], F32, kind="ExternalOutput").ap()
    wts = nc.dram_tensor("wts", [2, NNC, KT + 1, 128, NCW], BF16,
                         kind="Internal").ap()
    xscr = nc.dram_tensor("xscr", [t_core, D], BF16, kind="Internal").ap()

    identc = nc.inline_tensor(np.eye(128, dtype=BF), name="identc").ap()
    e4 = np.zeros((4, 128), dtype=BF)
    for b in range(4):
        e4[b, b * 32:(b + 1) * 32] = 1
    e4c = nc.inline_tensor(e4, name="e4c").ap()
    e4t = np.zeros((128, 4), dtype=BF)
    for b in range(4):
        e4t[b * 32:(b + 1) * 32, b] = 1
    e4tc = nc.inline_tensor(e4t, name="e4tc").ap()
    e16 = np.zeros((E, 128), dtype=BF)
    for e in range(E):
        e16[e, e * R:(e + 1) * R] = 1
    e16c = nc.inline_tensor(e16, name="e16c").ap()
    ones1c = nc.inline_tensor(np.ones((1, 128), dtype=BF), name="ones1c").ap()

    with contextlib.ExitStack() as st:
        sem = lambda name: Sem(st, nc, name)
        NG = 6
        sGs = [sem(f"sG{i}") for i in range(NG)]
        NH = 6
        sHs = [sem(f"sH{i}") for i in range(NH)]
        NS = 4
        sSs = [sem(f"sS{i}") for i in range(NS)]
        sGx = sem("sGx")
        sT = sem("sT")
        sV = sem("sV")
        sC = sem("sC")
        sP = sem("sP")
        gidx = [0]
        hidx = [0]

        def _qdma(pool, idx, q, dst, src):
            s = pool[idx[0] % len(pool)]
            idx[0] += 1
            if s.n:
                q.wait_ge(s.h, s.n)
            q.dma_start(dst, src).then_inc(s.h, 16)
            s.n += 16
            return (s, s.n)

        gidx2 = [0]

        def gdma(g, dst, src):
            return _qdma(sGs, gidx, g, dst, src)

        def hdma(q, dst, src):
            return _qdma(sHs, hidx, q, dst, src)

        def sdma(q, dst, src):
            return _qdma(sSs, gidx2, q, dst, src)

        sbuf = lambda name, shape, dt: st.enter_context(
            nc.sbuf_tensor(name, shape, dt))
        psum = lambda name, shape, dt: st.enter_context(
            nc.psum_tensor(name, shape, dt))

        ident = sbuf("ident", [128, 128], BF16)
        ones1 = sbuf("ones1", [1, 128], BF16)
        e4b = sbuf("e4b", [4, 128], BF16)
        e4tf = sbuf("e4tf", [128, 4], BF16)
        e16b = sbuf("e16b", [E, 128], BF16)
        nt_max = maxchunk // 128
        xnat = sbuf("xnat", [128, nt_max * D], BF16)
        x1t = sbuf("x1t", [128, KT * maxchunk], BF16)
        wtb = [sbuf("wtb0", [128, (KT + 1) * NCW], BF16),
               sbuf("wtb1", [128, (KT + 1) * NCW], BF16)]
        at = sbuf("at", [128, KT * 128], BF16)
        bmft = sbuf("bmft", [128, KT * 128], BF16)
        wrt = sbuf("wrt", [128, KT * E], BF16)
        wstg = [sbuf("wstg0", [128, D], BF16), sbuf("wstg1", [128, D], BF16)]
        mstg = [sbuf(f"mstg{i}", [128, D // 2], BF16) for i in range(4)]
        bs2t = [sbuf(f"bs2t{i}", [128, 128], BF16) for i in range(3)]
        bm4t = [sbuf("bm4t0", [4, 128], BF16), sbuf("bm4t1", [4, 128], BF16)]
        sgt0 = sbuf("sgt0", [4, 128], F32)
        sgt = [sgt0, sgt0]
        q2full = sbuf("q2full", [128, 8 * 128], BF16)
        outstg = [sbuf("outstg0", [128, D // 2], F32),
                  sbuf("outstg1", [128, D // 2], F32)]
        g_t = sbuf("g_t", [128, maxchunk], BF16)
        wselt = sbuf("wselt", [E, maxchunk], BF16)
        wsele = sbuf("wsele", [128, maxchunk], BF16)
        bnbt = sbuf("bnbt", [128, nt_max * 48], F32)
        mv = [sbuf("mv0", [128, 2], F32), sbuf("mv1", [128, 2], F32)]
        vpe = [sbuf("vpe0", [128, 1], F32), sbuf("vpe1", [128, 1], F32)]
        stdt = [sbuf("stdt0", [128, 1], F32), sbuf("stdt1", [128, 1], F32)]
        rstdt = sbuf("rstdt", [128, 8], F32)
        nmbt = sbuf("nmbt", [128, 8], F32)
        tlr = [sbuf("tlr0", [128, E], F32), sbuf("tlr1", [128, E], F32)]
        m1 = [sbuf("m10", [128, 1], F32), sbuf("m11", [128, 1], F32)]
        lm = [sbuf("lm0", [128, E], F32), sbuf("lm1", [128, E], F32)]
        m2 = [sbuf("m20", [128, 1], F32), sbuf("m21", [128, 1], F32)]
        exr = [sbuf("exr0", [128, E], F32), sbuf("exr1", [128, E], F32)]
        mskr = [sbuf("mskr0", [128, E], F32), sbuf("mskr1", [128, E], F32)]
        wvr = [sbuf("wvr0", [128, E], F32), sbuf("wvr1", [128, E], F32)]
        smr = [sbuf("smr0", [128, 1], F32), sbuf("smr1", [128, 1], F32)]
        rsr = [sbuf("rsr0", [128, 1], F32), sbuf("rsr1", [128, 1], F32)]
        wselb = [sbuf("wselb0", [128, E], BF16), sbuf("wselb1", [128, E], BF16)]

        pT = [psum("pT0", [128, 1024], BF16), psum("pT1", [128, 1024], BF16)]
        pmm = [psum("pmm0", [128, 512], F32), psum("pmm1", [128, 512], F32),
               psum("pmm2", [128, 512], F32)]
        pRE = psum("pRE", [128, 512], F32)
        pWT = psum("pWT", [128, 1024], BF16)
        pH = psum("pH", [128, 512], F32)
        pQ = pWT
        pQb = [pWT, pRE]
        pTa = [pT[0][:, :], pT[1][:, :], pH[:, :].bitcast(BF16)]

        block = st.enter_context(nc.Block())

        @block.scalar
        def _(sc):
            from concourse.hw_specs import get_activation_tables
            for idx, (nm, funcs) in enumerate(
                    get_activation_tables(nc.m.arch).items()):
                if nm == "natural_log_exp_and_others":
                    sc.add_instruction(mybir.InstLoadActFuncSet(
                        name=nc.get_next_instruction_name(),
                        act_func_set_id=idx, ins=[], outs=[]))
                    break

        state = {
            "pt_free": [None, None, None],
            "pt_i": 0,
            "wstg_free": [None, None],
            "mstg_free": [0, 0, 0, 0],
            "maskpipe_free": [0, 0],
            "wtb_free": [None, None],
            "wtb_scrw": [None, None],
            "bank_free": [0, 0, 0],
            "xnat_free": [],
            "x1t_tile": [[] for _ in range(8)],
            "pq_free": [None, None],
            "pq2_free": [None, None],
            "bm_v": [0, 0],
            "scr_done": [None, None],
            "ln_free": [None, None],
            "bm4_free": [None, None],
            "lnset_free": [None, None],
            "prep_done": [0] * 8,
            "rt_free": [None, None],
            "pre_free": 0,
            "pwt_free": None,
            "ph_free": 0,
            "g_free": 0,
            "wsel_free": 0,
            "wsele_free": 0,
        }

        xscr_done = {}
        tpool_last = [0]

        def tpool_take(t, allow3=True):
            while True:
                k = state["pt_i"] % 3
                state["pt_i"] += 1
                if k == 2 and not allow3:
                    continue
                break
            pf = state["pt_free"][k]
            if pf is not None:
                t.wait_ge(pf[0].h, pf[1])
            tpool_last[0] = k
            return k

        gres = [None]

        @block.gpsimd
        def _(g):
            gres[0] = gdma(g, wstg[0][0:E, :], wr)
        wr_s, wr_v = gres[0]

        @block.sync
        def _(sy):
            hdma(sy, ident[:, :], identc)
            hdma(sy, ones1[:, :], ones1c)
            hdma(sy, e4b[:, :], e4c)
            hdma(sy, e4tf[:, :], e4tc)
            hdma(sy, e16b[:, :], e16c)
        consts_done = tuple((s, s.n) for s in sHs if s.n)

        @block.tensor
        def _(t):
            for s, vv in consts_done:
                t.wait_ge(s.h, vv)
            t.wait_ge(wr_s.h, wr_v)
            k = tpool_take(t)
            mm = None
            for kt in range(KT):
                mm = t.transpose(pTa[k][0:128, kt * E:(kt + 1) * E],
                                 wstg[0][0:E, kt * 128:(kt + 1) * 128],
                                 ident[0:E, 0:E])
            sT.inc(mm)
        tv = sT.n

        @block.vector
        def _(v, tv=tv):
            v.wait_ge(sT.h, tv)
            sV.inc(v.tensor_copy(wrt[:, :], pTa[0][:, 0:KT * E]))
        state["pt_free"][0] = (sV, sV.n)
        wrt_done = sV.n

        gres = [None]

        @block.gpsimd
        def _(g):
            gres[0] = gdma(g, wstg[1][:, :], am)
        am_s, am_v = gres[0]

        for gp in range(4):
            @block.tensor
            def _(t, gp=gp, am_s=am_s, am_v=am_v):
                if gp == 0:
                    t.wait_ge(am_s.h, am_v)
                k = tpool_take(t)
                mm = None
                for j in range(8):
                    kt = gp * 8 + j
                    mm = t.transpose(pTa[k][:, j * 128:(j + 1) * 128],
                                     wstg[1][:, kt * 128:(kt + 1) * 128],
                                     ident[:, :])
                sT.inc(mm)
            tv = sT.n
            kk = tpool_last[0]

            @block.vector
            def _(v, gp=gp, tv=tv, kk=kk):
                v.wait_ge(sT.h, tv)
                sV.inc(v.tensor_copy(at[:, gp * 1024:(gp + 1) * 1024],
                                     pTa[kk][:, :]))
            state["pt_free"][kk] = (sV, sV.n)
        at_done = sV.n

        for e in range(E):
            @block.gpsimd
            def _(g, e=e):
                if e == 0:
                    g.wait_ge(sV.h, wrt_done)
                gdma(g,
                     wstg[0][:, :].rearrange("p (ot c) -> p ot c", c=128)
                     [:, :, e * R:(e + 1) * R],
                     bmm[e, :, :].rearrange("(ot p) r -> p ot r", p=128))
        bmfb_waits = tuple((s, s.n) for s in sGs if s.n)

        for gp in range(4):
            @block.tensor
            def _(t, gp=gp, bw=bmfb_waits):
                if gp == 0:
                    for s, vv in bw:
                        t.wait_ge(s.h, vv)
                k = tpool_take(t)
                mm = None
                for j in range(8):
                    ot = gp * 8 + j
                    mm = t.transpose(pTa[k][:, j * 128:(j + 1) * 128],
                                     wstg[0][:, ot * 128:(ot + 1) * 128],
                                     ident[:, :])
                sT.inc(mm)
            tv = sT.n
            kk = tpool_last[0]

            @block.vector
            def _(v, gp=gp, tv=tv, kk=kk):
                v.wait_ge(sT.h, tv)
                sV.inc(v.tensor_scalar(bmft[:, gp * 1024:(gp + 1) * 1024],
                                       pTa[kk][:, :], 2.0, None, ALU.mult))
            state["pt_free"][kk] = (sV, sV.n)
        smalls_done_T = sT.n

        state["wstg_free"] = [(sT, smalls_done_T), (sT, smalls_done_T)]

        @block.vector
        def _(v):
            sV.inc(v.memset(wtb[0][:, KT * NCW:(KT + 1) * NCW], 0.0))
            sV.inc(v.memset(wtb[1][:, KT * NCW:(KT + 1) * NCW], 0.0))
        wtb_ms = sV.n


        q2_ready = [None] * KT
        q2_consumed = [0] * KT
        red_ready = [0] * KT
        bs2_free = [None] * 3

        mdma_done = {}

        def maskpipe_a_dma(ntt):
            for hf in range(2):
                sl = (ntt * 2 + hf) % 4
                gres = [None]

                @block.gpsimd
                def _(g, ntt=ntt, hf=hf, sl=sl):
                    if state["mstg_free"][sl]:
                        g.wait_ge(sV.h, state["mstg_free"][sl])
                    gres[0] = gdma(
                        g, mstg[sl][:, :],
                        mask[ntt * 128:(ntt + 1) * 128,
                             hf * (D // 2):(hf + 1) * (D // 2)])
                mdma_done[(ntt, hf)] = gres[0]

        def maskpipe_a_red(ntt):
            s4 = ntt % 3
            for hf in range(2):
                sl = (ntt * 2 + hf) % 4
                gs, gv = mdma_done[(ntt, hf)]

                @block.vector
                def _(v, hf=hf, sl=sl, s4=s4, gs=gs, gv=gv,
                      bf2=(bs2_free[s4] if hf == 0 else None)):
                    v.wait_ge(gs.h, gv)
                    if bf2 is not None:
                        v.wait_ge(bf2[0].h, bf2[1])
                    with nc.allow_low_precision(reason="0/1 counts <= 32"):
                        sV.inc(v.tensor_reduce(
                            bs2t[s4][:, hf * 64:(hf + 1) * 64].unsqueeze(2),
                            mstg[sl][:, :].rearrange("p (kb j) -> p kb j",
                                                     j=32),
                            AX, ALU.add))
                state["mstg_free"][sl] = sV.n
            red_ready[ntt] = sV.n

        def maskpipe_b1(ntt):
            sl = ntt % 2
            s4 = ntt % 3
            pQf = pQb[sl][:, :] if sl else pQb[0][:, :].bitcast(F32)
            if sl == 0:
                pQf = pQb[0][:, :].bitcast(F32)
            else:
                pQf = pQb[1][:, :]

            @block.tensor
            def _(t, s4=s4, sl=sl, pQf=pQf, red_v=red_ready[ntt]):
                t.wait_ge(sV.h, red_v)
                pqf = state["pq_free"][sl]
                if pqf is not None:
                    t.wait_ge(pqf[0].h, pqf[1])
                sT.inc(t.matmul(pQf[0:4, 0:128], e4tf[:, :],
                                bs2t[s4][:, :], start=True, stop=True))
            tv = sT.n
            bs2_free[s4] = (sT, sT.n)

            @block.scalar
            def _(sc, sl=sl, tv=tv, pQf=pQf, bmf=state["bm4_free"][sl],
                  sgf=sC.n):
                sc.wait_ge(sT.h, tv)
                if bmf is not None:
                    sc.wait_ge(bmf[0].h, bmf[1])
                if sgf:
                    sc.wait_ge(sC.h, sgf)
                sC.inc(sc.activation(sgt[sl][0:4, :], pQf[0:4, 0:128],
                                     AF.Sign))
                sc.wait_ge(sC.h, sC.n)
                sC.inc(sc.activation(bm4t[sl][0:4, :], sgt[sl][0:4, :],
                                     AF.Relu))
            state["bm_v"][sl] = sC.n
            state["pq_free"][sl] = (sC, sC.n)

        def maskpipe_b2(ntt):
            sl = ntt % 2
            if sl == 0:
                pQf = pQb[0][:, :].bitcast(F32)
            else:
                pQf = pQb[1][:, :]

            @block.tensor
            def _(t, sl=sl, pQf=pQf, bm_v=state["bm_v"][sl]):
                t.wait_ge(sC.h, bm_v)
                pqf = state["pq2_free"][sl]
                if pqf is not None:
                    t.wait_ge(pqf[0].h, pqf[1])
                sT.inc(t.matmul(pQf[:, 128:256], e4b[0:4, :],
                                bm4t[sl][0:4, :], start=True, stop=True))
            tv = sT.n
            state["bm4_free"][sl] = (sT, sT.n)

            @block.scalar
            def _(sc, ntt=ntt, tv=tv, pQf=pQf,
                  qcf=(q2_consumed[ntt - 8] if ntt >= 8 else 0)):
                sc.wait_ge(sT.h, tv)
                if qcf:
                    sc.wait_ge(sV.h, qcf)
                sC.inc(sc.activation(
                    q2full[:, (ntt % 8) * 128:(ntt % 8 + 1) * 128],
                    pQf[:, 128:256], AF.Identity))
            state["pq2_free"][sl] = (sC, sC.n)
            q2_ready[ntt] = (sC, sC.n)

        def gen_subtile(L, ntt):
            sl = ntt % 2
            wsrc = w0 if L == 0 else wb
            fw = state["wstg_free"][sl]
            gres = [None]

            @block.gpsimd
            def _(g, ntt=ntt, sl=sl, wsrc=wsrc, fw=fw):
                if fw is not None:
                    g.wait_ge(fw[0].h, fw[1])
                gres[0] = gdma(g, wstg[sl][:, :],
                               wsrc[ntt * 128:(ntt + 1) * 128, :])
            gs, gv = gres[0]
            if L == 0:
                vs = []
                for pc in range(4):
                    @block.vector
                    def _(v, ntt=ntt, sl=sl, gs=gs, gv=gv, pc=pc,
                          q2v=q2_ready[ntt]):
                        v.wait_ge(gs.h, gv)
                        if pc == 0:
                            v.wait_ge(q2v[0].h, q2v[1])
                        sV.inc(v.tensor_tensor(
                            wstg[sl][:, pc * 1024:(pc + 1) * 1024]
                            .rearrange("p (kb j) -> p kb j", j=32),
                            wstg[sl][:, pc * 1024:(pc + 1) * 1024]
                            .rearrange("p (kb j) -> p kb j", j=32),
                            q2full[:, (ntt % 8) * 128 + pc * 32:
                                   (ntt % 8) * 128 + (pc + 1) * 32]
                            .unsqueeze(2).broadcast_to([128, 32, 32]),
                            ALU.mult))
                    vs.append((sV, sV.n))
                q2_consumed[ntt] = sV.n
                return vs
            return [(gs, gv)] * 4

        def gen_transpose_batch(nci, sub, gp, ready):
            sl2 = (nci * 2 + sub) % 2
            slw = nci % 2

            @block.tensor
            def _(t, gp=gp, sl2=sl2, ready=ready[gp]):
                t.wait_ge(ready[0].h, ready[1])
                k = tpool_take(t)
                mm = None
                for j in range(8):
                    kt = gp * 8 + j
                    mm = t.transpose(pTa[k][:, j * 128:(j + 1) * 128],
                                     wstg[sl2][:, kt * 128:(kt + 1) * 128],
                                     ident[:, :])
                sT.inc(mm)
            tv = sT.n
            kk = tpool_last[0]

            @block.scalar
            def _(sc, gp=gp, sub=sub, tv=tv, kk=kk, slw=slw,
                  scrw=state["wtb_scrw"][slw]):
                sc.wait_ge(sT.h, tv)
                if scrw is not None:
                    sc.wait_ge(scrw[0].h, scrw[1])
                sC.inc(sc.activation(
                    wtb[slw][:, :].rearrange("p (kt n) -> p kt n", n=NCW)
                    [:, gp * 8:(gp + 1) * 8, sub * 128:sub * 128 + 128],
                    pTa[kk][:, :].rearrange("p (j n) -> p j n", n=128),
                    AF.Identity))
            state["pt_free"][kk] = (sC, sC.n)
            if gp == 3:
                state["wstg_free"][sl2] = (sT, tv)

        def prep_nci(L, nci):
            sl = nci % 2
            bsrc = ba if L == 0 else bf_
            gres = [None]

            @block.gpsimd
            def _(g, nci=nci, sl=sl, bsrc=bsrc,
                  scrw=state["wtb_scrw"][sl],
                  wfw=state["wtb_free"][sl]):
                if scrw is not None:
                    g.wait_ge(scrw[0].h, scrw[1])
                else:
                    g.wait_ge(sV.h, wtb_ms)
                if wfw is not None:
                    g.wait_ge(wfw[0].h, wfw[1])
                gres[0] = gdma(
                    g, wtb[sl][0:1, KT * NCW:KT * NCW + NCW],
                    bsrc[0:1, nci * NCW:(nci + 1) * NCW])
            batches = []
            for sub in range(2):
                ntt = nci * 2 + sub
                ready = gen_subtile(L, ntt)
                for gp in range(4):
                    batches.append(
                        lambda nci=nci, sub=sub, gp=gp, ready=ready:
                        gen_transpose_batch(nci, sub, gp, ready))
            return batches, gres[0]

        def mm_tile(L, nci, nt, i, lora, wt_ready, extra_t_waits=(),
                    lora_gate=None):
            sl = nci % 2
            bk = (state.setdefault("bk_i", [0]))[0] % 3
            state["bk_i"][0] += 1

            @block.tensor
            def _(t, nci=nci, sl=sl, i=i, bk=bk, lora=lora,
                  wt_ready=wt_ready, bf_val=state["bank_free"][bk],
                  xr=tuple(state["x1t_tile"][i]) if nci == 0 else (),
                  ew=tuple(extra_t_waits), lora_gate=lora_gate):
                if i == 0 and wt_ready is not None:
                    wl = (wt_ready if isinstance(wt_ready, list)
                          else [wt_ready])
                    for s, vv in wl:
                        t.wait_ge(s.h, vv)
                if i == 0:
                    for s, vv in ew:
                        t.wait_ge(s.h, vv)
                for s, vv in xr:
                    t.wait_ge(s.h, vv)
                if bf_val:
                    t.wait_ge(sV.h, bf_val)
                t.matmul(pmm[bk][:, 0:NCW], ones1[0:1, :],
                         wtb[sl][0:1, KT * NCW:KT * NCW + NCW],
                         start=True, stop=False)
                mm = None
                for kt in range(KT):
                    mm = t.matmul(
                        pmm[bk][:, 0:NCW],
                        x1t[:, kt * maxchunk + i * 128:
                            kt * maxchunk + (i + 1) * 128],
                        wtb[sl][:, kt * NCW:(kt + 1) * NCW],
                        start=False, stop=(kt == KT - 1 and lora is None))
                if lora is not None:
                    if lora_gate is not None and nci == 0 and i == 0:
                        t.wait_ge(lora_gate[0].h, lora_gate[1])
                    mm = t.matmul(pmm[bk][:, 0:NCW],
                                  lora[:, i * 128:(i + 1) * 128],
                                  bmft[:, nci * NCW:(nci + 1) * NCW],
                                  start=False, stop=True)
                sT.inc(mm)
            tv = sT.n

            evv = [0]

            @block.vector
            def _(v, nci=nci, i=i, bk=bk, tv=tv):
                v.wait_ge(sT.h, tv)
                sl_ap = xnat[:, i * D + nci * NCW:i * D + (nci + 1) * NCW]
                sV.inc(v.scalar_tensor_tensor(
                    sl_ap, pmm[bk][:, 0:NCW], 1.0, sl_ap,
                    ALU.mult, ALU.add))
                evv[0] = sV.n
                if nci % 2 == 1:
                    v.wait_ge(sV.h, sV.n)
                    sV.inc(v.bn_stats(
                        bnbt[:, i * 48 + (nci // 2) * 6:
                             i * 48 + (nci // 2 + 1) * 6],
                        xnat[:, i * D + (nci - 1) * NCW:
                             i * D + (nci + 1) * NCW]))
            state["bank_free"][bk] = evv[0]
            if nci == NNC - 1:
                ln_prep(i, i % 2)
            if i == nt - 1:
                state["wtb_free"][sl] = (sT, tv)

        def nci_matmuls(L, nci, nt, lora, wt_ready, extra_t_waits=(),
                        lora_gate=None):
            for i in range(nt):
                mm_tile(L, nci, nt, i, lora, wt_ready, extra_t_waits,
                        lora_gate)

        def layer_pass(L, c, nt, lora, lora_gate=None):
            if c == 0 and L == 0:
                maskpipe_a_dma(0)
                maskpipe_a_red(0)
                maskpipe_a_dma(1)
                maskpipe_a_red(1)
                maskpipe_b1(0)
                maskpipe_a_dma(2)
                maskpipe_a_red(2)
                maskpipe_b1(1)
                maskpipe_b2(0)
                maskpipe_a_dma(3)
                maskpipe_a_red(3)
                maskpipe_b1(2)
                maskpipe_b2(1)
                maskpipe_a_dma(4)
                maskpipe_a_red(4)
                maskpipe_b1(3)
                maskpipe_b2(2)
                maskpipe_a_dma(5)
                maskpipe_a_red(5)
                maskpipe_b1(4)
                maskpipe_b2(3)
                maskpipe_a_dma(6)
                maskpipe_a_red(6)
                maskpipe_b1(5)
                maskpipe_b2(4)
                maskpipe_a_dma(7)
                maskpipe_a_red(7)
                maskpipe_b1(6)
                maskpipe_b2(5)
                maskpipe_a_dma(8)
                maskpipe_a_red(8)
                maskpipe_b1(7)
                maskpipe_a_dma(9)
                maskpipe_a_red(9)
                maskpipe_a_dma(10)
                maskpipe_a_dma(11)
            if c == 0:
                batches, bias_rdy = prep_nci(L, 0)
                for b in batches:
                    b()
                ev = sC.n
            for nci in range(NNC):
                sl = nci % 2
                if c == 0:
                    if L == 0:
                        if nci * 2 + 10 < KT:
                            maskpipe_a_red(nci * 2 + 10)
                        for ntt in (nci * 2 + 6, nci * 2 + 7):
                            if ntt < KT:
                                maskpipe_b2(ntt)
                        for ntt in (nci * 2 + 8, nci * 2 + 9):
                            if ntt < KT:
                                maskpipe_b1(ntt)
                        if nci * 2 + 11 < KT:
                            maskpipe_a_red(nci * 2 + 11)
                    if nci + 1 < NNC:
                        nbatches, nbias = prep_nci(L, nci + 1)
                    else:
                        nbatches, nbias = [], None
                    done = 0
                    for i in range(nt):
                        mm_tile(L, nci, nt, i, lora, bias_rdy,
                                extra_t_waits=[(sC, ev)],
                                lora_gate=lora_gate)
                        want = len(nbatches) * (i + 1) // nt
                        while done < want:
                            nbatches[done]()
                            done += 1
                    while done < len(nbatches):
                        nbatches[done]()
                        done += 1
                    nev = sC.n
                    hres = [None]

                    @block.sync
                    def _(sy, L=L, nci=nci, sl=sl, ev=ev, br=bias_rdy):
                        sy.wait_ge(sC.h, ev)
                        sy.wait_ge(br[0].h, br[1])
                        hres[0] = hdma(
                            sy, wts[L, nci, :, :, :].transpose([1, 0, 2]),
                            wtb[sl][:, :].rearrange("p (kt n) -> p kt n",
                                                    n=NCW))
                    state["wtb_scrw"][sl] = hres[0]
                    state["scr_done"][L] = hres[0]
                    ev = nev
                    bias_rdy = nbias
                    if L == 0:
                        for ntt in (nci * 2 + 12, nci * 2 + 13):
                            if ntt < KT:
                                maskpipe_a_dma(ntt)
                        nt0_tiles = chunk_sizes[0] // 128
                        if nci < t_core // 128 - nt0_tiles:
                            gt = nt0_tiles + nci

                            @block.gpsimd
                            def _(g, gt=gt):
                                if sGx.n:
                                    g.wait_ge(sGx.h, sGx.n)
                                g.dma_start(
                                    xscr[gt * 128:(gt + 1) * 128, :],
                                    xin[gt * 128:(gt + 1) * 128, :]
                                ).then_inc(sGx.h, 16)
                                sGx.n += 16
                                xscr_done[gt] = (sGx, sGx.n)
                else:
                    fw = state["wtb_free"][sl]
                    sd = state["scr_done"][L]
                    scrw = state["wtb_scrw"][sl]
                    hres = [None, None, None]

                    @block.sync
                    def _(sy, L=L, nci=nci, sl=sl, fw=fw, sd=sd, scrw=scrw):
                        if nci < 2 and sd is not None:
                            sy.wait_ge(sd[0].h, sd[1])
                        if scrw is not None:
                            sy.wait_ge(scrw[0].h, scrw[1])
                        if fw is not None:
                            sy.wait_ge(fw[0].h, fw[1])
                        wtb3 = wtb[sl][:, :].rearrange("p (kt n) -> p kt n",
                                                       n=NCW)
                        for qi, (k0, k1) in enumerate(
                                ((0, 11), (11, 22), (22, KT + 1))):
                            hres[qi] = hdma(
                                sy, wtb3[:, k0:k1, :],
                                wts[L, nci, k0:k1, :, :].transpose([1, 0, 2]))
                    nci_matmuls(L, nci, nt, lora, list(hres),
                                lora_gate=lora_gate)

        def transpose_tile(i, waits, allow3=True, waits_hi=None):
            for gp in range(4):
                @block.tensor
                def _(t, i=i, gp=gp, waits=tuple(waits),
                      waits_hi=tuple(waits_hi if waits_hi is not None
                                     else waits), allow3=allow3):
                    if gp == 0:
                        for s, vv in waits:
                            t.wait_ge(s.h, vv)
                    if gp == 2 and waits_hi != waits:
                        for s, vv in waits_hi:
                            t.wait_ge(s.h, vv)
                    k = tpool_take(t, allow3)
                    mm = None
                    for j in range(8):
                        kt = gp * 8 + j
                        mm = t.transpose(
                            pTa[k][:, j * 128:(j + 1) * 128],
                            xnat[:, i * D + kt * 128:i * D + (kt + 1) * 128],
                            ident[:, :])
                    sT.inc(mm)
                tv = sT.n
                kk = tpool_last[0]
                dst = x1t[:, :].rearrange("p (kt t) -> p kt t", t=maxchunk)[
                    :, gp * 8:(gp + 1) * 8, i * 128:(i + 1) * 128]
                src = pTa[kk][:, :].rearrange("p (j t) -> p j t", t=128)
                if gp % 2 == 0:
                    @block.vector
                    def _(v, tv=tv, dst=dst, src=src):
                        v.wait_ge(sT.h, tv)
                        sV.inc(v.tensor_copy(dst, src))
                    state["pt_free"][kk] = (sV, sV.n)
                else:
                    @block.scalar
                    def _(sc, tv=tv, dst=dst, src=src):
                        sc.wait_ge(sT.h, tv)
                        sC.inc(sc.activation(dst, src, AF.Identity))
                    state["pt_free"][kk] = (sC, sC.n)
            return (sV.n, sC.n)

        def input_chunk(c, t0, nt):
            for i in range(nt):
                in_gv = [None, None]

                @block.gpsimd
                def _(g, i=i, t0=t0):
                    xf = (state["xnat_free"][i]
                          if i < len(state["xnat_free"]) else None)
                    if xf is not None:
                        g.wait_ge(xf[0].h, xf[1])
                    in_gv[0] = gdma(
                        g, xnat[:, i * D:i * D + 2048],
                        xin[t0 + i * 128:t0 + (i + 1) * 128, 0:2048])
                    in_gv[1] = gdma(
                        g, xnat[:, i * D + 2048:(i + 1) * D],
                        xin[t0 + i * 128:t0 + (i + 1) * 128, 2048:D])
                tv = transpose_tile(i, [in_gv[0]], waits_hi=[in_gv[1]])
                state["x1t_tile"][i] = [(sV, tv[0]), (sC, tv[1])]

        def ln_prep(i, iset):
            lf = state["lnset_free"][iset]
            st_v = sV.n

            @block.vector
            def _(v, i=i, iset=iset, lf=lf, st_v=st_v):
                v.wait_ge(sV.h, st_v)
                if lf is not None:
                    v.wait_ge(lf[0].h, lf[1])
                sV.inc(v.bn_aggr(mv[iset][:, :],
                                 bnbt[:, i * 48:(i + 1) * 48]))
                v.wait_ge(sV.h, sV.n)
                sV.inc(v.tensor_scalar(vpe[iset][:, :], mv[iset][:, 1:2],
                                       EPS, None, ALU.add))
            ve_v = sV.n

            @block.scalar
            def _(sc, i=i, iset=iset, ve_v=ve_v):
                sc.wait_ge(sV.h, ve_v)
                sC.inc(sc.activation(stdt[iset][:, :], vpe[iset][:, :],
                                     AF.Ln))
                sc.wait_ge(sC.h, sC.n)
                sC.inc(sc.activation(rstdt[:, i:i + 1], stdt[iset][:, :],
                                     AF.Exp, scale=-0.5))
            sq_v = sC.n

            @block.vector
            def _(v, i=i, iset=iset, sq_v=sq_v):
                v.wait_ge(sC.h, sq_v)
                sV.inc(v.tensor_scalar(nmbt[:, i:i + 1], mv[iset][:, 0:1],
                                       rstdt[:, i:i + 1], -1.0,
                                       ALU.mult, ALU.mult))
            state["lnset_free"][iset] = (sV, sV.n)
            state["prep_done"][i] = sV.n

        def ln_apply(i, write_out=None, t0=0):
            pd = state["prep_done"][i]
            if write_out is not None:
                ofree = state.setdefault("outstg_free", [None, None])
                of0, of1 = ofree[0], ofree[1]

                @block.scalar
                def _(sc, i=i, pd=pd, of=of0):
                    sc.wait_ge(sV.h, pd)
                    if of is not None:
                        sc.wait_ge(of[0].h, of[1])
                    sC.inc(sc.activation(
                        outstg[0][:, :],
                        xnat[:, i * D:i * D + 2048],
                        AF.Identity, bias=nmbt[:, i:i + 1],
                        scale=rstdt[:, i:i + 1]))
                    sc.wait_ge(sC.h, sC.n)
                    sdma(sc,
                         write_out[t0 + i * 128:t0 + (i + 1) * 128, 0:2048],
                         outstg[0][:, :])
                ofree[0] = (sSs[(gidx2[0] - 1) % NS],
                            sSs[(gidx2[0] - 1) % NS].n)

                @block.vector
                def _(v, i=i, pd=pd, of=of1):
                    v.wait_ge(sV.h, pd)
                    if of is not None:
                        v.wait_ge(of[0].h, of[1])
                    sV.inc(v.tensor_scalar(
                        outstg[1][:, :],
                        xnat[:, i * D + 2048:i * D + 4096],
                        rstdt[:, i:i + 1], nmbt[:, i:i + 1],
                        ALU.mult, ALU.add))
                vv = sV.n
                hres = [None]

                @block.sync
                def _(sy, i=i, vv=vv):
                    sy.wait_ge(sV.h, vv)
                    hres[0] = hdma(
                        sy,
                        write_out[t0 + i * 128:t0 + (i + 1) * 128, 2048:4096],
                        outstg[1][:, :])
                ofree[1] = hres[0]
                return [(sC, sC.n), (sV, sV.n)]
            else:
                @block.scalar
                def _(sc, i=i, pd=pd):
                    sc.wait_ge(sV.h, pd)
                    sC.inc(sc.activation(
                        xnat[:, i * D:i * D + 2048],
                        xnat[:, i * D:i * D + 2048],
                        AF.Identity, bias=nmbt[:, i:i + 1],
                        scale=rstdt[:, i:i + 1]))

                @block.vector
                def _(v, i=i, pd=pd):
                    v.wait_ge(sV.h, pd)
                    sV.inc(v.tensor_scalar(
                        xnat[:, i * D + 2048:i * D + 4096],
                        xnat[:, i * D + 2048:i * D + 4096],
                        rstdt[:, i:i + 1], nmbt[:, i:i + 1],
                        ALU.mult, ALU.add))
                return [(sC, sC.n), (sV, sV.n)]

        def router_tile(i, iset, x2t_ok):
            rf = state["rt_free"][iset]

            @block.tensor
            def _(t, i=i, x2t_ok=tuple(x2t_ok), pref=state["pre_free"],
                  pq1=state["pq_free"][1], pq2=state["pq2_free"][1]):
                for s, vv in x2t_ok:
                    t.wait_ge(s.h, vv)
                if pref:
                    t.wait_ge(sV.h, pref)
                for q in (pq1, pq2):
                    if q is not None:
                        t.wait_ge(q[0].h, q[1])
                mm = None
                for kt in range(KT):
                    mm = t.matmul(pRE[:, 0:E],
                                  x1t[:, kt * maxchunk + i * 128:
                                      kt * maxchunk + (i + 1) * 128],
                                  wrt[:, kt * E:(kt + 1) * E],
                                  start=(kt == 0), stop=(kt == KT - 1))
                sT.inc(mm)
            tv = sT.n

            @block.vector
            def _(v, iset=iset, tv=tv, rf=rf):
                v.wait_ge(sT.h, tv)
                if rf is not None:
                    v.wait_ge(rf[0].h, rf[1])
                sV.inc(v.tensor_copy(tlr[iset][:, :], pRE[:, 0:E]))
            state["pre_free"] = sV.n
            cp_v = sV.n

            @block.scalar
            def _(sc, iset=iset, cp_v=cp_v):
                sc.wait_ge(sV.h, cp_v)
                sC.inc(sc.activation(exr[iset][:, :], tlr[iset][:, :],
                                     AF.Exp))
            ex_v = sC.n

            @block.vector
            def _(v, iset=iset, ex_v=ex_v, cp_v=cp_v):
                def stepv(inst):
                    sV.inc(inst)
                    v.wait_ge(sV.h, sV.n)
                v.wait_ge(sV.h, cp_v)
                stepv(v.reduce_max(m1[iset][:, :], tlr[iset][:, :], axis=AX))
                stepv(v.tensor_scalar(mskr[iset][:, :], tlr[iset][:, :],
                                      m1[iset][:, 0:1], None, ALU.is_ge))
                stepv(v.scalar_tensor_tensor(lm[iset][:, :], mskr[iset][:, :],
                                             -1e30, tlr[iset][:, :],
                                             ALU.mult, ALU.add))
                stepv(v.reduce_max(m2[iset][:, :], lm[iset][:, :], axis=AX))
                stepv(v.tensor_scalar(mskr[iset][:, :], tlr[iset][:, :],
                                      m2[iset][:, 0:1], None, ALU.is_ge))
                v.wait_ge(sC.h, ex_v)
                stepv(v.tensor_tensor(wvr[iset][:, :], exr[iset][:, :],
                                      mskr[iset][:, :], ALU.mult))
                stepv(v.reduce_sum(smr[iset][:, :], wvr[iset][:, :], axis=AX))
                stepv(v.reciprocal(rsr[iset][:, :], smr[iset][:, :]))
                stepv(v.tensor_scalar(wselb[iset][:, :], wvr[iset][:, :],
                                      rsr[iset][:, 0:1], None, ALU.mult))
            wb_v = sV.n
            state["rt_free"][iset] = (sV, sV.n)
            return wb_v

        def router_finish(i, iset, wb_v):
            @block.tensor
            def _(t, iset=iset, wb_v=wb_v, pwf=state["pwt_free"],
                  pqf=state["pq2_free"][0]):
                t.wait_ge(sV.h, wb_v)
                if pwf is not None:
                    t.wait_ge(pwf[0].h, pwf[1])
                if pqf is not None:
                    t.wait_ge(pqf[0].h, pqf[1])
                sT.inc(t.transpose(pWT[0:E, 0:128], wselb[iset][:, :],
                                   ident[:, :]))
            tw_v = sT.n

            @block.scalar
            def _(sc, i=i, tw_v=tw_v):
                sc.wait_ge(sT.h, tw_v)
                sC.inc(sc.activation(wselt[0:E, i * 128:(i + 1) * 128],
                                     pWT[0:E, 0:128], AF.Identity))
            state["pwt_free"] = (sC, sC.n)

        t0s = []
        acc = 0
        for cs in chunk_sizes:
            t0s.append(acc)
            acc += cs

        input_chunk(0, t0s[0], chunk_sizes[0] // 128)
        for c, chunk in enumerate(chunk_sizes):
            nt = chunk // 128
            t0 = t0s[c]

            layer_pass(0, c, nt, None)
            attn_done_v = sV.n

            ph_evac = state["ph_free"]
            pending = None

            def finish_one(pending):
                i_, iset_, wbv_ = pending
                router_finish(i_, iset_, wbv_)
                wt_v = sC.n

                @block.tensor
                def _(t, i=i_, wt_v=wt_v, ph_evac=ph_evac,
                      p2f=state["pt_free"][2]):
                    t.wait_ge(sC.h, wt_v)
                    if ph_evac:
                        t.wait_ge(sV.h, ph_evac)
                    if p2f is not None:
                        t.wait_ge(p2f[0].h, p2f[1])
                    sT.inc(t.matmul(
                        pH[:, (i % 4) * 128:(i % 4 + 1) * 128],
                        e16b[0:E, :],
                        wselt[0:E, i * 128:(i + 1) * 128],
                        start=True, stop=True))
                return sT.n

            evac_group = []
            for i in range(nt):
                apw = ln_apply(i)
                x2t_i = transpose_tile(i, [apw[0]], allow3=False,
                                       waits_hi=[apw[1]])
                state["x1t_tile"][i] = [(sV, x2t_i[0]), (sC, x2t_i[1])]
                wb_v = router_tile(i, i % 2, [(sV, x2t_i[0]),
                                              (sC, x2t_i[1])])
                if pending is not None:
                    tv = finish_one(pending)
                    evac_group.append((pending[0], tv))
                pending = (i, i % 2, wb_v)
                if evac_group and (evac_group[-1][0] % 4 == 3):
                    g0 = evac_group[0][0]
                    gi = evac_group[-1][0]
                    tv = evac_group[-1][1]

                    @block.vector
                    def _(v, g0=g0, gi=gi, tv=tv):
                        v.wait_ge(sT.h, tv)
                        sV.inc(v.tensor_copy(
                            wsele[:, g0 * 128:(gi + 1) * 128],
                            pH[:, (g0 % 4) * 128:(gi % 4 + 1) * 128]))
                    ph_evac = sV.n
                    evac_group = []
            if pending is not None:
                tv = finish_one(pending)
                evac_group.append((pending[0], tv))
            if evac_group:
                g0 = evac_group[0][0]
                gi = evac_group[-1][0]
                tv = evac_group[-1][1]

                @block.vector
                def _(v, g0=g0, gi=gi, tv=tv):
                    v.wait_ge(sT.h, tv)
                    sV.inc(v.tensor_copy(
                        wsele[:, g0 * 128:(gi + 1) * 128],
                        pH[:, (g0 % 4) * 128:(gi % 4 + 1) * 128]))
                ph_evac = sV.n
            wsele_done = sV.n

            hws = []
            off = 0
            while off < chunk:
                w_ = min(512, chunk - off)
                hws.append((off, w_))
                off += w_

            for off, w_ in hws:
                @block.tensor
                def _(t, off=off, w_=w_, phf=state["ph_free"],
                      wd=wsele_done):
                    t.wait_ge(sV.h, wd)
                    if phf:
                        t.wait_ge(sV.h, phf)
                    mm = None
                    for kt in range(KT):
                        mm = t.matmul(pH[:, 0:w_],
                                      at[:, kt * 128:(kt + 1) * 128],
                                      x1t[:, kt * maxchunk + off:
                                          kt * maxchunk + off + w_],
                                      start=(kt == 0), stop=(kt == KT - 1))
                    sT.inc(mm)
                tv = sT.n

                @block.vector
                def _(v, off=off, w_=w_, tv=tv, wd=wsele_done):
                    v.wait_ge(sT.h, tv)
                    v.wait_ge(sV.h, wd)
                    sV.inc(v.tensor_tensor(g_t[:, off:off + w_],
                                           pH[:, 0:w_],
                                           wsele[:, off:off + w_], ALU.mult))
                state["ph_free"] = sV.n
            g_done = sV.n
            state["pt_free"][2] = (sV, g_done)

            layer_pass(1, c, nt, g_t, lora_gate=(sV, g_done))
            state["g_free"] = sT.n
            ffn_done_v = sV.n

            nxt_nt = (chunk_sizes[c + 1] // 128
                      if c + 1 < len(chunk_sizes) else 0)
            assert nxt_nt <= nt
            in_gs = [None] * nt

            def emit_apply(i):
                apw = ln_apply(i, write_out=out, t0=t0)
                if i < nxt_nt:
                    in_g = [None, None]
                    gt = t0s[c + 1] // 128 + i
                    xd = xscr_done[gt]
                    r0 = gt * 128
                    r1 = (gt + 1) * 128

                    @block.gpsimd
                    def _(g, i=i, apw=tuple(apw), xd=xd, r0=r0, r1=r1):
                        g.wait_ge(xd[0].h, xd[1])
                        g.wait_ge(apw[0][0].h, apw[0][1])
                        in_g[0] = gdma(g, xnat[:, i * D:i * D + 2048],
                                       xscr[r0:r1, 0:2048])
                        g.wait_ge(apw[1][0].h, apw[1][1])
                        in_g[1] = gdma(g, xnat[:, i * D + 2048:(i + 1) * D],
                                       xscr[r0:r1, 2048:D])
                    in_gs[i] = in_g

            nlead = min(3, nt)
            for i in range(nlead):
                emit_apply(i)
            nxtj = nlead
            for i in range(nxt_nt):
                tv = transpose_tile(i, [in_gs[i][0]], waits_hi=[in_gs[i][1]])
                state["x1t_tile"][i] = [(sV, tv[0]), (sC, tv[1])]
                if nxtj < nt:
                    emit_apply(nxtj)
                    nxtj += 1
            while nxtj < nt:
                emit_apply(nxtj)
                nxtj += 1

        fin = tuple((s, s.n) for s in (sHs + sGs + sSs + [sGx]) if s.n)

        @block.sync
        def _(sy):
            for s, vv in fin:
                sy.wait_ge(s.h, vv)

        @block.gpsimd
        def _(g):
            for s, vv in fin:
                g.wait_ge(s.h, vv)

    return nc


def kernel(L0_attn_data, L0_attn_mask, L0_attn_weight, L0_attn_bias,
           L0_ffn_Wb, L0_ffn_bias, L0_ffn_Wr, L0_ffn_A, L0_ffn_Bm):
    t_total = L0_attn_data.shape[0] * L0_attn_data.shape[1]
    t_core = t_total // NCORES
    x = np.ascontiguousarray(
        np.asarray(L0_attn_data, dtype=np.float32).reshape(t_total, D))
    common = {
        "mask": np.ascontiguousarray(np.asarray(L0_attn_mask,
                                                dtype=np.int32)),
        "w0": np.ascontiguousarray(np.asarray(L0_attn_weight,
                                              dtype=np.float32)),
        "ba": np.ascontiguousarray(
            np.asarray(L0_attn_bias, dtype=np.float32).reshape(1, D)),
        "wb": np.ascontiguousarray(np.asarray(L0_ffn_Wb, dtype=np.float32)),
        "bf": np.ascontiguousarray(
            np.asarray(L0_ffn_bias, dtype=np.float32).reshape(1, D)),
        "wr": np.ascontiguousarray(np.asarray(L0_ffn_Wr, dtype=np.float32)),
        "am": np.ascontiguousarray(
            np.asarray(L0_ffn_A, dtype=np.float32).reshape(ER, D)),
        "bm": np.ascontiguousarray(np.asarray(L0_ffn_Bm, dtype=np.float32)),
    }
    in_maps = []
    for c in range(NCORES):
        m = dict(common)
        m["xin"] = np.ascontiguousarray(x[c * t_core:(c + 1) * t_core])
        in_maps.append(m)
    nc = build_program(t_core, [768, 640, 640])
    res = run_bass_kernel_spmd(nc, in_maps, core_ids=list(range(NCORES)))
    shards = [np.asarray(r["out"]) for r in res.results]
    full = np.concatenate(shards, axis=0)
    return full.reshape(L0_attn_data.shape[0], L0_attn_data.shape[1], D)
